# version 07
# baseline (speedup 1.0000x reference)
import numpy as np

try:
    import concourse.bass as bass
    import concourse.mybir as mybir
    from concourse.bass_utils import run_bass_kernel_spmd
except ImportError:
    import sys
    sys.path.insert(0, "/opt/trn_rl_repo")
    import concourse.bass as bass
    import concourse.mybir as mybir
    from concourse.bass_utils import run_bass_kernel_spmd

F32 = mybir.dt.float32
BF16 = mybir.dt.bfloat16
FP16 = mybir.dt.float16
AF = mybir.ActivationFunctionType
ALU = mybir.AluOpType

D = 1024
HID = 4096
DH = 256
H = 4
NB = 2
TSEQ = 8192
TCORE = 2048
TT = 256
CH = 128
GCH = 8
GT = CH * GCH
NKD = D // 128
NOH = HID // 128
NT = TCORE // TT
NG = TSEQ // GT
NSLICE = TSEQ // TCORE
EPS_LN = 1e-5
REPL_GROUPS = [[0, 1, 2, 3], [4, 5, 6, 7]]
EPS_L2 = 1e-6
EPS_RMS = 1e-5

SEM_NAMES = ("LDC", "LDX0", "LDX1", "LDR", "LDG", "LDA", "LDO", "LDL", "LDW0", "LDW1",
             "WL0", "WL1", "WF", "PE", "VE", "AC", "GP", "STR", "STXA", "STXB", "STOA", "STOB", "STU", "CC")


class SBAlloc:
    def __init__(self, nc, base, limit):
        self.nc = nc
        self.cur = base
        self.limit = limit

    def alloc(self, name, shape, dtype, align=64):
        per = int(np.prod(shape[1:])) * mybir.dt.size(dtype)
        off = (self.cur + align - 1) // align * align
        t = self.nc.alloc_sbuf_tensor_at(name, list(shape), dtype, offset=off)
        self.cur = off + per
        assert self.cur <= self.limit, f"SBUF overflow at {name}: {self.cur} > {self.limit}"
        return t


def build_nc():
    nc = bass.Bass("TRN2", target_bir_lowering=False)

    def din(name, shape):
        return nc.dram_tensor(name, list(shape), F32, kind="ExternalInput").ap()

    x0T = din("x0T", [D, TCORE])
    w12t = din("w12t", [8, NKD, 128, 1024])
    h1t = din("h1t", [16, NOH, 128, 256])
    h2t = din("h2t", [16, NOH, 128, 256])
    w3t = din("w3t", [4, NOH, 128, 256])
    w1s_i = din("w1s", [128, NOH])
    w2s_i = din("w2s", [128, NOH])
    h1s_i = din("h1s", [128, NOH])
    h2s_i = din("h2s", [128, NOH])
    w3s_i = din("w3s", [128, NKD])
    wq_i = din("wq", [D, DH])
    wk_i = din("wk", [D, DH])
    wv_i = din("wv", [D, DH])
    wb_i = din("wb", [D, 1])
    convq_i = din("convq", [128, 8])
    convk_i = din("convk", [128, 8])
    convv_i = din("convv", [128, 8])
    wo_i = din("wo", [D, D])
    grms_i = din("grms", [128, 2])
    lng_i = din("lngt", [128, NKD])
    lnb_i = din("lnbt", [128, NKD])
    outT = nc.dram_tensor("outT", [D, TCORE], F32, kind="ExternalOutput").ap()

    resid = nc.dram_tensor("resid", [D, TCORE], F32).ap()
    HC = TCORE // 2
    HS = TSEQ // 2
    xg_in = [nc.dram_tensor(f"xg_in{i}", [D, HC], F32).ap() for i in range(2)]
    xg_out = [nc.dram_tensor(f"xg_out{i}", [NSLICE * D, HC], F32).ap() for i in range(2)]
    og_in = [nc.dram_tensor(f"og_in{i}", [DH, HS], F32).ap() for i in range(2)]
    og_out = [nc.dram_tensor(f"og_out{i}", [H * DH, HS], F32).ap() for i in range(2)]

    ARENA = 205 * 1024
    arena_base = nc.sbuf_base
    nc.alloc_sbuf_tensor("arena", [128, ARENA], mybir.dt.uint8)
    arena_limit = arena_base + ARENA
    A = SBAlloc(nc, arena_base, arena_limit)
    ones_col = A.alloc("ones_col", [128, 1], F32)
    ones_row = A.alloc("ones_row", [1, 128], F32)
    iota_t = A.alloc("iota_t", [128, 128], F32)
    ident = A.alloc("ident", [128, 128], F32)
    m_trilneg = A.alloc("m_trilneg", [128, 128], F32)
    m_triu = A.alloc("m_triu", [128, 128], F32)
    s_w1 = A.alloc("s_w1", [128, NOH], F32)
    s_w2 = A.alloc("s_w2", [128, NOH], F32)
    s_h1 = A.alloc("s_h1", [128, NOH], F32)
    s_h2 = A.alloc("s_h2", [128, NOH], F32)
    s_w3 = A.alloc("s_w3", [128, NKD], F32)
    s_w3h = A.alloc("s_w3h", [128, NKD], F32)
    c_q = A.alloc("c_q", [128, 8], F32)
    c_k = A.alloc("c_k", [128, 8], F32)
    c_v = A.alloc("c_v", [128, 8], F32)
    g_rms = A.alloc("g_rms", [128, 2], F32)
    ln_g = A.alloc("ln_g", [128, NKD], F32)
    ln_b = A.alloc("ln_b", [128, NKD], F32)
    row_mu = A.alloc("row_mu", [1, TT], F32)
    row_m2 = A.alloc("row_m2", [1, TT], F32)
    row_t = A.alloc("row_t", [1, TT], F32)
    row_var = A.alloc("row_var", [1, TT], F32)
    row_sd = A.alloc("row_sd", [1, TT], F32)
    row_ri = A.alloc("row_ri", [1, TT], F32)
    row_a = A.alloc("row_a", [1, TT], F32)
    row_l2 = A.alloc("row_l2", [1, 512], F32)
    row_l2b = A.alloc("row_l2b", [1, 512], F32)
    x_tile2 = [A.alloc(f"x_tile{i}", [128, NKD * TT], F32) for i in range(2)]
    r2b2 = [A.alloc(f"r2b{i}", [128, NKD * TT], F32) for i in range(2)]
    xlnb_tile = A.alloc("xlnb_tile", [128, NKD * TT], F32)
    sq_tile = A.alloc("sq_tile", [128, 2 * TT], F32)
    z_tile = A.alloc("z_tile", [128, TT], F32)
    e_tile = A.alloc("e_tile", [128, TT], F32)
    silu_t = A.alloc("silu_t", [128, TT], F32)
    hf_t = [A.alloc(f"hf_t{i}", [128, TT], F32) for i in range(2)]
    Rb = A.alloc("Rb", [128, TT], F32)
    Ab = A.alloc("Ab", [128, TT], F32)
    og_yp2 = [A.alloc(f"og_yp{i}", [128, NKD * TT], F32) for i in range(2)]
    const_vals = [EPS_LN, EPS_L2, 0.5, 1.0 / DH]
    const_ts = []
    for i, v in enumerate(sorted(set(const_vals))):
        tcv = A.alloc(f"cst{i}", [128, 1], F32)
        nc.const_aps.register(F32, v, tcv[:, :])
        const_ts.append((tcv, v))
    wo_buf = [A.alloc(f"wo_buf{i}", [128, D], F32) for i in range(2)]

    perm_end = A.cur
    Fz = SBAlloc(nc, perm_end, arena_limit)
    xsp2 = [[Fz.alloc(f"xsp{q}_{p}", [128, NKD * TT], FP16) for p in range(2)] for q in range(2)]
    wbuf = [Fz.alloc(f"wbuf{p}", [128, 8192], FP16) for p in range(2)]
    hspA = [Fz.alloc(f"hspA{p}", [128, NOH * TT], FP16) for p in range(2)]
    hspB = [Fz.alloc(f"hspB{p}", [128, NOH * TT], FP16) for p in range(2)]

    P2 = SBAlloc(nc, perm_end, arena_limit)
    xg_sb = P2.alloc("xg_sb", [128, NKD * GT], F32)
    u_sb = P2.alloc("u_sb", [128, 2 * (GT + 3)], F32)
    qT_sb = P2.alloc("qT_sb", [128, 2 * GT], F32)
    kT_sb = P2.alloc("kT_sb", [128, 2 * GT], F32)
    vT_sb = P2.alloc("vT_sb", [128, 2 * GT], F32)
    wq_sb = P2.alloc("wq_sb", [128, NKD * 256], F32)
    wk_sb = P2.alloc("wk_sb", [128, NKD * 256], F32)
    wv_sb = P2.alloc("wv_sb", [128, NKD * 256], F32)
    wb_sb = P2.alloc("wb_sb", [128, NKD], F32)
    beta_sb = P2.alloc("beta_sb", [128, GCH], F32)
    bh_sb = P2.alloc("bh_sb", [128, 1], F32)
    S_sb = P2.alloc("S_sb", [128, 512], F32)
    NSLOT = 4
    P_pp = [P2.alloc(f"P_pp{i}", [128, 128], F32) for i in range(2)]
    T_pp = [P2.alloc(f"T_pp{i}", [128, 128], F32) for i in range(2)]
    X_a = [P2.alloc(f"X_a{i}", [128, 128], F32) for i in range(NSLOT)]
    W1Tn_a = [P2.alloc(f"W1Tn_a{i}", [128, 256], F32) for i in range(NSLOT)]
    Bv_a = [P2.alloc(f"Bv_a{i}", [128, 256], F32) for i in range(NSLOT)]
    LT_a = [P2.alloc(f"LT_a{i}", [128, 128], F32) for i in range(NSLOT)]
    Ktm_a = [P2.alloc(f"Ktm_a{i}", [128, 256], F32) for i in range(NSLOT)]
    Bkn_t = P2.alloc("Bkn_t", [128, 256], F32)
    U_sb = P2.alloc("U_sb", [128, 256], F32)
    o_sb = P2.alloc("o_sb", [128, 256], F32)
    on_sb = P2.alloc("on_sb", [128, 256], F32)
    osq_sb = P2.alloc("osq_sb", [128, 256], F32)
    ossq = P2.alloc("ossq", [128, 1], F32)
    orr = P2.alloc("orr", [128, 1], F32)
    orinv = P2.alloc("orinv", [128, 1], F32)
    oT_sb = P2.alloc("oT_sb", [128, 256], F32)
    utail = P2.alloc("utail", [128, 18], F32)

    pb = [nc.alloc_psum_tensor(f"pb{i}", [128, 512], F32) for i in range(8)]

    sems = {s: nc.alloc_semaphore(s) for s in SEM_NAMES}

    def emit(me, eng_handle):
        C = {k: 0 for k in SEM_NAMES}
        bank_free = [None] * 8
        rr = [0]

        def op(engine, build, waits=(), incs=(), drain=True):
            assert len(incs) <= 1, "one semaphore update per instruction"
            for s, n in incs:
                C[s] += n
            if me == engine:
                for s, v in waits:
                    if v > 0:
                        eng_handle.wait_ge(sems[s], v)
                ins = build(eng_handle)
                for s, n in incs:
                    ins = ins.then_inc(sems[s], n)
                if drain and engine in ("vector", "scalar", "gpsimd"):
                    eng_handle.drain()

        def waitonly(engine, waits):
            if me == engine:
                for s, v in waits:
                    if v > 0:
                        eng_handle.wait_ge(sems[s], v)

        def get_bank(extra=()):
            b = rr[0] % 8
            rr[0] += 1
            w = list(extra)
            if bank_free[b] is not None:
                w.append(bank_free[b])
            return b, w

        def rel_bank(b, sem):
            bank_free[b] = (sem, C[sem])

        for dst, src in [(s_w1, w1s_i), (s_w2, w2s_i), (s_h1, h1s_i), (s_h2, h2s_i),
                         (s_w3, w3s_i), (c_q, convq_i), (c_k, convk_i), (c_v, convv_i),
                         (g_rms, grms_i), (ln_g, lng_i), (ln_b, lnb_i)]:
            op("sync", lambda e, d=dst, s=src: e.dma_start(out=d[:, :], in_=s),
               incs=[("LDC", 16)])
        ld_consts = C["LDC"]

        op("gpsimd", lambda e: e.iota(iota_t[:, :], pattern=[[1, 128]], base=0,
                                      channel_multiplier=-1,
                                      allow_small_or_imprecise_dtypes=True),
           incs=[("GP", 1)])
        gp_iota = C["GP"]
        op("vector", lambda e: e.tensor_scalar(m_trilneg[:, :], iota_t[:, :], 0.0, -1.0,
                                               ALU.is_lt, ALU.mult),
           waits=[("GP", gp_iota)], incs=[("VE", 1)])
        op("vector", lambda e: e.tensor_single_scalar(m_triu[:, :], iota_t[:, :], 0.0, ALU.is_ge),
           incs=[("VE", 1)])
        op("vector", lambda e: e.tensor_single_scalar(ident[:, :], iota_t[:, :], 0.0, ALU.is_equal),
           incs=[("VE", 1)])
        op("vector", lambda e: e.memset(ones_col[:, :], 1.0), incs=[("VE", 1)])
        op("vector", lambda e: e.memset(ones_row[:, :], 1.0), incs=[("VE", 1)])
        for tcv, v in const_ts:
            op("vector", lambda e, tcv=tcv, v=v: e.memset(tcv[:, :], v), incs=[("VE", 1)])
        op("vector", lambda e: e.tensor_scalar_mul(s_w3h[:, :], s_w3[:, :], 0.5),
           waits=[("LDC", ld_consts)], incs=[("VE", 1)])
        ve_setup = C["VE"]

        S = {
            "pe_sq": 0,
            "ac_rows": 0,
            "pe_rows": 0,
            "ve_silu": 0,
            "ve_hf": [0, 0],
            "st_r": 0, "st_xlnb": ("STXA", 0), "st_out": 0, "st_og": ("STOA", 0),
            "ve_x_read": 0,
            "pe_og_read": 0,
            "ve_yp_read": 0,
            "ve_xlnb_read": 0,
            "pe_l2": 0,
            "ac_osq": 0,
            "ve_orr": 0,
            "pe_scan_q": 0,
            "pe_chain_prev": 0,
            "pe_bkn_prev": 0,
            "pe_u_prev": 0,
            "tail_pe": [0, 0, 0, 0],
            "ve_mt": [0, 0, 0, 0],
            "ve_indep": [0, 0, 0, 0],
            "pe_xg": 0,
            "ve_wo_rd": 0,
        }

        def ln_stats(src, src_ready):
            b1, w1_ = get_bank(list(src_ready) + [("VE", ve_setup)])
            b2, w2_ = get_bank()
            pe_mm2 = {}
            for kc in range(NKD):
                sqs = slice((kc % 2) * TT, (kc % 2) * TT + TT)
                w = []
                if kc == 0:
                    w = list(src_ready) + [("PE", S["pe_sq"])]
                elif kc >= 2:
                    w = [("PE", pe_mm2[kc - 2])]
                op("scalar", lambda e, kc=kc, sqs=sqs: e.activation(
                    sq_tile[:, sqs], src[:, kc * TT:(kc + 1) * TT], AF.Square),
                   waits=w, incs=[("AC", 1)])
                ac_kc = C["AC"]
                op("tensor", lambda e, kc=kc, b=b1: e.matmul(
                    pb[b][0:1, :TT], ones_col[:, :], src[:, kc * TT:(kc + 1) * TT],
                    start=(kc == 0), stop=(kc == NKD - 1)),
                   waits=(w1_ if kc == 0 else ()),
                   incs=([("PE", 1)] if kc == NKD - 1 else ()))
                if kc == NKD - 1:
                    pe_s1 = C["PE"]
                op("tensor", lambda e, kc=kc, b=b2, sqs=sqs: e.matmul(
                    pb[b][0:1, :TT], ones_col[:, :], sq_tile[:, sqs],
                    start=(kc == 0), stop=(kc == NKD - 1)),
                   waits=(list(w2_) if kc == 0 else []) + [("AC", ac_kc)],
                   incs=[("PE", 1)])
                pe_mm2[kc] = C["PE"]
            pe_s2 = C["PE"]
            S["pe_sq"] = pe_s2
            op("vector", lambda e, b=b1: e.tensor_scalar_mul(row_mu[:, :], pb[b][0:1, :TT], 1.0 / D),
               waits=[("PE", pe_s1), ("AC", S["ac_rows"]), ("PE", S["pe_rows"])],
               incs=[("VE", 1)])
            op("vector", lambda e, b=b2: e.tensor_scalar_mul(row_m2[:, :], pb[b][0:1, :TT], 1.0 / D),
               waits=[("PE", pe_s2)], incs=[("VE", 1)])
            rel_bank(b1, "VE")
            rel_bank(b2, "VE")
            op("vector", lambda e: e.tensor_mul(row_t[:, :], row_mu[:, :], row_mu[:, :]),
               incs=[("VE", 1)])
            op("vector", lambda e: e.tensor_sub(row_var[:, :], row_m2[:, :], row_t[:, :]),
               incs=[("VE", 1)])
            ve_var = C["VE"]
            op("scalar", lambda e: e.activation(row_sd[:, :], row_var[:, :], AF.Sqrt, bias=EPS_LN),
               waits=[("VE", ve_var)], incs=[("AC", 1)])
            ac_sd = C["AC"]
            S["ac_rows"] = ac_sd
            op("vector", lambda e: e.reciprocal(row_ri[:, :], row_sd[:, :]),
               waits=[("AC", ac_sd)], incs=[("VE", 1)])
            op("vector", lambda e: e.tensor_mul(row_a[:, :], row_mu[:, :], row_ri[:, :]),
               incs=[("VE", 1)])
            ve_rows = C["VE"]
            b3, w3_ = get_bank([("VE", ve_rows)])
            op("tensor", lambda e, b=b3: e.matmul(pb[b][:, :TT], ones_row[:, :], row_ri[0:1, :TT],
                                                  start=True, stop=True),
               waits=w3_, incs=[("PE", 1)])
            pe_b1 = C["PE"]
            b4, w4_ = get_bank()
            op("tensor", lambda e, b=b4: e.matmul(pb[b][:, :TT], ones_row[:, :], row_a[0:1, :TT],
                                                  start=True, stop=True),
               waits=w4_, incs=[("PE", 1)])
            pe_b2 = C["PE"]
            S["pe_rows"] = pe_b2
            op("vector", lambda e, b=b3: e.tensor_copy(Rb[:, :], pb[b][:, :TT]),
               waits=[("PE", pe_b1)], incs=[("VE", 1)])
            rel_bank(b3, "VE")
            op("vector", lambda e, b=b4: e.tensor_copy(Ab[:, :], pb[b][:, :TT]),
               waits=[("PE", pe_b2)], incs=[("VE", 1)])
            rel_bank(b4, "VE")
            return C["VE"]

        def ln_split(src, ve_rb, xsp, pe_guard=0):
            for kc in range(NKD):
                cs = slice(kc * TT, (kc + 1) * TT)
                op("vector", lambda e, cs=cs: e.tensor_mul(z_tile[:, :], src[:, cs], Rb[:, :]),
                   waits=([("VE", ve_rb), ("PE", pe_guard)] if kc == 0 else ()),
                   incs=[("VE", 1)])
                op("vector", lambda e: e.tensor_sub(z_tile[:, :], z_tile[:, :], Ab[:, :]),
                   incs=[("VE", 1)])
                op("vector", lambda e, cs=cs, xsp=xsp: e.tensor_copy(xsp[0][:, cs], z_tile[:, :]),
                   incs=[("VE", 1)])
                op("vector", lambda e, cs=cs, xsp=xsp: e.tensor_sub(xsp[1][:, cs], z_tile[:, :], xsp[0][:, cs]),
                   incs=[("VE", 1)])
            return C["VE"]

        wl_state = {"idx": 0}

        def wload(src_ap, nk, sub):
            idx = wl_state["idx"]
            par = idx % 2
            sem = "WL0" if par == 0 else "WL1"
            waits = []
            if idx >= 2:
                waits.append(("WF", idx - 1))
            op("gpsimd", lambda e: e.dma_start(
                out=wbuf[par].ap().rearrange("p (k j) -> p k j", k=nk)[:, :, :sub],
                in_=src_ap.rearrange("k p j -> p k j")),
               waits=waits, incs=[(sem, 16)], drain=False)
            wl_state["idx"] += 1
            return par, (sem, 16 * (idx // 2 + 1))

        def ffn(ve_split, in1_tile, out_tile, st_out_guard, xsp, inject=None):
            inj = inject or {}

            def fire(k):
                for fn in inj.get(k, ()):
                    fn()
            for og in range(8):
                par, wlv = wload(w12t[og], NKD, 1024)
                for ocp in range(4):
                    oc = og * 4 + ocp
                    bA, wA = get_bank()
                    if ocp == 0:
                        wA += [wlv, ("VE", ve_split)]
                    for kc in range(NKD):
                        for p in range(2):
                            last = (kc == NKD - 1 and p == 1)
                            op("tensor", lambda e, kc=kc, p=p, b=bA, ocp=ocp, par=par: e.matmul(
                                pb[b][:, :TT],
                                wbuf[par][:, kc * 1024 + ocp * 128: kc * 1024 + ocp * 128 + 128],
                                xsp[p][:, kc * TT:(kc + 1) * TT],
                                start=(kc == 0 and p == 0), stop=last),
                               waits=(wA if (kc == 0 and p == 0) else ()),
                               incs=([("PE", 1)] if last else ()))
                    peA = C["PE"]
                    bB, wB = get_bank()
                    for kc in range(NKD):
                        for p in range(2):
                            last = (kc == NKD - 1 and p == 1)
                            op("tensor", lambda e, kc=kc, p=p, b=bB, ocp=ocp, par=par: e.matmul(
                                pb[b][:, :TT],
                                wbuf[par][:, kc * 1024 + 512 + ocp * 128: kc * 1024 + 512 + ocp * 128 + 128],
                                xsp[p][:, kc * TT:(kc + 1) * TT],
                                start=(kc == 0 and p == 0), stop=last),
                               waits=(wB if (kc == 0 and p == 0) else ()),
                               incs=([("PE", 1)] if last else ()))
                    peB = C["PE"]
                    op("scalar", lambda e, b=bA, oc=oc: e.activation(
                        silu_t[:, :], pb[b][:, :TT], AF.Silu, scale=s_w1[:, oc:oc + 1]),
                       waits=[("PE", peA), ("VE", S["ve_silu"])], incs=[("AC", 1)])
                    rel_bank(bA, "AC")
                    ac_silu = C["AC"]
                    ht = hf_t[oc % 2]
                    op("vector", lambda e, b=bB, oc=oc, ht=ht: e.scalar_tensor_tensor(
                        ht[:, :], pb[b][:, :TT], s_w2[:, oc:oc + 1], silu_t[:, :],
                        ALU.mult, ALU.mult),
                       waits=[("PE", peB), ("AC", ac_silu)], incs=[("VE", 1)])
                    S["ve_silu"] = C["VE"]
                    rel_bank(bB, "VE")
                    if ocp == 3:
                        op("vector", lambda e: e.engine_nop(), incs=[("WF", 1)], drain=False)
                    cs = slice(oc * TT, (oc + 1) * TT)
                    op("vector", lambda e, cs=cs, ht=ht: e.tensor_copy(hspA[0][:, cs], ht[:, :]),
                       incs=[("VE", 1)])
                    op("vector", lambda e, cs=cs, ht=ht: e.tensor_sub(hspA[1][:, cs], ht[:, :], hspA[0][:, cs]),
                       incs=[("VE", 1)])
                    S["ve_hf"][oc % 2] = C["VE"]
            ve_hA = C["VE"]
            pe_stageA = C["PE"]
            fire("A_end")

            def hidden_stage(wt_ap, s_t, src_sp, dst_sp, ve_src, mid, end):
                for og in range(16):
                    if og == 8:
                        fire(mid)
                    par, wlv = wload(wt_ap[og], NOH, 256)
                    for ocj in range(2):
                        oc = og * 2 + ocj
                        bk, wk_ = get_bank()
                        if ocj == 0:
                            wk_ += [wlv, ("VE", ve_src)]
                        for kc in range(NOH):
                            for p in range(2):
                                last = (kc == NOH - 1 and p == 1)
                                op("tensor", lambda e, kc=kc, p=p, b=bk, ocj=ocj, par=par, sp=src_sp: e.matmul(
                                    pb[b][:, :TT],
                                    wbuf[par][:, kc * 256 + ocj * 128: kc * 256 + ocj * 128 + 128],
                                    sp[p][:, kc * TT:(kc + 1) * TT],
                                    start=(kc == 0 and p == 0), stop=last),
                                   waits=(wk_ if (kc == 0 and p == 0) else ()),
                                   incs=([("PE", 1)] if last else ()))
                        peg = C["PE"]
                        ht = hf_t[oc % 2]
                        op("scalar", lambda e, b=bk, oc=oc, ht=ht, s_t=s_t: e.activation(
                            ht[:, :], pb[b][:, :TT], AF.Gelu, scale=s_t[:, oc:oc + 1]),
                           waits=[("PE", peg), ("VE", S["ve_hf"][oc % 2])], incs=[("AC", 1)])
                        rel_bank(bk, "AC")
                        ac_g = C["AC"]
                        cs = slice(oc * TT, (oc + 1) * TT)
                        op("vector", lambda e, cs=cs, ht=ht, sp=dst_sp: e.tensor_copy(sp[0][:, cs], ht[:, :]),
                           waits=[("AC", ac_g)], incs=[("VE", 1)])
                        op("vector", lambda e, cs=cs, ht=ht, sp=dst_sp: e.tensor_sub(sp[1][:, cs], ht[:, :], sp[0][:, cs]),
                           incs=[("VE", 1)])
                        S["ve_hf"][oc % 2] = C["VE"]
                        if ocj == 1:
                            op("vector", lambda e: e.engine_nop(), incs=[("WF", 1)], drain=False)
                fire(end)
                return C["VE"]

            ve_hB = hidden_stage(h1t, s_h1, hspA, hspB, ve_hA, "B_mid", "B_end")
            ve_hC = hidden_stage(h2t, s_h2, hspB, hspA, ve_hB, "C_mid", "C_end")
            ffn.pe_stageA = pe_stageA

            for og in range(4):
                par, wlv = wload(w3t[og], NOH, 256)
                for ocj in range(2):
                    oc = og * 2 + ocj
                    bk, wk_ = get_bank()
                    if ocj == 0:
                        wk_ += [wlv, ("VE", ve_hC)]
                    for kc in range(NOH):
                        for p in range(2):
                            last = (kc == NOH - 1 and p == 1)
                            op("tensor", lambda e, kc=kc, p=p, b=bk, ocj=ocj, par=par: e.matmul(
                                pb[b][:, :TT],
                                wbuf[par][:, kc * 256 + ocj * 128: kc * 256 + ocj * 128 + 128],
                                hspA[p][:, kc * TT:(kc + 1) * TT],
                                start=(kc == 0 and p == 0), stop=last),
                               waits=(wk_ if (kc == 0 and p == 0) else ()),
                               incs=([("PE", 1)] if last else ()))
                    peg = C["PE"]
                    cs = slice(oc * TT, (oc + 1) * TT)
                    w = [("PE", peg)]
                    if oc == 0:
                        w.append(st_out_guard())
                    op("vector", lambda e, b=bk, oc=oc, cs=cs: e.scalar_tensor_tensor(
                        out_tile[:, cs], pb[b][:, :TT], s_w3h[:, oc:oc + 1], in1_tile[:, cs],
                        ALU.mult, ALU.add),
                       waits=w, incs=[("VE", 1)])
                    rel_bank(bk, "VE")
                    if ocj == 1:
                        op("vector", lambda e: e.engine_nop(), incs=[("WF", 1)], drain=False)
            return C["VE"]

        stxa_full = [16 * max(1, NT // 2)]
        ve_split_t = {}
        pe_xsp_par = [0, 0]
        ve_x_par = [0, 0]
        pend1 = {}
        rb1 = {}

        def p1_load_stats(t):
            par = t % 2
            cols = slice(t * TT, (t + 1) * TT)
            xt = x_tile2[par]
            sem = "LDX0" if par == 0 else "LDX1"
            op("sync", lambda e, xt=xt, cols=cols: e.dma_start(
                out=xt.ap().rearrange("p (k c) -> p k c", k=NKD),
                in_=x0T[:, cols].rearrange("(k p) c -> p k c", p=128)),
               waits=[("VE", ve_x_par[par])], incs=[(sem, 16)])
            rb1[t] = ln_stats(xt, [(sem, C[sem])])

        def p1_splits(t):
            par = t % 2
            ve_split_t[t] = ln_split(x_tile2[par], rb1.pop(t), xsp2[par], pe_xsp_par[par])

        def p1_lnb(t):
            r_tile = r2b2[0]
            ve_r = pend1[t][0]
            ve_rb2 = ln_stats(r_tile, [("VE", ve_r)])
            for kc in range(NKD):
                cs = slice(kc * TT, (kc + 1) * TT)
                w = []
                if kc == 0:
                    w = [("VE", ve_rb2), S["st_xlnb"]]
                op("vector", lambda e, cs=cs, r_tile=r_tile: e.tensor_mul(z_tile[:, :], r_tile[:, cs], Rb[:, :]),
                   waits=w, incs=[("VE", 1)])
                op("vector", lambda e, cs=cs: e.tensor_sub(xlnb_tile[:, cs], z_tile[:, :], Ab[:, :]),
                   incs=[("VE", 1)])
            pend1[t] = (ve_r, C["VE"])

        def p1_stores(t):
            cols = slice(t * TT, (t + 1) * TT)
            ve_r, ve_xlnb = pend1.pop(t)
            op("sync", lambda e, cols=cols: e.dma_start(
                out=resid[:, cols].rearrange("(k p) c -> p k c", p=128),
                in_=r2b2[0].ap().rearrange("p (k c) -> p k c", k=NKD)),
               waits=[("VE", ve_r)], incs=[("STR", 16)])
            S["st_r"] = C["STR"]
            half = (t * TT) // HC
            colh = (t * TT) % HC
            sxs = "STXA" if half == 0 else "STXB"
            op("sync", lambda e, half=half, colh=colh: e.dma_start(
                out=xg_in[half][:, colh:colh + TT].rearrange("(k p) c -> p k c", p=128),
                in_=xlnb_tile.ap().rearrange("p (k c) -> p k c", k=NKD)),
               waits=[("VE", ve_xlnb)], incs=[(sxs, 16)])
            S["st_xlnb"] = (sxs, C[sxs])

        p1_load_stats(0)
        p1_splits(0)
        for t in range(NT):
            par = t % 2
            inj = {}
            if t + 1 < NT:
                inj["A_end"] = [lambda tt=t + 1: p1_load_stats(tt)]
                inj["B_end"] = [lambda tt=t + 1: p1_splits(tt)]
            if t - 1 in pend1:
                inj.setdefault("A_end", []).append(lambda tt=t - 1: p1_stores(tt))
            pend1[t] = (0, 0)
            ve_r = ffn(ve_split_t.pop(t), x_tile2[par], r2b2[0],
                       lambda: ("STR", S["st_r"]), xsp2[par], inject=inj)
            pe_xsp_par[par] = ffn.pe_stageA
            ve_x_par[par] = ve_r
            pend1[t] = (ve_r, None)
            p1_lnb(t)
        p1_stores(NT - 1)
        op("gpsimd", lambda e: e.collective_compute(
            "AllGather", ALU.bypass, replica_groups=REPL_GROUPS,
            ins=[xg_in[0]], outs=[xg_out[0]]),
           waits=[("STXA", stxa_full[0])], incs=[("CC", 1)], drain=False)
        op("gpsimd", lambda e: e.collective_compute(
            "AllGather", ALU.bypass, replica_groups=REPL_GROUPS,
            ins=[xg_in[1]], outs=[xg_out[1]]),
           waits=[("STXB", C["STXB"])], incs=[("CC", 1)], drain=False)
        cc1 = C["CC"]

        f_pe, f_ac, f_ve = C["PE"], C["AC"], C["VE"]
        aw = [("PE", f_pe), ("AC", f_ac), ("VE", f_ve)]
        for dst, src in [(wq_sb, wq_i), (wk_sb, wk_i), (wv_sb, wv_i)]:
            op("sync", lambda e, d=dst, s=src: e.dma_start(
                out=d.ap().rearrange("p (k j) -> p k j", k=NKD),
                in_=s.rearrange("(k p) j -> p k j", p=128)),
               waits=aw, incs=[("LDA", 16)])
            aw = []
        op("sync", lambda e: e.dma_start(
            out=wb_sb.ap().rearrange("p (k j) -> p k j", j=1),
            in_=wb_i.rearrange("(k p) j -> p k j", p=128),
            allow_slow_non_contiguous=True),
           incs=[("LDA", 16)])
        ld_attnw = C["LDA"]
        op("vector", lambda e: e.memset(S_sb[:, :], 0.0),
           waits=[("PE", f_pe), ("AC", f_ac)], incs=[("VE", 1)])
        op("vector", lambda e: e.memset(utail[:, :], 0.0), incs=[("VE", 1)])
        ve_supd = C["VE"]

        NPIECE = max(1, GT // 512)
        PW = GT // NPIECE
        for g in range(NG):
            gt0 = g * GT
            blk = gt0 // TCORE
            col0 = gt0 % TCORE
            hb = col0 // HC
            colh = col0 % HC
            cc_need = 1 if hb == 0 else 2
            ld_xg = []
            for kc in range(NKD):
                w = [("PE", S["pe_xg"]), ("CC", cc_need)] if kc == 0 else []
                op("sync", lambda e, kc=kc, hb=hb, colh=colh: e.dma_start(
                    out=xg_sb[:, kc * GT:(kc + 1) * GT],
                    in_=xg_out[hb][blk * D + kc * 128: blk * D + kc * 128 + 128,
                                   colh:colh + GT]),
                   waits=w, incs=[("LDG", 16)])
                ld_xg.append(C["LDG"])

            for j in range(GCH):
                bk, wk_ = get_bank()
                for kc in range(NKD):
                    w = (wk_ + [("LDG", ld_xg[-1]), ("LDA", ld_attnw)]) if kc == 0 else []
                    op("tensor", lambda e, kc=kc, j=j, b=bk: e.matmul(
                        pb[b][:, 0:1],
                        xg_sb[:, kc * GT + j * CH: kc * GT + j * CH + 128],
                        wb_sb[:, kc:kc + 1],
                        start=(kc == 0), stop=(kc == NKD - 1)),
                       waits=w, incs=([("PE", 1)] if kc == NKD - 1 else ()))
                peb = C["PE"]
                op("scalar", lambda e, b=bk: e.activation(
                    bh_sb[:, :], pb[b][:, 0:1], AF.Tanh, scale=0.5),
                   waits=[("PE", peb), ("VE", S.get("ve_bh", 0))], incs=[("AC", 1)])
                rel_bank(bk, "AC")
                acb = C["AC"]
                op("vector", lambda e, j=j: e.tensor_scalar(
                    beta_sb[:, j:j + 1], bh_sb[:, :], 0.5, 0.5, ALU.mult, ALU.add),
                   waits=[("AC", acb)], incs=[("VE", 1)])
                S["ve_bh"] = C["VE"]
            ve_beta = C["VE"]

            for wi, (wsb, csb, dst) in enumerate([(wq_sb, c_q, qT_sb),
                                                  (wk_sb, c_k, kT_sb),
                                                  (wv_sb, c_v, vT_sb)]):
                for c2 in range(2):
                    op("vector", lambda e, c2=c2, wi=wi: e.tensor_copy(
                        u_sb[:, c2 * (GT + 3): c2 * (GT + 3) + 3],
                        utail[:, (wi * 2 + c2) * 3: (wi * 2 + c2) * 3 + 3]),
                       incs=[("VE", 1)])
                for c2 in range(2):
                    for piece in range(NPIECE):
                        bk, wk_ = get_bank()
                        w = wk_ + [("LDG", ld_xg[-1]), ("LDA", ld_attnw)]
                        for kc in range(NKD):
                            op("tensor", lambda e, kc=kc, c2=c2, piece=piece, b=bk, wsb=wsb: e.matmul(
                                pb[b][:, :PW],
                                wsb[:, kc * 256 + c2 * 128: kc * 256 + c2 * 128 + 128],
                                xg_sb[:, kc * GT + piece * PW: kc * GT + piece * PW + PW],
                                start=(kc == 0), stop=(kc == NKD - 1)),
                               waits=(w if kc == 0 else ()),
                               incs=([("PE", 1)] if kc == NKD - 1 else ()))
                        pep = C["PE"]
                        op("vector", lambda e, c2=c2, piece=piece, b=bk: e.tensor_copy(
                            u_sb[:, c2 * (GT + 3) + 3 + piece * PW: c2 * (GT + 3) + 3 + piece * PW + PW],
                            pb[b][:, :PW]),
                           waits=[("PE", pep), ("AC", S.get("ac_u", 0))], incs=[("VE", 1)])
                        rel_bank(bk, "VE")
                for c2 in range(2):
                    op("vector", lambda e, c2=c2, wi=wi: e.tensor_copy(
                        utail[:, (wi * 2 + c2) * 3: (wi * 2 + c2) * 3 + 3],
                        u_sb[:, c2 * (GT + 3) + GT: c2 * (GT + 3) + GT + 3]),
                       incs=[("VE", 1)])
                for c2 in range(2):
                    ub = c2 * (GT + 3)
                    dcol = c2 * GT
                    w = [("PE", S["pe_scan_q"])] if (c2 == 0) else []
                    op("vector", lambda e, ub=ub, dcol=dcol, csb=csb, dst=dst, c2=c2: e.tensor_scalar_mul(
                        dst[:, dcol:dcol + GT], u_sb[:, ub:ub + GT], csb[:, c2 * 4:c2 * 4 + 1]),
                       waits=w, incs=[("VE", 1)])
                    for tap in range(1, 4):
                        op("vector", lambda e, ub=ub, dcol=dcol, csb=csb, dst=dst, c2=c2, tap=tap: e.scalar_tensor_tensor(
                            dst[:, dcol:dcol + GT], u_sb[:, ub + tap:ub + tap + GT],
                            csb[:, c2 * 4 + tap:c2 * 4 + tap + 1], dst[:, dcol:dcol + GT],
                            ALU.mult, ALU.add),
                           incs=[("VE", 1)])
                ve_conv = C["VE"]
                for c2 in range(2):
                    op("scalar", lambda e, c2=c2, dst=dst: e.activation(
                        dst[:, c2 * GT:(c2 + 1) * GT], dst[:, c2 * GT:(c2 + 1) * GT], AF.Silu),
                       waits=([("VE", ve_conv)] if c2 == 0 else ()), incs=[("AC", 1)])
            ac_silu_all = C["AC"]

            for wi, dst in [(0, qT_sb), (1, kT_sb)]:
                for c2 in range(2):
                    w = [("PE", S["pe_l2"])]
                    if wi == 0 and c2 == 0:
                        w.append(("AC", ac_silu_all))
                    op("scalar", lambda e, c2=c2, dst=dst: e.activation(
                        u_sb[:, c2 * (GT + 3) + 3: c2 * (GT + 3) + 3 + GT],
                        dst[:, c2 * GT:(c2 + 1) * GT], AF.Square),
                       waits=w, incs=[("AC", 1)])
                ac_sq2 = C["AC"]
                S["ac_u"] = ac_sq2
                for piece in range(NPIECE):
                    bk, wk_ = get_bank([("AC", ac_sq2)])
                    for c2 in range(2):
                        op("tensor", lambda e, c2=c2, piece=piece, b=bk: e.matmul(
                            pb[b][0:1, :PW], ones_col[:, :],
                            u_sb[:, c2 * (GT + 3) + 3 + piece * PW: c2 * (GT + 3) + 3 + piece * PW + PW],
                            start=(c2 == 0), stop=(c2 == 1)),
                           waits=(wk_ if c2 == 0 else ()),
                           incs=([("PE", 1)] if c2 == 1 else ()))
                    pel = C["PE"]
                    S["pe_l2"] = pel
                    op("scalar", lambda e, b=bk: e.activation(
                        row_l2[0:1, :PW], pb[b][0:1, :PW], AF.Sqrt, bias=EPS_L2),
                       waits=[("PE", pel), ("VE", S["ve_wo_rd"])], incs=[("AC", 1)])
                    rel_bank(bk, "AC")
                    acl = C["AC"]
                    op("vector", lambda e: e.reciprocal(row_l2b[0:1, :PW], row_l2[0:1, :PW]),
                       waits=[("AC", acl)], incs=[("VE", 1)])
                    vel = C["VE"]
                    S["ve_wo_rd"] = vel
                    b2, w2_ = get_bank([("VE", vel)])
                    op("tensor", lambda e, b=b2: e.matmul(
                        pb[b][:, :PW], ones_row[:, :], row_l2b[0:1, :PW],
                        start=True, stop=True),
                       waits=w2_, incs=[("PE", 1)])
                    pebc = C["PE"]
                    for c2 in range(2):
                        op("vector", lambda e, c2=c2, piece=piece, b=b2, dst=dst: e.tensor_mul(
                            dst[:, c2 * GT + piece * PW: c2 * GT + piece * PW + PW],
                            dst[:, c2 * GT + piece * PW: c2 * GT + piece * PW + PW],
                            pb[b][:, :PW]),
                           waits=([("PE", pebc)] if c2 == 0 else ()), incs=[("VE", 1)])
                    rel_bank(b2, "VE")
            ve_qk = C["VE"]

            SKEW = 2

            def indep(j):
                slot = j % NSLOT
                jb = j * CH
                bcol = beta_sb[:, j:j + 1]

                def qs(c2):
                    return qT_sb[:, c2 * GT + jb: c2 * GT + jb + 128]

                def ks(c2):
                    return kT_sb[:, c2 * GT + jb: c2 * GT + jb + 128]

                def vs(c2):
                    return vT_sb[:, c2 * GT + jb: c2 * GT + jb + 128]

                slot_guard = [("PE", S["tail_pe"][slot])]
                for c2 in range(2):
                    bk, wk_ = get_bank([("VE", ve_qk), ("AC", ac_silu_all)])
                    op("tensor", lambda e, b=bk, c2=c2: e.transpose(
                        pb[b][:, :128], vs(c2), ident[:, :]),
                       waits=wk_, incs=[("PE", 1)])
                    pet = C["PE"]
                    op("vector", lambda e, b=bk, c2=c2, slot=slot: e.tensor_scalar_mul(
                        Bv_a[slot][:, c2 * 128:(c2 + 1) * 128], pb[b][:, :128], bcol),
                       waits=[("PE", pet), ("VE", ve_beta)] + (slot_guard if c2 == 0 else []),
                       incs=[("VE", 1)])
                    rel_bank(bk, "VE")
                for c2 in range(2):
                    bk, wk_ = get_bank()
                    op("tensor", lambda e, b=bk, c2=c2: e.transpose(
                        pb[b][:, :128], ks(c2), ident[:, :]),
                       waits=wk_, incs=[("PE", 1)])
                    pet = C["PE"]
                    op("vector", lambda e, b=bk, c2=c2, slot=slot: e.tensor_copy(
                        Ktm_a[slot][:, c2 * 128:(c2 + 1) * 128], pb[b][:, :128]),
                       waits=[("PE", pet)], incs=[("VE", 1)])
                    rel_bank(bk, "VE")
                bk, wk_ = get_bank()
                for c2 in range(2):
                    op("tensor", lambda e, b=bk, c2=c2: e.matmul(
                        pb[b][:, :128], ks(c2), ks(c2), start=(c2 == 0), stop=(c2 == 1)),
                       waits=(wk_ if c2 == 0 else ()),
                       incs=([("PE", 1)] if c2 == 1 else ()))
                pea = C["PE"]
                op("vector", lambda e, b=bk: e.scalar_tensor_tensor(
                    P_pp[0][:, :], pb[b][:, :128], bcol, m_trilneg[:, :], ALU.mult, ALU.mult),
                   waits=[("PE", pea), ("PE", S["pe_chain_prev"])], incs=[("VE", 1)])
                rel_bank(bk, "VE")
                ve_P = C["VE"]
                bk, wk_ = get_bank([("VE", ve_P)])
                op("tensor", lambda e, b=bk: e.transpose(pb[b][:, :128], P_pp[0][:, :], ident[:, :]),
                   waits=wk_, incs=[("PE", 1)])
                pet0 = C["PE"]
                op("vector", lambda e, b=bk: e.tensor_copy(T_pp[0][:, :], pb[b][:, :128]),
                   waits=[("PE", pet0)], incs=[("VE", 1)])
                rel_bank(bk, "VE")
                ve_T = C["VE"]
                op("vector", lambda e, slot=slot: e.tensor_add(
                    X_a[slot][:, :], T_pp[0][:, :], ident[:, :]), incs=[("VE", 1)])
                ve_X = C["VE"]
                for l in range(1, 7):
                    pprev, pcur = P_pp[(l - 1) % 2], P_pp[l % 2]
                    tprev, tcur = T_pp[(l - 1) % 2], T_pp[l % 2]
                    bk, wk_ = get_bank([("VE", ve_T), ("VE", ve_P)])
                    op("tensor", lambda e, b=bk, tp=tprev, pp=pprev: e.matmul(
                        pb[b][:, :128], tp[:, :], pp[:, :], start=True, stop=True),
                       waits=wk_, incs=[("PE", 1)])
                    pe1 = C["PE"]
                    bk2, wk2_ = get_bank()
                    op("tensor", lambda e, b=bk2, tp=tprev, pp=pprev: e.matmul(
                        pb[b][:, :128], pp[:, :], tp[:, :], start=True, stop=True),
                       waits=wk2_, incs=[("PE", 1)])
                    pe2 = C["PE"]
                    op("vector", lambda e, b=bk, pc=pcur: e.tensor_copy(pc[:, :], pb[b][:, :128]),
                       waits=[("PE", pe1)], incs=[("VE", 1)])
                    rel_bank(bk, "VE")
                    ve_P = C["VE"]
                    op("vector", lambda e, b=bk2, tc=tcur: e.tensor_copy(tc[:, :], pb[b][:, :128]),
                       waits=[("PE", pe2)], incs=[("VE", 1)])
                    rel_bank(bk2, "VE")
                    ve_T = C["VE"]
                    bk3, wk3_ = get_bank([("VE", ve_P), ("VE", ve_X)])
                    op("tensor", lambda e, b=bk3, pc=pcur, slot=slot: e.matmul(
                        pb[b][:, :128], pc[:, :], X_a[slot][:, :], start=True, stop=True),
                       waits=wk3_, incs=[("PE", 1)])
                    pex = C["PE"]
                    op("vector", lambda e, b=bk3, slot=slot: e.tensor_add(
                        X_a[slot][:, :], X_a[slot][:, :], pb[b][:, :128]),
                       waits=[("PE", pex)], incs=[("VE", 1)])
                    rel_bank(bk3, "VE")
                    ve_X = C["VE"]
                S["pe_chain_prev"] = C["PE"]
                S["ve_mt"][slot] = ve_X
                op("vector", lambda e, slot=slot: e.tensor_scalar(
                    Bkn_t[:, :], Ktm_a[slot][:, :], bcol, -1.0, ALU.mult, ALU.mult),
                   waits=[("PE", S["pe_bkn_prev"])], incs=[("VE", 1)])
                ve_bkn = C["VE"]
                for c2 in range(2):
                    bk, wk_ = get_bank([("VE", ve_bkn), ("VE", ve_X)])
                    op("tensor", lambda e, b=bk, c2=c2, slot=slot: e.matmul(
                        pb[b][:, :128], Bkn_t[:, c2 * 128:(c2 + 1) * 128], X_a[slot][:, :],
                        start=True, stop=True),
                       waits=wk_, incs=[("PE", 1)])
                    pew = C["PE"]
                    if c2 == 1:
                        S["pe_bkn_prev"] = pew
                    op("vector", lambda e, b=bk, c2=c2, slot=slot: e.tensor_copy(
                        W1Tn_a[slot][:, c2 * 128:(c2 + 1) * 128], pb[b][:, :128]),
                       waits=[("PE", pew)], incs=[("VE", 1)])
                    rel_bank(bk, "VE")
                bk, wk_ = get_bank()
                for c2 in range(2):
                    op("tensor", lambda e, b=bk, c2=c2: e.matmul(
                        pb[b][:, :128], ks(c2), qs(c2), start=(c2 == 0), stop=(c2 == 1)),
                       waits=(wk_ if c2 == 0 else ()),
                       incs=([("PE", 1)] if c2 == 1 else ()))
                pelt = C["PE"]
                op("vector", lambda e, b=bk, slot=slot: e.tensor_mul(
                    LT_a[slot][:, :], pb[b][:, :128], m_triu[:, :]),
                   waits=[("PE", pelt)], incs=[("VE", 1)])
                rel_bank(bk, "VE")
                S["ve_indep"][slot] = C["VE"]

            def tail(j):
                slot = j % NSLOT
                jb = j * CH
                nonlocal ve_supd

                def qs(c2):
                    return qT_sb[:, c2 * GT + jb: c2 * GT + jb + 128]

                bk, wk_ = get_bank([("VE", S["ve_indep"][slot]), ("VE", ve_supd),
                                    ("PE", S["pe_u_prev"])])
                op("tensor", lambda e, b=bk, slot=slot: e.matmul(
                    pb[b][:, :256], X_a[slot][:, :], Bv_a[slot][:, :],
                    start=True, stop=False),
                   waits=wk_)
                for c2 in range(2):
                    op("tensor", lambda e, b=bk, c2=c2, slot=slot: e.matmul(
                        pb[b][:, :256], W1Tn_a[slot][:, c2 * 128:(c2 + 1) * 128],
                        S_sb[:, c2 * 256:(c2 + 1) * 256],
                        start=False, stop=(c2 == 1)),
                       incs=([("PE", 1)] if c2 == 1 else ()))
                peu = C["PE"]
                op("vector", lambda e, b=bk: e.tensor_copy(U_sb[:, :], pb[b][:, :256]),
                   waits=[("PE", peu)], incs=[("VE", 1)])
                rel_bank(bk, "VE")
                ve_U = C["VE"]
                bo, wo_ = get_bank()
                for c2 in range(2):
                    op("tensor", lambda e, b=bo, c2=c2: e.matmul(
                        pb[b][:, :256], qs(c2), S_sb[:, c2 * 256:(c2 + 1) * 256],
                        start=(c2 == 0), stop=False),
                       waits=(wo_ if c2 == 0 else ()))
                op("tensor", lambda e, b=bo, slot=slot: e.matmul(
                    pb[b][:, :256], LT_a[slot][:, :], U_sb[:, :], start=False, stop=True),
                   waits=[("VE", ve_U)], incs=[("PE", 1)])
                peo = C["PE"]
                for c2 in range(2):
                    bk, wk_ = get_bank()
                    op("tensor", lambda e, b=bk, c2=c2, slot=slot: e.matmul(
                        pb[b][:, :256], Ktm_a[slot][:, c2 * 128:(c2 + 1) * 128], U_sb[:, :],
                        start=True, stop=True),
                       waits=wk_, incs=[("PE", 1)])
                    pes = C["PE"]
                    if c2 == 1:
                        S["pe_u_prev"] = pes
                        S["tail_pe"][slot] = pes
                    op("vector", lambda e, b=bk, c2=c2: e.tensor_add(
                        S_sb[:, c2 * 256:(c2 + 1) * 256], S_sb[:, c2 * 256:(c2 + 1) * 256],
                        pb[b][:, :256]),
                       waits=[("PE", pes)], incs=[("VE", 1)])
                    rel_bank(bk, "VE")
                ve_supd = C["VE"]
                op("vector", lambda e, b=bo: e.tensor_copy(o_sb[:, :], pb[b][:, :256]),
                   waits=[("PE", peo), ("AC", S["ac_osq"])], incs=[("VE", 1)])
                rel_bank(bo, "VE")
                ve_o = C["VE"]
                op("scalar", lambda e: e.activation(osq_sb[:, :], o_sb[:, :], AF.Square,
                                                    accum_out=ossq[:, :]),
                   waits=[("VE", ve_o), ("VE", S["ve_orr"])], incs=[("AC", 1)])
                op("scalar", lambda e: e.activation(orr[:, :], ossq[:, :], AF.Sqrt,
                                                    scale=1.0 / DH, bias=EPS_RMS),
                   incs=[("AC", 1)])
                ac_or = C["AC"]
                S["ac_osq"] = ac_or
                op("vector", lambda e: e.reciprocal(orinv[:, :], orr[:, :]),
                   waits=[("AC", ac_or)], incs=[("VE", 1)])
                S["ve_orr"] = C["VE"]
                op("vector", lambda e: e.tensor_scalar_mul(on_sb[:, :], o_sb[:, :], orinv[:, :]),
                   incs=[("VE", 1)])
                ve_on = C["VE"]
                for c2 in range(2):
                    bk, wk_ = get_bank([("VE", ve_on)])
                    op("tensor", lambda e, b=bk, c2=c2: e.transpose(
                        pb[b][:, :128], on_sb[:, c2 * 128:(c2 + 1) * 128], ident[:, :]),
                       waits=wk_, incs=[("PE", 1)])
                    peot = C["PE"]
                    w = [("PE", peot)]
                    if c2 == 0:
                        w.append(S["st_og"])
                    op("vector", lambda e, b=bk, c2=c2: e.tensor_scalar_mul(
                        oT_sb[:, c2 * 128:(c2 + 1) * 128], pb[b][:, :128], g_rms[:, c2:c2 + 1]),
                       waits=w, incs=[("VE", 1)])
                    rel_bank(bk, "VE")
                ve_ot = C["VE"]
                sc = gt0 + jb
                oh = sc // HS
                sco = sc % HS
                sos = "STOA" if oh == 0 else "STOB"
                op("sync", lambda e, oh=oh, sco=sco: e.dma_start(
                    out=og_in[oh][:, sco: sco + 128].rearrange("(c p) t -> p c t", p=128),
                    in_=oT_sb.ap().rearrange("p (c t) -> p c t", c=2)),
                   waits=[("VE", ve_ot)], incs=[(sos, 16)])
                S["st_og"] = (sos, C[sos])

            for j in range(GCH + SKEW):
                if j < GCH:
                    indep(j)
                if j >= SKEW:
                    tail(j - SKEW)
            S["pe_scan_q"] = C["PE"]
            S["pe_xg"] = C["PE"]
            if g == max(0, NG // 2 - 1):
                op("gpsimd", lambda e: e.collective_compute(
                    "AllGather", ALU.bypass, replica_groups=REPL_GROUPS,
                    ins=[og_in[0]], outs=[og_out[0]]),
                   waits=[("STOA", C["STOA"])], incs=[("CC", 1)], drain=False)

        op("gpsimd", lambda e: e.collective_compute(
            "AllGather", ALU.bypass, replica_groups=REPL_GROUPS,
            ins=[og_in[1]], outs=[og_out[1]]),
           waits=[("STOB", C["STOB"])], incs=[("CC", 1)], drain=False)
        cc2 = C["CC"]

        f3_pe, f3_ac = C["PE"], C["AC"]
        if me == "sync":
            pid = nc.partition_id(engines=[mybir.EngineType.SP])
            slice_reg = eng_handle.to_reg(pid % 4)

        first3 = [[("PE", f3_pe), ("AC", f3_ac)]]
        pe_y_kc = {}
        st3 = {"ve_ogyp": [0, 0], "ve_r2b": [0, 0], "ve_xln": 0, "ve_x3": 0,
               "pe_og": [0, 0]}
        ve_split3 = {}
        pend3 = {}
        pst3 = {}

        def q_loads_y(t):
            par = t % 2
            cols = slice(t * TT, (t + 1) * TT)
            ogt = og_yp2[par]
            C["LDO"] += 16
            if me == "sync":
                for s_, v_ in (first3[0] + [("PE", st3["pe_og"][par]), ("VE", st3["ve_ogyp"][par])]):
                    if v_ > 0:
                        eng_handle.wait_ge(sems[s_], v_)
                for sl in range(4):
                    with eng_handle.If_cmp(slice_reg, sl, "IS_EQ"):
                        c0 = sl * TCORE + t * TT
                        oh = c0 // HS
                        c0h = c0 % HS
                        eng_handle.wait_ge(sems["CC"], cc2 - 1 if oh == 0 else cc2)
                        eng_handle.dma_start(
                            out=ogt.ap().rearrange("p (k c) -> p k c", k=NKD),
                            in_=og_out[oh][:, c0h:c0h + TT].rearrange("(k p) c -> p k c", p=128),
                        ).then_inc(sems["LDO"], 16)
                eng_handle.end_ifs()
            first3[0] = []
            ld_og = C["LDO"]
            halfL = (t * TT) // HC
            colL = (t * TT) % HC
            op("sync", lambda e, halfL=halfL, colL=colL: e.dma_start(
                out=xlnb_tile.ap().rearrange("p (k c) -> p k c", k=NKD),
                in_=xg_in[halfL][:, colL:colL + TT].rearrange("(k p) c -> p k c", p=128)),
               waits=[("VE", st3["ve_xln"])], incs=[("LDL", 16)])
            ld_xln = C["LDL"]
            op("sync", lambda e, cols=cols: e.dma_start(
                out=x_tile2[0].ap().rearrange("p (k c) -> p k c", k=NKD),
                in_=resid[:, cols].rearrange("(k p) c -> p k c", p=128)),
               waits=[("VE", st3["ve_x3"]), ("STR", S["st_r"])], incs=[("LDR", 16)])
            ld_r1 = C["LDR"]
            ld_wo = {}

            def load_wo(kc):
                sem = "LDW0" if kc % 2 == 0 else "LDW1"
                key = (kc - 2) % NKD if pe_y_kc else None
                w = [("PE", pe_y_kc.get(key, 0))] if key is not None else []
                op("sync", lambda e, kc=kc: e.dma_start(
                    out=wo_buf[kc % 2][:, :], in_=wo_i[kc * 128:(kc + 1) * 128, :]),
                   waits=w, incs=[(sem, 16)])
                ld_wo[kc] = (sem, C[sem])

            load_wo(0)
            load_wo(1)
            ybanks = []
            for oc in range(NKD):
                bk, wk_ = get_bank()
                ybanks.append((bk, wk_))
            for kc in range(NKD):
                for oc in range(NKD):
                    bk, wk_ = ybanks[oc]
                    w = []
                    if kc == 0:
                        w = list(wk_) + [("LDO", ld_og)]
                    if oc == 0:
                        w = w + [ld_wo[kc]]
                    op("tensor", lambda e, kc=kc, oc=oc, b=bk, ogt=ogt: e.matmul(
                        pb[b][:, :TT], wo_buf[kc % 2][:, oc * 128:(oc + 1) * 128],
                        ogt[:, kc * TT:(kc + 1) * TT],
                        start=(kc == 0), stop=(kc == NKD - 1)),
                       waits=w,
                       incs=([("PE", 1)] if oc == NKD - 1 else ()))
                pe_y_kc[kc] = C["PE"]
                if kc + 2 < NKD:
                    load_wo(kc + 2)
            pe_y = C["PE"]
            st3["pe_og"][par] = pe_y
            for oc in range(NKD):
                bk, _ = ybanks[oc]
                cs = slice(oc * TT, (oc + 1) * TT)
                w = [("PE", pe_y)]
                if oc == 0:
                    w += [("LDL", ld_xln)]
                op("vector", lambda e, b=bk, cs=cs, ogt=ogt: e.tensor_add(
                    ogt[:, cs], pb[b][:, :TT], xlnb_tile[:, cs]),
                   waits=w, incs=[("VE", 1)])
                rel_bank(bk, "VE")
            pst3[t] = {"ve_yp": C["VE"], "ld_r1": ld_r1}
            st3["ve_xln"] = C["VE"]

        def q_stats1(t):
            par = t % 2
            pst3[t]["ve_rb3"] = ln_stats(og_yp2[par], [("VE", pst3[t]["ve_yp"])])

        def q_zchain(t):
            par = t % 2
            ogt = og_yp2[par]
            r2t = r2b2[par]
            for kc in range(NKD):
                cs = slice(kc * TT, (kc + 1) * TT)
                w = []
                if kc == 0:
                    w = [("VE", pst3[t]["ve_rb3"])]
                op("vector", lambda e, cs=cs, ogt=ogt: e.tensor_mul(z_tile[:, :], ogt[:, cs], Rb[:, :]),
                   waits=w, incs=[("VE", 1)])
                op("vector", lambda e: e.tensor_sub(z_tile[:, :], z_tile[:, :], Ab[:, :]),
                   incs=[("VE", 1)])
                ve_z = C["VE"]
                op("scalar", lambda e, kc=kc: e.activation(
                    e_tile[:, :], z_tile[:, :], AF.Identity,
                    bias=ln_b[:, kc:kc + 1], scale=ln_g[:, kc:kc + 1]),
                   waits=[("VE", ve_z)], incs=[("AC", 1)])
                ac_z = C["AC"]
                w = [("AC", ac_z)]
                if kc == 0:
                    w += [("LDR", pst3[t]["ld_r1"]), ("VE", st3["ve_r2b"][par])]
                op("vector", lambda e, cs=cs, r2t=r2t: e.tensor_add(
                    r2t[:, cs], e_tile[:, :], x_tile2[0][:, cs]),
                   waits=w, incs=[("VE", 1)])
            ve_r2 = C["VE"]
            pst3[t]["ve_r2"] = ve_r2
            st3["ve_ogyp"][par] = ve_r2
            st3["ve_x3"] = ve_r2

        def q_stats2(t):
            par = t % 2
            pst3[t]["ve_rb4"] = ln_stats(r2b2[par], [("VE", pst3[t]["ve_r2"])])

        def q_splits(t):
            par = t % 2
            ve_split3[t] = ln_split(r2b2[par], pst3[t]["ve_rb4"], xsp2[par], pe_xsp_par[par])
            pst3.pop(t)

        def p3_store(t):
            cols = slice(t * TT, (t + 1) * TT)
            ve_out = pend3.pop(t)
            op("sync", lambda e, cols=cols: e.dma_start(
                out=outT[:, cols].rearrange("(k p) c -> p k c", p=128),
                in_=out3.ap().rearrange("p (k c) -> p k c", k=NKD)),
               waits=[("VE", ve_out)], incs=[("STU", 16)])
            S["st_out"] = C["STU"]

        out3 = x_tile2[1]
        q_loads_y(0)
        q_stats1(0)
        q_zchain(0)
        q_stats2(0)
        q_splits(0)
        for t in range(NT):
            par = t % 2
            inj = {}
            if t + 1 < NT:
                inj["A_end"] = [lambda tt=t + 1: q_loads_y(tt)]
                inj["B_mid"] = [lambda tt=t + 1: q_stats1(tt)]
                inj["B_end"] = [lambda tt=t + 1: q_zchain(tt)]
                inj["C_mid"] = [lambda tt=t + 1: q_stats2(tt)]
                inj["C_end"] = [lambda tt=t + 1: q_splits(tt)]
            if t - 1 in pend3:
                inj.setdefault("A_end", []).append(lambda tt=t - 1: p3_store(tt))
            ve_out = ffn(ve_split3.pop(t), r2b2[par], out3,
                         lambda: ("STU", S["st_out"]), xsp2[par], inject=inj)
            pe_xsp_par[par] = ffn.pe_stageA
            st3["ve_r2b"][par] = ve_out
            pend3[t] = ve_out
        p3_store(NT - 1)

        waitonly("sync", [("STU", C["STU"]), ("STR", C["STR"]), ("STXA", C["STXA"]),
                          ("STXB", C["STXB"]), ("STOA", C["STOA"]), ("STOB", C["STOB"])])
        return C

    counts = {}
    with nc.Block() as block:
        def mk(name):
            def f(eng):
                counts[name] = emit(name, eng)
            return f
        block.sync(mk("sync"))
        block.gpsimd(mk("gpsimd"))
        block.tensor(mk("tensor"))
        block.vector(mk("vector"))
        block.scalar(mk("scalar"))

    base = counts["sync"]
    for k, c in counts.items():
        assert c == base, f"counter divergence in {k}"
    return nc


def _prep_shared(inp):
    f = np.float32
    w1T = np.ascontiguousarray(np.asarray(inp["L0_ffn_w1_w"], dtype=f).T)
    w2T = np.ascontiguousarray(np.asarray(inp["L0_ffn_w2_w"], dtype=f).T)
    h1T = np.ascontiguousarray(np.asarray(inp["L0_ffn_h1_w"], dtype=f).T)
    h2T = np.ascontiguousarray(np.asarray(inp["L0_ffn_h2_w"], dtype=f).T)
    w3T = np.ascontiguousarray(np.asarray(inp["L0_ffn_w3_w"], dtype=f).T)

    w12t = np.empty((8, NKD, 128, 1024), f)
    w12t[:, :, :, :512] = w1T.reshape(NKD, 128, 8, 512).transpose(2, 0, 1, 3)
    w12t[:, :, :, 512:] = w2T.reshape(NKD, 128, 8, 512).transpose(2, 0, 1, 3)
    h1t = np.ascontiguousarray(h1T.reshape(NOH, 128, 16, 256).transpose(2, 0, 1, 3))
    h2t = np.ascontiguousarray(h2T.reshape(NOH, 128, 16, 256).transpose(2, 0, 1, 3))
    w3t = np.ascontiguousarray(w3T.reshape(NOH, 128, 4, 256).transpose(2, 0, 1, 3))

    tile_s = lambda s, n: np.ascontiguousarray(np.asarray(s, dtype=f).reshape(n, 128).T)
    return {
        "w12t": np.ascontiguousarray(w12t),
        "h1t": h1t, "h2t": h2t, "w3t": w3t,
        "w1s": tile_s(inp["L0_ffn_w1_s"], NOH),
        "w2s": tile_s(inp["L0_ffn_w2_s"], NOH),
        "h1s": tile_s(inp["L0_ffn_h1_s"], NOH),
        "h2s": tile_s(inp["L0_ffn_h2_s"], NOH),
        "w3s": tile_s(inp["L0_ffn_w3_s"], NKD),
        "wo": np.ascontiguousarray(np.asarray(inp["L0_attn_Wo"], dtype=f)),
        "grms": tile_s(inp["L0_attn_g_rms"], 2),
        "lngt": tile_s(inp["L0_attn_ln_g"], NKD),
        "lnbt": tile_s(inp["L0_attn_ln_b"], NKD),
    }


def _prep_core(inp, c):
    f = np.float32
    b, s = c // 4, c % 4
    h = c % 4
    x = np.asarray(inp["L0_attn_x"], dtype=f)
    x0T = np.ascontiguousarray(x[b, s * TCORE:(s + 1) * TCORE, :].T)
    conv_tile = lambda a: np.ascontiguousarray(
        np.asarray(a, dtype=f)[:, h * DH:(h + 1) * DH].T.reshape(2, 128, 4)
        .transpose(1, 0, 2).reshape(128, 8))
    return {
        "x0T": x0T,
        "wq": np.ascontiguousarray(np.asarray(inp["L0_attn_Wq"], dtype=f)[:, h * DH:(h + 1) * DH]),
        "wk": np.ascontiguousarray(np.asarray(inp["L0_attn_Wk"], dtype=f)[:, h * DH:(h + 1) * DH]),
        "wv": np.ascontiguousarray(np.asarray(inp["L0_attn_Wv"], dtype=f)[:, h * DH:(h + 1) * DH]),
        "wb": np.ascontiguousarray(np.asarray(inp["L0_attn_Wb"], dtype=f)[:, h:h + 1]),
        "convq": conv_tile(inp["L0_attn_conv_q"]),
        "convk": conv_tile(inp["L0_attn_conv_k"]),
        "convv": conv_tile(inp["L0_attn_conv_v"]),
    }


def kernel(**inputs):
    shared = _prep_shared(inputs)
    nc = build_nc()
    in_maps = []
    for c in range(8):
        m = dict(shared)
        m.update(_prep_core(inputs, c))
        in_maps.append(m)
    res = run_bass_kernel_spmd(nc, in_maps, core_ids=list(range(8)))
    out = np.empty((NB, TSEQ, D), dtype=np.float32)
    for c in range(8):
        b, s = c // 4, c % 4
        out[b, s * TCORE:(s + 1) * TCORE, :] = res.results[c]["outT"].T
    return out


# version 08
# speedup vs baseline: 1.0511x; 1.0511x over previous
import numpy as np

try:
    import concourse.bass as bass
    import concourse.mybir as mybir
    from concourse.bass_utils import run_bass_kernel_spmd
except ImportError:
    import sys
    sys.path.insert(0, "/opt/trn_rl_repo")
    import concourse.bass as bass
    import concourse.mybir as mybir
    from concourse.bass_utils import run_bass_kernel_spmd

F32 = mybir.dt.float32
BF16 = mybir.dt.bfloat16
FP16 = mybir.dt.float16
AF = mybir.ActivationFunctionType
ALU = mybir.AluOpType

D = 1024
HID = 4096
DH = 256
H = 4
NB = 2
TSEQ = 8192
TCORE = 2048
TT = 256
CH = 128
GCH = 8
GT = CH * GCH
NKD = D // 128
NOH = HID // 128
NT = TCORE // TT
NG = TSEQ // GT
NSLICE = TSEQ // TCORE
EPS_LN = 1e-5
REPL_GROUPS = [[0, 1, 2, 3], [4, 5, 6, 7]]
EPS_L2 = 1e-6
EPS_RMS = 1e-5

SEM_NAMES = ("LDC", "LDX0", "LDX1", "LDR", "LDG", "LDA", "LDO", "LDL", "LDW0", "LDW1", "WS0", "WS1",
             "WL0", "WL1", "WF", "PE", "VE", "AC", "GP", "STR", "STXA", "STXB", "STOA", "STOB", "STU", "CC")


class SBAlloc:
    def __init__(self, nc, base, limit):
        self.nc = nc
        self.cur = base
        self.limit = limit

    def alloc(self, name, shape, dtype, align=64):
        per = int(np.prod(shape[1:])) * mybir.dt.size(dtype)
        off = (self.cur + align - 1) // align * align
        t = self.nc.alloc_sbuf_tensor_at(name, list(shape), dtype, offset=off)
        self.cur = off + per
        assert self.cur <= self.limit, f"SBUF overflow at {name}: {self.cur} > {self.limit}"
        return t


def build_nc():
    nc = bass.Bass("TRN2", target_bir_lowering=False)

    def din(name, shape):
        return nc.dram_tensor(name, list(shape), F32, kind="ExternalInput").ap()

    x0T = din("x0T", [D, TCORE])
    w12t = din("w12t", [8, NKD, 128, 1024])
    h1t = din("h1t", [16, NOH, 128, 256])
    h2t = din("h2t", [16, NOH, 128, 256])
    w3t = din("w3t", [4, NOH, 128, 256])
    w1s_i = din("w1s", [128, NOH])
    w2s_i = din("w2s", [128, NOH])
    h1s_i = din("h1s", [128, NOH])
    h2s_i = din("h2s", [128, NOH])
    w3s_i = din("w3s", [128, NKD])
    wq_i = din("wq", [D, DH])
    wk_i = din("wk", [D, DH])
    wv_i = din("wv", [D, DH])
    wb_i = din("wb", [D, 1])
    convq_i = din("convq", [128, 8])
    convk_i = din("convk", [128, 8])
    convv_i = din("convv", [128, 8])
    wo_i = din("wo", [D, D])
    grms_i = din("grms", [128, 2])
    lng_i = din("lngt", [128, NKD])
    lnb_i = din("lnbt", [128, NKD])
    outT = nc.dram_tensor("outT", [D, TCORE], F32, kind="ExternalOutput").ap()

    resid = nc.dram_tensor("resid", [D, TCORE], F32).ap()
    HC = TCORE // 2
    HS = TSEQ // 2
    xg_in = [nc.dram_tensor(f"xg_in{i}", [D, HC], F32).ap() for i in range(2)]
    xg_out = [nc.dram_tensor(f"xg_out{i}", [NSLICE * D, HC], F32).ap() for i in range(2)]
    og_in = [nc.dram_tensor(f"og_in{i}", [DH, HS], F32).ap() for i in range(2)]
    og_out = [nc.dram_tensor(f"og_out{i}", [H * DH, HS], F32).ap() for i in range(2)]
    wsc = nc.dram_tensor("wsc", [44, 128, 8192], FP16).ap()

    ARENA = 205 * 1024
    arena_base = nc.sbuf_base
    nc.alloc_sbuf_tensor("arena", [128, ARENA], mybir.dt.uint8)
    arena_limit = arena_base + ARENA
    A = SBAlloc(nc, arena_base, arena_limit)
    ones_col = A.alloc("ones_col", [128, 1], F32)
    ones_row = A.alloc("ones_row", [1, 128], F32)
    iota_t = A.alloc("iota_t", [128, 128], F32)
    ident = A.alloc("ident", [128, 128], F32)
    m_trilneg = A.alloc("m_trilneg", [128, 128], F32)
    m_triu = A.alloc("m_triu", [128, 128], F32)
    s_w1 = A.alloc("s_w1", [128, NOH], F32)
    s_w2 = A.alloc("s_w2", [128, NOH], F32)
    s_h1 = A.alloc("s_h1", [128, NOH], F32)
    s_h2 = A.alloc("s_h2", [128, NOH], F32)
    s_w3 = A.alloc("s_w3", [128, NKD], F32)
    s_w3h = A.alloc("s_w3h", [128, NKD], F32)
    c_q = A.alloc("c_q", [128, 8], F32)
    c_k = A.alloc("c_k", [128, 8], F32)
    c_v = A.alloc("c_v", [128, 8], F32)
    g_rms = A.alloc("g_rms", [128, 2], F32)
    ln_g = A.alloc("ln_g", [128, NKD], F32)
    ln_b = A.alloc("ln_b", [128, NKD], F32)
    row_mu = A.alloc("row_mu", [1, TT], F32)
    row_m2 = A.alloc("row_m2", [1, TT], F32)
    row_t = A.alloc("row_t", [1, TT], F32)
    row_var = A.alloc("row_var", [1, TT], F32)
    row_sd = A.alloc("row_sd", [1, TT], F32)
    row_ri = A.alloc("row_ri", [1, TT], F32)
    row_a = A.alloc("row_a", [1, TT], F32)
    row_l2 = A.alloc("row_l2", [1, 512], F32)
    row_l2b = A.alloc("row_l2b", [1, 512], F32)
    x_tile2 = [A.alloc(f"x_tile{i}", [128, NKD * TT], F32) for i in range(2)]
    r2b2 = [A.alloc(f"r2b{i}", [128, NKD * TT], F32) for i in range(2)]
    xlnb_tile = A.alloc("xlnb_tile", [128, NKD * TT], F32)
    sq_tile = A.alloc("sq_tile", [128, 2 * TT], F32)
    z_tile = A.alloc("z_tile", [128, TT], F32)
    e_tile = A.alloc("e_tile", [128, TT], F32)
    silu_t = A.alloc("silu_t", [128, TT], F32)
    hf_t = [A.alloc(f"hf_t{i}", [128, TT], F32) for i in range(2)]
    Rb = A.alloc("Rb", [128, TT], F32)
    Ab = A.alloc("Ab", [128, TT], F32)
    og_yp2 = [A.alloc(f"og_yp{i}", [128, NKD * TT], F32) for i in range(2)]
    const_vals = [EPS_LN, EPS_L2, 0.5, 1.0 / DH]
    const_ts = []
    for i, v in enumerate(sorted(set(const_vals))):
        tcv = A.alloc(f"cst{i}", [128, 1], F32)
        nc.const_aps.register(F32, v, tcv[:, :])
        const_ts.append((tcv, v))
    wo_buf = [A.alloc(f"wo_buf{i}", [128, D], F32) for i in range(2)]

    perm_end = A.cur
    Fz = SBAlloc(nc, perm_end, arena_limit)
    xsp2 = [[Fz.alloc(f"xsp{q}_{p}", [128, NKD * TT], FP16) for p in range(2)] for q in range(2)]
    wbuf = [Fz.alloc(f"wbuf{p}", [128, 8192], FP16) for p in range(2)]
    hspA = [Fz.alloc(f"hspA{p}", [128, NOH * TT], FP16) for p in range(2)]
    hspB = [Fz.alloc(f"hspB{p}", [128, NOH * TT], FP16) for p in range(2)]

    P2 = SBAlloc(nc, perm_end, arena_limit)
    xg_sb = P2.alloc("xg_sb", [128, NKD * GT], F32)
    u_sb = P2.alloc("u_sb", [128, 2 * (GT + 3)], F32)
    qT_sb = P2.alloc("qT_sb", [128, 2 * GT], F32)
    kT_sb = P2.alloc("kT_sb", [128, 2 * GT], F32)
    vT_sb = P2.alloc("vT_sb", [128, 2 * GT], F32)
    wq_sb = P2.alloc("wq_sb", [128, NKD * 256], F32)
    wk_sb = P2.alloc("wk_sb", [128, NKD * 256], F32)
    wv_sb = P2.alloc("wv_sb", [128, NKD * 256], F32)
    wb_sb = P2.alloc("wb_sb", [128, NKD], F32)
    beta_sb = P2.alloc("beta_sb", [128, GCH], F32)
    bh_sb = P2.alloc("bh_sb", [128, 1], F32)
    S_sb = P2.alloc("S_sb", [128, 512], F32)
    NSLOT = 4
    P_pp = [P2.alloc(f"P_pp{i}", [128, 128], F32) for i in range(2)]
    T_pp = [P2.alloc(f"T_pp{i}", [128, 128], F32) for i in range(2)]
    X_a = [P2.alloc(f"X_a{i}", [128, 128], F32) for i in range(NSLOT)]
    W1Tn_a = [P2.alloc(f"W1Tn_a{i}", [128, 256], F32) for i in range(NSLOT)]
    Bv_a = [P2.alloc(f"Bv_a{i}", [128, 256], F32) for i in range(NSLOT)]
    LT_a = [P2.alloc(f"LT_a{i}", [128, 128], F32) for i in range(NSLOT)]
    Ktm_a = [P2.alloc(f"Ktm_a{i}", [128, 256], F32) for i in range(NSLOT)]
    Bkn_t = P2.alloc("Bkn_t", [128, 256], F32)
    U_sb = P2.alloc("U_sb", [128, 256], F32)
    o_sb = P2.alloc("o_sb", [128, 256], F32)
    on_sb = P2.alloc("on_sb", [128, 256], F32)
    osq_sb = P2.alloc("osq_sb", [128, 256], F32)
    ossq = P2.alloc("ossq", [128, 1], F32)
    orr = P2.alloc("orr", [128, 1], F32)
    orinv = P2.alloc("orinv", [128, 1], F32)
    oT_sb = P2.alloc("oT_sb", [128, 256], F32)
    utail = P2.alloc("utail", [128, 18], F32)

    pb = [nc.alloc_psum_tensor(f"pb{i}", [128, 512], F32) for i in range(8)]

    sems = {s: nc.alloc_semaphore(s) for s in SEM_NAMES}

    def emit(me, eng_handle):
        C = {k: 0 for k in SEM_NAMES}
        bank_free = [None] * 8
        rr = [0]

        def op(engine, build, waits=(), incs=(), drain=True):
            assert len(incs) <= 1, "one semaphore update per instruction"
            for s, n in incs:
                C[s] += n
            if me == engine:
                for s, v in waits:
                    if v > 0:
                        eng_handle.wait_ge(sems[s], v)
                ins = build(eng_handle)
                for s, n in incs:
                    ins = ins.then_inc(sems[s], n)
                if drain and engine in ("vector", "scalar", "gpsimd"):
                    eng_handle.drain()

        def waitonly(engine, waits):
            if me == engine:
                for s, v in waits:
                    if v > 0:
                        eng_handle.wait_ge(sems[s], v)

        def get_bank(extra=()):
            b = rr[0] % 8
            rr[0] += 1
            w = list(extra)
            if bank_free[b] is not None:
                w.append(bank_free[b])
            return b, w

        def rel_bank(b, sem):
            bank_free[b] = (sem, C[sem])

        for dst, src in [(s_w1, w1s_i), (s_w2, w2s_i), (s_h1, h1s_i), (s_h2, h2s_i),
                         (s_w3, w3s_i), (c_q, convq_i), (c_k, convk_i), (c_v, convv_i),
                         (g_rms, grms_i), (ln_g, lng_i), (ln_b, lnb_i)]:
            op("sync", lambda e, d=dst, s=src: e.dma_start(out=d[:, :], in_=s),
               incs=[("LDC", 16)])
        ld_consts = C["LDC"]

        op("gpsimd", lambda e: e.iota(iota_t[:, :], pattern=[[1, 128]], base=0,
                                      channel_multiplier=-1,
                                      allow_small_or_imprecise_dtypes=True),
           incs=[("GP", 1)])
        gp_iota = C["GP"]
        op("vector", lambda e: e.tensor_scalar(m_trilneg[:, :], iota_t[:, :], 0.0, -1.0,
                                               ALU.is_lt, ALU.mult),
           waits=[("GP", gp_iota)], incs=[("VE", 1)])
        op("vector", lambda e: e.tensor_single_scalar(m_triu[:, :], iota_t[:, :], 0.0, ALU.is_ge),
           incs=[("VE", 1)])
        op("vector", lambda e: e.tensor_single_scalar(ident[:, :], iota_t[:, :], 0.0, ALU.is_equal),
           incs=[("VE", 1)])
        op("vector", lambda e: e.memset(ones_col[:, :], 1.0), incs=[("VE", 1)])
        op("vector", lambda e: e.memset(ones_row[:, :], 1.0), incs=[("VE", 1)])
        for tcv, v in const_ts:
            op("vector", lambda e, tcv=tcv, v=v: e.memset(tcv[:, :], v), incs=[("VE", 1)])
        op("vector", lambda e: e.tensor_scalar_mul(s_w3h[:, :], s_w3[:, :], 0.5),
           waits=[("LDC", ld_consts)], incs=[("VE", 1)])
        ve_setup = C["VE"]

        S = {
            "pe_sq": 0,
            "ac_rows": 0,
            "pe_rows": 0,
            "ve_silu": 0,
            "ve_hf": [0, 0],
            "st_r": 0, "st_xlnb": ("STXA", 0), "st_out": 0, "st_og": ("STOA", 0),
            "ve_x_read": 0,
            "pe_og_read": 0,
            "ve_yp_read": 0,
            "ve_xlnb_read": 0,
            "pe_l2": 0,
            "ac_osq": 0,
            "ve_orr": 0,
            "pe_scan_q": 0,
            "pe_chain_prev": 0,
            "pe_bkn_prev": 0,
            "pe_u_prev": 0,
            "tail_pe": [0, 0, 0, 0],
            "ve_mt": [0, 0, 0, 0],
            "ve_indep": [0, 0, 0, 0],
            "pe_xg": 0,
            "ve_wo_rd": 0,
        }

        def ln_stats(src, src_ready):
            b1, w1_ = get_bank(list(src_ready) + [("VE", ve_setup)])
            b2, w2_ = get_bank()
            pe_mm2 = {}
            for kc in range(NKD):
                sqs = slice((kc % 2) * TT, (kc % 2) * TT + TT)
                w = []
                if kc == 0:
                    w = list(src_ready) + [("PE", S["pe_sq"])]
                elif kc >= 2:
                    w = [("PE", pe_mm2[kc - 2])]
                op("scalar", lambda e, kc=kc, sqs=sqs: e.activation(
                    sq_tile[:, sqs], src[:, kc * TT:(kc + 1) * TT], AF.Square),
                   waits=w, incs=[("AC", 1)])
                ac_kc = C["AC"]
                op("tensor", lambda e, kc=kc, b=b1: e.matmul(
                    pb[b][0:1, :TT], ones_col[:, :], src[:, kc * TT:(kc + 1) * TT],
                    start=(kc == 0), stop=(kc == NKD - 1)),
                   waits=(w1_ if kc == 0 else ()),
                   incs=([("PE", 1)] if kc == NKD - 1 else ()))
                if kc == NKD - 1:
                    pe_s1 = C["PE"]
                op("tensor", lambda e, kc=kc, b=b2, sqs=sqs: e.matmul(
                    pb[b][0:1, :TT], ones_col[:, :], sq_tile[:, sqs],
                    start=(kc == 0), stop=(kc == NKD - 1)),
                   waits=(list(w2_) if kc == 0 else []) + [("AC", ac_kc)],
                   incs=[("PE", 1)])
                pe_mm2[kc] = C["PE"]
            pe_s2 = C["PE"]
            S["pe_sq"] = pe_s2
            op("vector", lambda e, b=b1: e.tensor_scalar_mul(row_mu[:, :], pb[b][0:1, :TT], 1.0 / D),
               waits=[("PE", pe_s1), ("AC", S["ac_rows"]), ("PE", S["pe_rows"])],
               incs=[("VE", 1)])
            op("vector", lambda e, b=b2: e.tensor_scalar_mul(row_m2[:, :], pb[b][0:1, :TT], 1.0 / D),
               waits=[("PE", pe_s2)], incs=[("VE", 1)])
            rel_bank(b1, "VE")
            rel_bank(b2, "VE")
            op("vector", lambda e: e.tensor_mul(row_t[:, :], row_mu[:, :], row_mu[:, :]),
               incs=[("VE", 1)])
            op("vector", lambda e: e.tensor_sub(row_var[:, :], row_m2[:, :], row_t[:, :]),
               incs=[("VE", 1)])
            ve_var = C["VE"]
            op("scalar", lambda e: e.activation(row_sd[:, :], row_var[:, :], AF.Sqrt, bias=EPS_LN),
               waits=[("VE", ve_var)], incs=[("AC", 1)])
            ac_sd = C["AC"]
            S["ac_rows"] = ac_sd
            op("vector", lambda e: e.reciprocal(row_ri[:, :], row_sd[:, :]),
               waits=[("AC", ac_sd)], incs=[("VE", 1)])
            op("vector", lambda e: e.tensor_mul(row_a[:, :], row_mu[:, :], row_ri[:, :]),
               incs=[("VE", 1)])
            ve_rows = C["VE"]
            b3, w3_ = get_bank([("VE", ve_rows)])
            op("tensor", lambda e, b=b3: e.matmul(pb[b][:, :TT], ones_row[:, :], row_ri[0:1, :TT],
                                                  start=True, stop=True),
               waits=w3_, incs=[("PE", 1)])
            pe_b1 = C["PE"]
            b4, w4_ = get_bank()
            op("tensor", lambda e, b=b4: e.matmul(pb[b][:, :TT], ones_row[:, :], row_a[0:1, :TT],
                                                  start=True, stop=True),
               waits=w4_, incs=[("PE", 1)])
            pe_b2 = C["PE"]
            S["pe_rows"] = pe_b2
            op("vector", lambda e, b=b3: e.tensor_copy(Rb[:, :], pb[b][:, :TT]),
               waits=[("PE", pe_b1)], incs=[("VE", 1)])
            rel_bank(b3, "VE")
            op("vector", lambda e, b=b4: e.tensor_copy(Ab[:, :], pb[b][:, :TT]),
               waits=[("PE", pe_b2)], incs=[("VE", 1)])
            rel_bank(b4, "VE")
            return C["VE"]

        def ln_split(src, ve_rb, xsp, pe_guard=0):
            for kc in range(NKD):
                cs = slice(kc * TT, (kc + 1) * TT)
                op("vector", lambda e, cs=cs: e.tensor_mul(z_tile[:, :], src[:, cs], Rb[:, :]),
                   waits=([("VE", ve_rb), ("PE", pe_guard)] if kc == 0 else ()),
                   incs=[("VE", 1)])
                op("vector", lambda e: e.tensor_sub(z_tile[:, :], z_tile[:, :], Ab[:, :]),
                   incs=[("VE", 1)])
                op("vector", lambda e, cs=cs, xsp=xsp: e.tensor_copy(xsp[0][:, cs], z_tile[:, :]),
                   incs=[("VE", 1)])
                op("vector", lambda e, cs=cs, xsp=xsp: e.tensor_sub(xsp[1][:, cs], z_tile[:, :], xsp[0][:, cs]),
                   incs=[("VE", 1)])
            return C["VE"]

        wl_state = {"idx": 0}
        NGRP_W = 44

        def wload(src_ap, nk, sub):
            idx = wl_state["idx"]
            gidx = idx % NGRP_W
            par = idx % 2
            sem = "WL0" if par == 0 else "WL1"
            wssem = "WS0" if par == 0 else "WS1"
            waits = []
            if idx >= 2:
                waits.append(("WF", idx - 1))
                if idx - 2 < NGRP_W:
                    waits.append((wssem, 16 * (idx // 2)))
            if idx < NGRP_W:
                op("gpsimd", lambda e: e.dma_start(
                    out=wbuf[par].ap().rearrange("p (k j) -> p k j", k=nk)[:, :, :sub],
                    in_=src_ap.rearrange("k p j -> p k j")),
                   waits=waits, incs=[(sem, 16)], drain=False)
                op("sync", lambda e: e.dma_start(
                    out=wsc[gidx], in_=wbuf[par][:, :]),
                   waits=[(sem, 16 * (idx // 2 + 1))], incs=[(wssem, 16)])
            else:
                gs = "WS0" if gidx % 2 == 0 else "WS1"
                w2 = (gs, 16 * (gidx // 2 + 1))
                if w2 not in waits and not any(s == gs and v >= w2[1] for s, v in waits):
                    waits.append(w2)
                op("gpsimd", lambda e: e.dma_start(
                    out=wbuf[par][:, :], in_=wsc[gidx]),
                   waits=waits, incs=[(sem, 16)], drain=False)
            wl_state["idx"] += 1
            return par, (sem, 16 * (idx // 2 + 1))

        def ffn(ve_split, in1_tile, out_tile, st_out_guard, xsp, inject=None):
            inj = inject or {}

            def fire(k):
                for fn in inj.get(k, ()):
                    fn()
            for og in range(8):
                par, wlv = wload(w12t[og], NKD, 1024)
                for ocp in range(4):
                    oc = og * 4 + ocp
                    bA, wA = get_bank()
                    if ocp == 0:
                        wA += [wlv, ("VE", ve_split)]
                    for kc in range(NKD):
                        for p in range(2):
                            last = (kc == NKD - 1 and p == 1)
                            op("tensor", lambda e, kc=kc, p=p, b=bA, ocp=ocp, par=par: e.matmul(
                                pb[b][:, :TT],
                                wbuf[par][:, kc * 1024 + ocp * 128: kc * 1024 + ocp * 128 + 128],
                                xsp[p][:, kc * TT:(kc + 1) * TT],
                                start=(kc == 0 and p == 0), stop=last),
                               waits=(wA if (kc == 0 and p == 0) else ()),
                               incs=([("PE", 1)] if last else ()))
                    peA = C["PE"]
                    bB, wB = get_bank()
                    for kc in range(NKD):
                        for p in range(2):
                            last = (kc == NKD - 1 and p == 1)
                            op("tensor", lambda e, kc=kc, p=p, b=bB, ocp=ocp, par=par: e.matmul(
                                pb[b][:, :TT],
                                wbuf[par][:, kc * 1024 + 512 + ocp * 128: kc * 1024 + 512 + ocp * 128 + 128],
                                xsp[p][:, kc * TT:(kc + 1) * TT],
                                start=(kc == 0 and p == 0), stop=last),
                               waits=(wB if (kc == 0 and p == 0) else ()),
                               incs=([("PE", 1)] if last else ()))
                    peB = C["PE"]
                    op("scalar", lambda e, b=bA, oc=oc: e.activation(
                        silu_t[:, :], pb[b][:, :TT], AF.Silu, scale=s_w1[:, oc:oc + 1]),
                       waits=[("PE", peA), ("VE", S["ve_silu"])], incs=[("AC", 1)])
                    rel_bank(bA, "AC")
                    ac_silu = C["AC"]
                    ht = hf_t[oc % 2]
                    op("vector", lambda e, b=bB, oc=oc, ht=ht: e.scalar_tensor_tensor(
                        ht[:, :], pb[b][:, :TT], s_w2[:, oc:oc + 1], silu_t[:, :],
                        ALU.mult, ALU.mult),
                       waits=[("PE", peB), ("AC", ac_silu)], incs=[("VE", 1)])
                    S["ve_silu"] = C["VE"]
                    rel_bank(bB, "VE")
                    if ocp == 3:
                        op("vector", lambda e: e.engine_nop(), incs=[("WF", 1)], drain=False)
                    cs = slice(oc * TT, (oc + 1) * TT)
                    op("vector", lambda e, cs=cs, ht=ht: e.tensor_copy(hspA[0][:, cs], ht[:, :]),
                       incs=[("VE", 1)])
                    op("vector", lambda e, cs=cs, ht=ht: e.tensor_sub(hspA[1][:, cs], ht[:, :], hspA[0][:, cs]),
                       incs=[("VE", 1)])
                    S["ve_hf"][oc % 2] = C["VE"]
            ve_hA = C["VE"]
            pe_stageA = C["PE"]
            fire("A_end")

            def hidden_stage(wt_ap, s_t, src_sp, dst_sp, ve_src, mid, end):
                for og in range(16):
                    if og == 8:
                        fire(mid)
                    par, wlv = wload(wt_ap[og], NOH, 256)
                    for ocj in range(2):
                        oc = og * 2 + ocj
                        bk, wk_ = get_bank()
                        if ocj == 0:
                            wk_ += [wlv, ("VE", ve_src)]
                        for kc in range(NOH):
                            for p in range(2):
                                last = (kc == NOH - 1 and p == 1)
                                op("tensor", lambda e, kc=kc, p=p, b=bk, ocj=ocj, par=par, sp=src_sp: e.matmul(
                                    pb[b][:, :TT],
                                    wbuf[par][:, kc * 256 + ocj * 128: kc * 256 + ocj * 128 + 128],
                                    sp[p][:, kc * TT:(kc + 1) * TT],
                                    start=(kc == 0 and p == 0), stop=last),
                                   waits=(wk_ if (kc == 0 and p == 0) else ()),
                                   incs=([("PE", 1)] if last else ()))
                        peg = C["PE"]
                        ht = hf_t[oc % 2]
                        op("scalar", lambda e, b=bk, oc=oc, ht=ht, s_t=s_t: e.activation(
                            ht[:, :], pb[b][:, :TT], AF.Gelu, scale=s_t[:, oc:oc + 1]),
                           waits=[("PE", peg), ("VE", S["ve_hf"][oc % 2])], incs=[("AC", 1)])
                        rel_bank(bk, "AC")
                        ac_g = C["AC"]
                        cs = slice(oc * TT, (oc + 1) * TT)
                        op("vector", lambda e, cs=cs, ht=ht, sp=dst_sp: e.tensor_copy(sp[0][:, cs], ht[:, :]),
                           waits=[("AC", ac_g)], incs=[("VE", 1)])
                        op("vector", lambda e, cs=cs, ht=ht, sp=dst_sp: e.tensor_sub(sp[1][:, cs], ht[:, :], sp[0][:, cs]),
                           incs=[("VE", 1)])
                        S["ve_hf"][oc % 2] = C["VE"]
                        if ocj == 1:
                            op("vector", lambda e: e.engine_nop(), incs=[("WF", 1)], drain=False)
                fire(end)
                return C["VE"]

            ve_hB = hidden_stage(h1t, s_h1, hspA, hspB, ve_hA, "B_mid", "B_end")
            ve_hC = hidden_stage(h2t, s_h2, hspB, hspA, ve_hB, "C_mid", "C_end")
            ffn.pe_stageA = pe_stageA

            for og in range(4):
                par, wlv = wload(w3t[og], NOH, 256)
                for ocj in range(2):
                    oc = og * 2 + ocj
                    bk, wk_ = get_bank()
                    if ocj == 0:
                        wk_ += [wlv, ("VE", ve_hC)]
                    for kc in range(NOH):
                        for p in range(2):
                            last = (kc == NOH - 1 and p == 1)
                            op("tensor", lambda e, kc=kc, p=p, b=bk, ocj=ocj, par=par: e.matmul(
                                pb[b][:, :TT],
                                wbuf[par][:, kc * 256 + ocj * 128: kc * 256 + ocj * 128 + 128],
                                hspA[p][:, kc * TT:(kc + 1) * TT],
                                start=(kc == 0 and p == 0), stop=last),
                               waits=(wk_ if (kc == 0 and p == 0) else ()),
                               incs=([("PE", 1)] if last else ()))
                    peg = C["PE"]
                    cs = slice(oc * TT, (oc + 1) * TT)
                    w = [("PE", peg)]
                    if oc == 0:
                        w.append(st_out_guard())
                    op("vector", lambda e, b=bk, oc=oc, cs=cs: e.scalar_tensor_tensor(
                        out_tile[:, cs], pb[b][:, :TT], s_w3h[:, oc:oc + 1], in1_tile[:, cs],
                        ALU.mult, ALU.add),
                       waits=w, incs=[("VE", 1)])
                    rel_bank(bk, "VE")
                    if ocj == 1:
                        op("vector", lambda e: e.engine_nop(), incs=[("WF", 1)], drain=False)
            return C["VE"]

        stxa_full = [16 * max(1, NT // 2)]
        ve_split_t = {}
        pe_xsp_par = [0, 0]
        ve_x_par = [0, 0]
        pend1 = {}
        rb1 = {}

        def p1_load_stats(t):
            par = t % 2
            cols = slice(t * TT, (t + 1) * TT)
            xt = x_tile2[par]
            sem = "LDX0" if par == 0 else "LDX1"
            op("sync", lambda e, xt=xt, cols=cols: e.dma_start(
                out=xt.ap().rearrange("p (k c) -> p k c", k=NKD),
                in_=x0T[:, cols].rearrange("(k p) c -> p k c", p=128)),
               waits=[("VE", ve_x_par[par])], incs=[(sem, 16)])
            rb1[t] = ln_stats(xt, [(sem, C[sem])])

        def p1_splits(t):
            par = t % 2
            ve_split_t[t] = ln_split(x_tile2[par], rb1.pop(t), xsp2[par], pe_xsp_par[par])

        def p1_lnb(t):
            r_tile = r2b2[0]
            ve_r = pend1[t][0]
            ve_rb2 = ln_stats(r_tile, [("VE", ve_r)])
            for kc in range(NKD):
                cs = slice(kc * TT, (kc + 1) * TT)
                w = []
                if kc == 0:
                    w = [("VE", ve_rb2), S["st_xlnb"]]
                op("vector", lambda e, cs=cs, r_tile=r_tile: e.tensor_mul(z_tile[:, :], r_tile[:, cs], Rb[:, :]),
                   waits=w, incs=[("VE", 1)])
                op("vector", lambda e, cs=cs: e.tensor_sub(xlnb_tile[:, cs], z_tile[:, :], Ab[:, :]),
                   incs=[("VE", 1)])
            pend1[t] = (ve_r, C["VE"])

        def p1_stores(t):
            cols = slice(t * TT, (t + 1) * TT)
            ve_r, ve_xlnb = pend1.pop(t)
            op("sync", lambda e, cols=cols: e.dma_start(
                out=resid[:, cols].rearrange("(k p) c -> p k c", p=128),
                in_=r2b2[0].ap().rearrange("p (k c) -> p k c", k=NKD)),
               waits=[("VE", ve_r)], incs=[("STR", 16)])
            S["st_r"] = C["STR"]
            half = (t * TT) // HC
            colh = (t * TT) % HC
            sxs = "STXA" if half == 0 else "STXB"
            op("sync", lambda e, half=half, colh=colh: e.dma_start(
                out=xg_in[half][:, colh:colh + TT].rearrange("(k p) c -> p k c", p=128),
                in_=xlnb_tile.ap().rearrange("p (k c) -> p k c", k=NKD)),
               waits=[("VE", ve_xlnb)], incs=[(sxs, 16)])
            S["st_xlnb"] = (sxs, C[sxs])

        p1_load_stats(0)
        p1_splits(0)
        for t in range(NT):
            par = t % 2
            inj = {}
            if t + 1 < NT:
                inj["A_end"] = [lambda tt=t + 1: p1_load_stats(tt)]
                inj["B_end"] = [lambda tt=t + 1: p1_splits(tt)]
            if t - 1 in pend1:
                inj.setdefault("A_end", []).append(lambda tt=t - 1: p1_stores(tt))
            pend1[t] = (0, 0)
            ve_r = ffn(ve_split_t.pop(t), x_tile2[par], r2b2[0],
                       lambda: ("STR", S["st_r"]), xsp2[par], inject=inj)
            pe_xsp_par[par] = ffn.pe_stageA
            ve_x_par[par] = ve_r
            pend1[t] = (ve_r, None)
            p1_lnb(t)
        p1_stores(NT - 1)
        op("gpsimd", lambda e: e.collective_compute(
            "AllGather", ALU.bypass, replica_groups=REPL_GROUPS,
            ins=[xg_in[0]], outs=[xg_out[0]]),
           waits=[("STXA", stxa_full[0])], incs=[("CC", 1)], drain=False)
        op("gpsimd", lambda e: e.collective_compute(
            "AllGather", ALU.bypass, replica_groups=REPL_GROUPS,
            ins=[xg_in[1]], outs=[xg_out[1]]),
           waits=[("STXB", C["STXB"])], incs=[("CC", 1)], drain=False)
        cc1 = C["CC"]

        f_pe, f_ac, f_ve = C["PE"], C["AC"], C["VE"]
        aw = [("PE", f_pe), ("AC", f_ac), ("VE", f_ve)]
        for dst, src in [(wq_sb, wq_i), (wk_sb, wk_i), (wv_sb, wv_i)]:
            op("sync", lambda e, d=dst, s=src: e.dma_start(
                out=d.ap().rearrange("p (k j) -> p k j", k=NKD),
                in_=s.rearrange("(k p) j -> p k j", p=128)),
               waits=aw, incs=[("LDA", 16)])
            aw = []
        op("sync", lambda e: e.dma_start(
            out=wb_sb.ap().rearrange("p (k j) -> p k j", j=1),
            in_=wb_i.rearrange("(k p) j -> p k j", p=128),
            allow_slow_non_contiguous=True),
           incs=[("LDA", 16)])
        ld_attnw = C["LDA"]
        op("vector", lambda e: e.memset(S_sb[:, :], 0.0),
           waits=[("PE", f_pe), ("AC", f_ac)], incs=[("VE", 1)])
        op("vector", lambda e: e.memset(utail[:, :], 0.0), incs=[("VE", 1)])
        ve_supd = C["VE"]

        NPIECE = max(1, GT // 512)
        PW = GT // NPIECE
        for g in range(NG):
            gt0 = g * GT
            blk = gt0 // TCORE
            col0 = gt0 % TCORE
            hb = col0 // HC
            colh = col0 % HC
            cc_need = 1 if hb == 0 else 2
            ld_xg = []
            for kc in range(NKD):
                w = [("PE", S["pe_xg"]), ("CC", cc_need)] if kc == 0 else []
                op("sync", lambda e, kc=kc, hb=hb, colh=colh: e.dma_start(
                    out=xg_sb[:, kc * GT:(kc + 1) * GT],
                    in_=xg_out[hb][blk * D + kc * 128: blk * D + kc * 128 + 128,
                                   colh:colh + GT]),
                   waits=w, incs=[("LDG", 16)])
                ld_xg.append(C["LDG"])

            for j in range(GCH):
                bk, wk_ = get_bank()
                for kc in range(NKD):
                    w = (wk_ + [("LDG", ld_xg[-1]), ("LDA", ld_attnw)]) if kc == 0 else []
                    op("tensor", lambda e, kc=kc, j=j, b=bk: e.matmul(
                        pb[b][:, 0:1],
                        xg_sb[:, kc * GT + j * CH: kc * GT + j * CH + 128],
                        wb_sb[:, kc:kc + 1],
                        start=(kc == 0), stop=(kc == NKD - 1)),
                       waits=w, incs=([("PE", 1)] if kc == NKD - 1 else ()))
                peb = C["PE"]
                op("scalar", lambda e, b=bk: e.activation(
                    bh_sb[:, :], pb[b][:, 0:1], AF.Tanh, scale=0.5),
                   waits=[("PE", peb), ("VE", S.get("ve_bh", 0))], incs=[("AC", 1)])
                rel_bank(bk, "AC")
                acb = C["AC"]
                op("vector", lambda e, j=j: e.tensor_scalar(
                    beta_sb[:, j:j + 1], bh_sb[:, :], 0.5, 0.5, ALU.mult, ALU.add),
                   waits=[("AC", acb)], incs=[("VE", 1)])
                S["ve_bh"] = C["VE"]
            ve_beta = C["VE"]

            for wi, (wsb, csb, dst) in enumerate([(wq_sb, c_q, qT_sb),
                                                  (wk_sb, c_k, kT_sb),
                                                  (wv_sb, c_v, vT_sb)]):
                for c2 in range(2):
                    op("vector", lambda e, c2=c2, wi=wi: e.tensor_copy(
                        u_sb[:, c2 * (GT + 3): c2 * (GT + 3) + 3],
                        utail[:, (wi * 2 + c2) * 3: (wi * 2 + c2) * 3 + 3]),
                       incs=[("VE", 1)])
                for c2 in range(2):
                    for piece in range(NPIECE):
                        bk, wk_ = get_bank()
                        w = wk_ + [("LDG", ld_xg[-1]), ("LDA", ld_attnw)]
                        for kc in range(NKD):
                            op("tensor", lambda e, kc=kc, c2=c2, piece=piece, b=bk, wsb=wsb: e.matmul(
                                pb[b][:, :PW],
                                wsb[:, kc * 256 + c2 * 128: kc * 256 + c2 * 128 + 128],
                                xg_sb[:, kc * GT + piece * PW: kc * GT + piece * PW + PW],
                                start=(kc == 0), stop=(kc == NKD - 1)),
                               waits=(w if kc == 0 else ()),
                               incs=([("PE", 1)] if kc == NKD - 1 else ()))
                        pep = C["PE"]
                        op("vector", lambda e, c2=c2, piece=piece, b=bk: e.tensor_copy(
                            u_sb[:, c2 * (GT + 3) + 3 + piece * PW: c2 * (GT + 3) + 3 + piece * PW + PW],
                            pb[b][:, :PW]),
                           waits=[("PE", pep), ("AC", S.get("ac_u", 0))], incs=[("VE", 1)])
                        rel_bank(bk, "VE")
                for c2 in range(2):
                    op("vector", lambda e, c2=c2, wi=wi: e.tensor_copy(
                        utail[:, (wi * 2 + c2) * 3: (wi * 2 + c2) * 3 + 3],
                        u_sb[:, c2 * (GT + 3) + GT: c2 * (GT + 3) + GT + 3]),
                       incs=[("VE", 1)])
                for c2 in range(2):
                    ub = c2 * (GT + 3)
                    dcol = c2 * GT
                    w = [("PE", S["pe_scan_q"])] if (c2 == 0) else []
                    op("vector", lambda e, ub=ub, dcol=dcol, csb=csb, dst=dst, c2=c2: e.tensor_scalar_mul(
                        dst[:, dcol:dcol + GT], u_sb[:, ub:ub + GT], csb[:, c2 * 4:c2 * 4 + 1]),
                       waits=w, incs=[("VE", 1)])
                    for tap in range(1, 4):
                        op("vector", lambda e, ub=ub, dcol=dcol, csb=csb, dst=dst, c2=c2, tap=tap: e.scalar_tensor_tensor(
                            dst[:, dcol:dcol + GT], u_sb[:, ub + tap:ub + tap + GT],
                            csb[:, c2 * 4 + tap:c2 * 4 + tap + 1], dst[:, dcol:dcol + GT],
                            ALU.mult, ALU.add),
                           incs=[("VE", 1)])
                ve_conv = C["VE"]
                for c2 in range(2):
                    op("scalar", lambda e, c2=c2, dst=dst: e.activation(
                        dst[:, c2 * GT:(c2 + 1) * GT], dst[:, c2 * GT:(c2 + 1) * GT], AF.Silu),
                       waits=([("VE", ve_conv)] if c2 == 0 else ()), incs=[("AC", 1)])
            ac_silu_all = C["AC"]

            for wi, dst in [(0, qT_sb), (1, kT_sb)]:
                for c2 in range(2):
                    w = [("PE", S["pe_l2"])]
                    if wi == 0 and c2 == 0:
                        w.append(("AC", ac_silu_all))
                    op("scalar", lambda e, c2=c2, dst=dst: e.activation(
                        u_sb[:, c2 * (GT + 3) + 3: c2 * (GT + 3) + 3 + GT],
                        dst[:, c2 * GT:(c2 + 1) * GT], AF.Square),
                       waits=w, incs=[("AC", 1)])
                ac_sq2 = C["AC"]
                S["ac_u"] = ac_sq2
                for piece in range(NPIECE):
                    bk, wk_ = get_bank([("AC", ac_sq2)])
                    for c2 in range(2):
                        op("tensor", lambda e, c2=c2, piece=piece, b=bk: e.matmul(
                            pb[b][0:1, :PW], ones_col[:, :],
                            u_sb[:, c2 * (GT + 3) + 3 + piece * PW: c2 * (GT + 3) + 3 + piece * PW + PW],
                            start=(c2 == 0), stop=(c2 == 1)),
                           waits=(wk_ if c2 == 0 else ()),
                           incs=([("PE", 1)] if c2 == 1 else ()))
                    pel = C["PE"]
                    S["pe_l2"] = pel
                    op("scalar", lambda e, b=bk: e.activation(
                        row_l2[0:1, :PW], pb[b][0:1, :PW], AF.Sqrt, bias=EPS_L2),
                       waits=[("PE", pel), ("VE", S["ve_wo_rd"])], incs=[("AC", 1)])
                    rel_bank(bk, "AC")
                    acl = C["AC"]
                    op("vector", lambda e: e.reciprocal(row_l2b[0:1, :PW], row_l2[0:1, :PW]),
                       waits=[("AC", acl)], incs=[("VE", 1)])
                    vel = C["VE"]
                    S["ve_wo_rd"] = vel
                    b2, w2_ = get_bank([("VE", vel)])
                    op("tensor", lambda e, b=b2: e.matmul(
                        pb[b][:, :PW], ones_row[:, :], row_l2b[0:1, :PW],
                        start=True, stop=True),
                       waits=w2_, incs=[("PE", 1)])
                    pebc = C["PE"]
                    for c2 in range(2):
                        op("vector", lambda e, c2=c2, piece=piece, b=b2, dst=dst: e.tensor_mul(
                            dst[:, c2 * GT + piece * PW: c2 * GT + piece * PW + PW],
                            dst[:, c2 * GT + piece * PW: c2 * GT + piece * PW + PW],
                            pb[b][:, :PW]),
                           waits=([("PE", pebc)] if c2 == 0 else ()), incs=[("VE", 1)])
                    rel_bank(b2, "VE")
            ve_qk = C["VE"]

            SKEW = 2

            def indep(j):
                slot = j % NSLOT
                jb = j * CH
                bcol = beta_sb[:, j:j + 1]

                def qs(c2):
                    return qT_sb[:, c2 * GT + jb: c2 * GT + jb + 128]

                def ks(c2):
                    return kT_sb[:, c2 * GT + jb: c2 * GT + jb + 128]

                def vs(c2):
                    return vT_sb[:, c2 * GT + jb: c2 * GT + jb + 128]

                slot_guard = [("PE", S["tail_pe"][slot])]
                for c2 in range(2):
                    bk, wk_ = get_bank([("VE", ve_qk), ("AC", ac_silu_all)])
                    op("tensor", lambda e, b=bk, c2=c2: e.transpose(
                        pb[b][:, :128], vs(c2), ident[:, :]),
                       waits=wk_, incs=[("PE", 1)])
                    pet = C["PE"]
                    op("vector", lambda e, b=bk, c2=c2, slot=slot: e.tensor_scalar_mul(
                        Bv_a[slot][:, c2 * 128:(c2 + 1) * 128], pb[b][:, :128], bcol),
                       waits=[("PE", pet), ("VE", ve_beta)] + (slot_guard if c2 == 0 else []),
                       incs=[("VE", 1)])
                    rel_bank(bk, "VE")
                for c2 in range(2):
                    bk, wk_ = get_bank()
                    op("tensor", lambda e, b=bk, c2=c2: e.transpose(
                        pb[b][:, :128], ks(c2), ident[:, :]),
                       waits=wk_, incs=[("PE", 1)])
                    pet = C["PE"]
                    op("vector", lambda e, b=bk, c2=c2, slot=slot: e.tensor_copy(
                        Ktm_a[slot][:, c2 * 128:(c2 + 1) * 128], pb[b][:, :128]),
                       waits=[("PE", pet)], incs=[("VE", 1)])
                    rel_bank(bk, "VE")
                bk, wk_ = get_bank()
                for c2 in range(2):
                    op("tensor", lambda e, b=bk, c2=c2: e.matmul(
                        pb[b][:, :128], ks(c2), ks(c2), start=(c2 == 0), stop=(c2 == 1)),
                       waits=(wk_ if c2 == 0 else ()),
                       incs=([("PE", 1)] if c2 == 1 else ()))
                pea = C["PE"]
                op("vector", lambda e, b=bk: e.scalar_tensor_tensor(
                    P_pp[0][:, :], pb[b][:, :128], bcol, m_trilneg[:, :], ALU.mult, ALU.mult),
                   waits=[("PE", pea), ("PE", S["pe_chain_prev"])], incs=[("VE", 1)])
                rel_bank(bk, "VE")
                ve_P = C["VE"]
                bk, wk_ = get_bank([("VE", ve_P)])
                op("tensor", lambda e, b=bk: e.transpose(pb[b][:, :128], P_pp[0][:, :], ident[:, :]),
                   waits=wk_, incs=[("PE", 1)])
                pet0 = C["PE"]
                op("vector", lambda e, b=bk: e.tensor_copy(T_pp[0][:, :], pb[b][:, :128]),
                   waits=[("PE", pet0)], incs=[("VE", 1)])
                rel_bank(bk, "VE")
                ve_T = C["VE"]
                op("vector", lambda e, slot=slot: e.tensor_add(
                    X_a[slot][:, :], T_pp[0][:, :], ident[:, :]), incs=[("VE", 1)])
                ve_X = C["VE"]
                for l in range(1, 7):
                    pprev, pcur = P_pp[(l - 1) % 2], P_pp[l % 2]
                    tprev, tcur = T_pp[(l - 1) % 2], T_pp[l % 2]
                    bk, wk_ = get_bank([("VE", ve_T), ("VE", ve_P)])
                    op("tensor", lambda e, b=bk, tp=tprev, pp=pprev: e.matmul(
                        pb[b][:, :128], tp[:, :], pp[:, :], start=True, stop=True),
                       waits=wk_, incs=[("PE", 1)])
                    pe1 = C["PE"]
                    bk2, wk2_ = get_bank()
                    op("tensor", lambda e, b=bk2, tp=tprev, pp=pprev: e.matmul(
                        pb[b][:, :128], pp[:, :], tp[:, :], start=True, stop=True),
                       waits=wk2_, incs=[("PE", 1)])
                    pe2 = C["PE"]
                    op("vector", lambda e, b=bk, pc=pcur: e.tensor_copy(pc[:, :], pb[b][:, :128]),
                       waits=[("PE", pe1)], incs=[("VE", 1)])
                    rel_bank(bk, "VE")
                    ve_P = C["VE"]
                    op("vector", lambda e, b=bk2, tc=tcur: e.tensor_copy(tc[:, :], pb[b][:, :128]),
                       waits=[("PE", pe2)], incs=[("VE", 1)])
                    rel_bank(bk2, "VE")
                    ve_T = C["VE"]
                    bk3, wk3_ = get_bank([("VE", ve_P), ("VE", ve_X)])
                    op("tensor", lambda e, b=bk3, pc=pcur, slot=slot: e.matmul(
                        pb[b][:, :128], pc[:, :], X_a[slot][:, :], start=True, stop=True),
                       waits=wk3_, incs=[("PE", 1)])
                    pex = C["PE"]
                    op("vector", lambda e, b=bk3, slot=slot: e.tensor_add(
                        X_a[slot][:, :], X_a[slot][:, :], pb[b][:, :128]),
                       waits=[("PE", pex)], incs=[("VE", 1)])
                    rel_bank(bk3, "VE")
                    ve_X = C["VE"]
                S["pe_chain_prev"] = C["PE"]
                S["ve_mt"][slot] = ve_X
                op("vector", lambda e, slot=slot: e.tensor_scalar(
                    Bkn_t[:, :], Ktm_a[slot][:, :], bcol, -1.0, ALU.mult, ALU.mult),
                   waits=[("PE", S["pe_bkn_prev"])], incs=[("VE", 1)])
                ve_bkn = C["VE"]
                for c2 in range(2):
                    bk, wk_ = get_bank([("VE", ve_bkn), ("VE", ve_X)])
                    op("tensor", lambda e, b=bk, c2=c2, slot=slot: e.matmul(
                        pb[b][:, :128], Bkn_t[:, c2 * 128:(c2 + 1) * 128], X_a[slot][:, :],
                        start=True, stop=True),
                       waits=wk_, incs=[("PE", 1)])
                    pew = C["PE"]
                    if c2 == 1:
                        S["pe_bkn_prev"] = pew
                    op("vector", lambda e, b=bk, c2=c2, slot=slot: e.tensor_copy(
                        W1Tn_a[slot][:, c2 * 128:(c2 + 1) * 128], pb[b][:, :128]),
                       waits=[("PE", pew)], incs=[("VE", 1)])
                    rel_bank(bk, "VE")
                bk, wk_ = get_bank()
                for c2 in range(2):
                    op("tensor", lambda e, b=bk, c2=c2: e.matmul(
                        pb[b][:, :128], ks(c2), qs(c2), start=(c2 == 0), stop=(c2 == 1)),
                       waits=(wk_ if c2 == 0 else ()),
                       incs=([("PE", 1)] if c2 == 1 else ()))
                pelt = C["PE"]
                op("vector", lambda e, b=bk, slot=slot: e.tensor_mul(
                    LT_a[slot][:, :], pb[b][:, :128], m_triu[:, :]),
                   waits=[("PE", pelt)], incs=[("VE", 1)])
                rel_bank(bk, "VE")
                S["ve_indep"][slot] = C["VE"]

            def tail(j):
                slot = j % NSLOT
                jb = j * CH
                nonlocal ve_supd

                def qs(c2):
                    return qT_sb[:, c2 * GT + jb: c2 * GT + jb + 128]

                bk, wk_ = get_bank([("VE", S["ve_indep"][slot]), ("VE", ve_supd),
                                    ("PE", S["pe_u_prev"])])
                op("tensor", lambda e, b=bk, slot=slot: e.matmul(
                    pb[b][:, :256], X_a[slot][:, :], Bv_a[slot][:, :],
                    start=True, stop=False),
                   waits=wk_)
                for c2 in range(2):
                    op("tensor", lambda e, b=bk, c2=c2, slot=slot: e.matmul(
                        pb[b][:, :256], W1Tn_a[slot][:, c2 * 128:(c2 + 1) * 128],
                        S_sb[:, c2 * 256:(c2 + 1) * 256],
                        start=False, stop=(c2 == 1)),
                       incs=([("PE", 1)] if c2 == 1 else ()))
                peu = C["PE"]
                op("vector", lambda e, b=bk: e.tensor_copy(U_sb[:, :], pb[b][:, :256]),
                   waits=[("PE", peu)], incs=[("VE", 1)])
                rel_bank(bk, "VE")
                ve_U = C["VE"]
                bo, wo_ = get_bank()
                for c2 in range(2):
                    op("tensor", lambda e, b=bo, c2=c2: e.matmul(
                        pb[b][:, :256], qs(c2), S_sb[:, c2 * 256:(c2 + 1) * 256],
                        start=(c2 == 0), stop=False),
                       waits=(wo_ if c2 == 0 else ()))
                op("tensor", lambda e, b=bo, slot=slot: e.matmul(
                    pb[b][:, :256], LT_a[slot][:, :], U_sb[:, :], start=False, stop=True),
                   waits=[("VE", ve_U)], incs=[("PE", 1)])
                peo = C["PE"]
                for c2 in range(2):
                    bk, wk_ = get_bank()
                    op("tensor", lambda e, b=bk, c2=c2, slot=slot: e.matmul(
                        pb[b][:, :256], Ktm_a[slot][:, c2 * 128:(c2 + 1) * 128], U_sb[:, :],
                        start=True, stop=True),
                       waits=wk_, incs=[("PE", 1)])
                    pes = C["PE"]
                    if c2 == 1:
                        S["pe_u_prev"] = pes
                        S["tail_pe"][slot] = pes
                    op("vector", lambda e, b=bk, c2=c2: e.tensor_add(
                        S_sb[:, c2 * 256:(c2 + 1) * 256], S_sb[:, c2 * 256:(c2 + 1) * 256],
                        pb[b][:, :256]),
                       waits=[("PE", pes)], incs=[("VE", 1)])
                    rel_bank(bk, "VE")
                ve_supd = C["VE"]
                op("vector", lambda e, b=bo: e.tensor_copy(o_sb[:, :], pb[b][:, :256]),
                   waits=[("PE", peo), ("AC", S["ac_osq"])], incs=[("VE", 1)])
                rel_bank(bo, "VE")
                ve_o = C["VE"]
                op("scalar", lambda e: e.activation(osq_sb[:, :], o_sb[:, :], AF.Square,
                                                    accum_out=ossq[:, :]),
                   waits=[("VE", ve_o), ("VE", S["ve_orr"])], incs=[("AC", 1)])
                op("scalar", lambda e: e.activation(orr[:, :], ossq[:, :], AF.Sqrt,
                                                    scale=1.0 / DH, bias=EPS_RMS),
                   incs=[("AC", 1)])
                ac_or = C["AC"]
                S["ac_osq"] = ac_or
                op("vector", lambda e: e.reciprocal(orinv[:, :], orr[:, :]),
                   waits=[("AC", ac_or)], incs=[("VE", 1)])
                S["ve_orr"] = C["VE"]
                op("vector", lambda e: e.tensor_scalar_mul(on_sb[:, :], o_sb[:, :], orinv[:, :]),
                   incs=[("VE", 1)])
                ve_on = C["VE"]
                for c2 in range(2):
                    bk, wk_ = get_bank([("VE", ve_on)])
                    op("tensor", lambda e, b=bk, c2=c2: e.transpose(
                        pb[b][:, :128], on_sb[:, c2 * 128:(c2 + 1) * 128], ident[:, :]),
                       waits=wk_, incs=[("PE", 1)])
                    peot = C["PE"]
                    w = [("PE", peot)]
                    if c2 == 0:
                        w.append(S["st_og"])
                    op("vector", lambda e, b=bk, c2=c2: e.tensor_scalar_mul(
                        oT_sb[:, c2 * 128:(c2 + 1) * 128], pb[b][:, :128], g_rms[:, c2:c2 + 1]),
                       waits=w, incs=[("VE", 1)])
                    rel_bank(bk, "VE")
                ve_ot = C["VE"]
                sc = gt0 + jb
                oh = sc // HS
                sco = sc % HS
                sos = "STOA" if oh == 0 else "STOB"
                op("sync", lambda e, oh=oh, sco=sco: e.dma_start(
                    out=og_in[oh][:, sco: sco + 128].rearrange("(c p) t -> p c t", p=128),
                    in_=oT_sb.ap().rearrange("p (c t) -> p c t", c=2)),
                   waits=[("VE", ve_ot)], incs=[(sos, 16)])
                S["st_og"] = (sos, C[sos])

            for j in range(GCH + SKEW):
                if j < GCH:
                    indep(j)
                if j >= SKEW:
                    tail(j - SKEW)
            S["pe_scan_q"] = C["PE"]
            S["pe_xg"] = C["PE"]
            if g == max(0, NG // 2 - 1):
                op("gpsimd", lambda e: e.collective_compute(
                    "AllGather", ALU.bypass, replica_groups=REPL_GROUPS,
                    ins=[og_in[0]], outs=[og_out[0]]),
                   waits=[("STOA", C["STOA"])], incs=[("CC", 1)], drain=False)

        op("gpsimd", lambda e: e.collective_compute(
            "AllGather", ALU.bypass, replica_groups=REPL_GROUPS,
            ins=[og_in[1]], outs=[og_out[1]]),
           waits=[("STOB", C["STOB"])], incs=[("CC", 1)], drain=False)
        cc2 = C["CC"]

        f3_pe, f3_ac = C["PE"], C["AC"]
        if me == "sync":
            pid = nc.partition_id(engines=[mybir.EngineType.SP])
            slice_reg = eng_handle.to_reg(pid % 4)

        first3 = [[("PE", f3_pe), ("AC", f3_ac)]]
        pe_y_kc = {}
        st3 = {"ve_ogyp": [0, 0], "ve_r2b": [0, 0], "ve_xln": 0, "ve_x3": 0,
               "pe_og": [0, 0]}
        ve_split3 = {}
        pend3 = {}
        pst3 = {}

        def q_loads_y(t):
            par = t % 2
            cols = slice(t * TT, (t + 1) * TT)
            ogt = og_yp2[par]
            C["LDO"] += 16
            if me == "sync":
                for s_, v_ in (first3[0] + [("PE", st3["pe_og"][par]), ("VE", st3["ve_ogyp"][par])]):
                    if v_ > 0:
                        eng_handle.wait_ge(sems[s_], v_)
                for sl in range(4):
                    with eng_handle.If_cmp(slice_reg, sl, "IS_EQ"):
                        c0 = sl * TCORE + t * TT
                        oh = c0 // HS
                        c0h = c0 % HS
                        eng_handle.wait_ge(sems["CC"], cc2 - 1 if oh == 0 else cc2)
                        eng_handle.dma_start(
                            out=ogt.ap().rearrange("p (k c) -> p k c", k=NKD),
                            in_=og_out[oh][:, c0h:c0h + TT].rearrange("(k p) c -> p k c", p=128),
                        ).then_inc(sems["LDO"], 16)
                eng_handle.end_ifs()
            first3[0] = []
            ld_og = C["LDO"]
            halfL = (t * TT) // HC
            colL = (t * TT) % HC
            op("sync", lambda e, halfL=halfL, colL=colL: e.dma_start(
                out=xlnb_tile.ap().rearrange("p (k c) -> p k c", k=NKD),
                in_=xg_in[halfL][:, colL:colL + TT].rearrange("(k p) c -> p k c", p=128)),
               waits=[("VE", st3["ve_xln"])], incs=[("LDL", 16)])
            ld_xln = C["LDL"]
            op("sync", lambda e, cols=cols: e.dma_start(
                out=x_tile2[0].ap().rearrange("p (k c) -> p k c", k=NKD),
                in_=resid[:, cols].rearrange("(k p) c -> p k c", p=128)),
               waits=[("VE", st3["ve_x3"]), ("STR", S["st_r"])], incs=[("LDR", 16)])
            ld_r1 = C["LDR"]
            ld_wo = {}

            def load_wo(kc):
                sem = "LDW0" if kc % 2 == 0 else "LDW1"
                key = (kc - 2) % NKD if pe_y_kc else None
                w = [("PE", pe_y_kc.get(key, 0))] if key is not None else []
                op("sync", lambda e, kc=kc: e.dma_start(
                    out=wo_buf[kc % 2][:, :], in_=wo_i[kc * 128:(kc + 1) * 128, :]),
                   waits=w, incs=[(sem, 16)])
                ld_wo[kc] = (sem, C[sem])

            load_wo(0)
            load_wo(1)
            ybanks = []
            for oc in range(NKD):
                bk, wk_ = get_bank()
                ybanks.append((bk, wk_))
            for kc in range(NKD):
                for oc in range(NKD):
                    bk, wk_ = ybanks[oc]
                    w = []
                    if kc == 0:
                        w = list(wk_) + [("LDO", ld_og)]
                    if oc == 0:
                        w = w + [ld_wo[kc]]
                    op("tensor", lambda e, kc=kc, oc=oc, b=bk, ogt=ogt: e.matmul(
                        pb[b][:, :TT], wo_buf[kc % 2][:, oc * 128:(oc + 1) * 128],
                        ogt[:, kc * TT:(kc + 1) * TT],
                        start=(kc == 0), stop=(kc == NKD - 1)),
                       waits=w,
                       incs=([("PE", 1)] if oc == NKD - 1 else ()))
                pe_y_kc[kc] = C["PE"]
                if kc + 2 < NKD:
                    load_wo(kc + 2)
            pe_y = C["PE"]
            st3["pe_og"][par] = pe_y
            for oc in range(NKD):
                bk, _ = ybanks[oc]
                cs = slice(oc * TT, (oc + 1) * TT)
                w = [("PE", pe_y)]
                if oc == 0:
                    w += [("LDL", ld_xln)]
                op("vector", lambda e, b=bk, cs=cs, ogt=ogt: e.tensor_add(
                    ogt[:, cs], pb[b][:, :TT], xlnb_tile[:, cs]),
                   waits=w, incs=[("VE", 1)])
                rel_bank(bk, "VE")
            pst3[t] = {"ve_yp": C["VE"], "ld_r1": ld_r1}
            st3["ve_xln"] = C["VE"]

        def q_stats1(t):
            par = t % 2
            pst3[t]["ve_rb3"] = ln_stats(og_yp2[par], [("VE", pst3[t]["ve_yp"])])

        def q_zchain(t):
            par = t % 2
            ogt = og_yp2[par]
            r2t = r2b2[par]
            for kc in range(NKD):
                cs = slice(kc * TT, (kc + 1) * TT)
                w = []
                if kc == 0:
                    w = [("VE", pst3[t]["ve_rb3"])]
                op("vector", lambda e, cs=cs, ogt=ogt: e.tensor_mul(z_tile[:, :], ogt[:, cs], Rb[:, :]),
                   waits=w, incs=[("VE", 1)])
                op("vector", lambda e: e.tensor_sub(z_tile[:, :], z_tile[:, :], Ab[:, :]),
                   incs=[("VE", 1)])
                ve_z = C["VE"]
                op("scalar", lambda e, kc=kc: e.activation(
                    e_tile[:, :], z_tile[:, :], AF.Identity,
                    bias=ln_b[:, kc:kc + 1], scale=ln_g[:, kc:kc + 1]),
                   waits=[("VE", ve_z)], incs=[("AC", 1)])
                ac_z = C["AC"]
                w = [("AC", ac_z)]
                if kc == 0:
                    w += [("LDR", pst3[t]["ld_r1"]), ("VE", st3["ve_r2b"][par])]
                op("vector", lambda e, cs=cs, r2t=r2t: e.tensor_add(
                    r2t[:, cs], e_tile[:, :], x_tile2[0][:, cs]),
                   waits=w, incs=[("VE", 1)])
            ve_r2 = C["VE"]
            pst3[t]["ve_r2"] = ve_r2
            st3["ve_ogyp"][par] = ve_r2
            st3["ve_x3"] = ve_r2

        def q_stats2(t):
            par = t % 2
            pst3[t]["ve_rb4"] = ln_stats(r2b2[par], [("VE", pst3[t]["ve_r2"])])

        def q_splits(t):
            par = t % 2
            ve_split3[t] = ln_split(r2b2[par], pst3[t]["ve_rb4"], xsp2[par], pe_xsp_par[par])
            pst3.pop(t)

        def p3_store(t):
            cols = slice(t * TT, (t + 1) * TT)
            ve_out = pend3.pop(t)
            op("sync", lambda e, cols=cols: e.dma_start(
                out=outT[:, cols].rearrange("(k p) c -> p k c", p=128),
                in_=out3.ap().rearrange("p (k c) -> p k c", k=NKD)),
               waits=[("VE", ve_out)], incs=[("STU", 16)])
            S["st_out"] = C["STU"]

        out3 = x_tile2[1]
        q_loads_y(0)
        q_stats1(0)
        q_zchain(0)
        q_stats2(0)
        q_splits(0)
        for t in range(NT):
            par = t % 2
            inj = {}
            if t + 1 < NT:
                inj["A_end"] = [lambda tt=t + 1: q_loads_y(tt)]
                inj["B_mid"] = [lambda tt=t + 1: q_stats1(tt)]
                inj["B_end"] = [lambda tt=t + 1: q_zchain(tt)]
                inj["C_mid"] = [lambda tt=t + 1: q_stats2(tt)]
                inj["C_end"] = [lambda tt=t + 1: q_splits(tt)]
            if t - 1 in pend3:
                inj.setdefault("A_end", []).append(lambda tt=t - 1: p3_store(tt))
            ve_out = ffn(ve_split3.pop(t), r2b2[par], out3,
                         lambda: ("STU", S["st_out"]), xsp2[par], inject=inj)
            pe_xsp_par[par] = ffn.pe_stageA
            st3["ve_r2b"][par] = ve_out
            pend3[t] = ve_out
        p3_store(NT - 1)

        waitonly("sync", [("STU", C["STU"]), ("STR", C["STR"]), ("STXA", C["STXA"]),
                          ("STXB", C["STXB"]), ("STOA", C["STOA"]), ("STOB", C["STOB"])])
        return C

    counts = {}
    with nc.Block() as block:
        def mk(name):
            def f(eng):
                counts[name] = emit(name, eng)
            return f
        block.sync(mk("sync"))
        block.gpsimd(mk("gpsimd"))
        block.tensor(mk("tensor"))
        block.vector(mk("vector"))
        block.scalar(mk("scalar"))

    base = counts["sync"]
    for k, c in counts.items():
        assert c == base, f"counter divergence in {k}"
    return nc


def _prep_shared(inp):
    f = np.float32
    w1T = np.ascontiguousarray(np.asarray(inp["L0_ffn_w1_w"], dtype=f).T)
    w2T = np.ascontiguousarray(np.asarray(inp["L0_ffn_w2_w"], dtype=f).T)
    h1T = np.ascontiguousarray(np.asarray(inp["L0_ffn_h1_w"], dtype=f).T)
    h2T = np.ascontiguousarray(np.asarray(inp["L0_ffn_h2_w"], dtype=f).T)
    w3T = np.ascontiguousarray(np.asarray(inp["L0_ffn_w3_w"], dtype=f).T)

    w12t = np.empty((8, NKD, 128, 1024), f)
    w12t[:, :, :, :512] = w1T.reshape(NKD, 128, 8, 512).transpose(2, 0, 1, 3)
    w12t[:, :, :, 512:] = w2T.reshape(NKD, 128, 8, 512).transpose(2, 0, 1, 3)
    h1t = np.ascontiguousarray(h1T.reshape(NOH, 128, 16, 256).transpose(2, 0, 1, 3))
    h2t = np.ascontiguousarray(h2T.reshape(NOH, 128, 16, 256).transpose(2, 0, 1, 3))
    w3t = np.ascontiguousarray(w3T.reshape(NOH, 128, 4, 256).transpose(2, 0, 1, 3))

    tile_s = lambda s, n: np.ascontiguousarray(np.asarray(s, dtype=f).reshape(n, 128).T)
    return {
        "w12t": np.ascontiguousarray(w12t),
        "h1t": h1t, "h2t": h2t, "w3t": w3t,
        "w1s": tile_s(inp["L0_ffn_w1_s"], NOH),
        "w2s": tile_s(inp["L0_ffn_w2_s"], NOH),
        "h1s": tile_s(inp["L0_ffn_h1_s"], NOH),
        "h2s": tile_s(inp["L0_ffn_h2_s"], NOH),
        "w3s": tile_s(inp["L0_ffn_w3_s"], NKD),
        "wo": np.ascontiguousarray(np.asarray(inp["L0_attn_Wo"], dtype=f)),
        "grms": tile_s(inp["L0_attn_g_rms"], 2),
        "lngt": tile_s(inp["L0_attn_ln_g"], NKD),
        "lnbt": tile_s(inp["L0_attn_ln_b"], NKD),
    }


def _prep_core(inp, c):
    f = np.float32
    b, s = c // 4, c % 4
    h = c % 4
    x = np.asarray(inp["L0_attn_x"], dtype=f)
    x0T = np.ascontiguousarray(x[b, s * TCORE:(s + 1) * TCORE, :].T)
    conv_tile = lambda a: np.ascontiguousarray(
        np.asarray(a, dtype=f)[:, h * DH:(h + 1) * DH].T.reshape(2, 128, 4)
        .transpose(1, 0, 2).reshape(128, 8))
    return {
        "x0T": x0T,
        "wq": np.ascontiguousarray(np.asarray(inp["L0_attn_Wq"], dtype=f)[:, h * DH:(h + 1) * DH]),
        "wk": np.ascontiguousarray(np.asarray(inp["L0_attn_Wk"], dtype=f)[:, h * DH:(h + 1) * DH]),
        "wv": np.ascontiguousarray(np.asarray(inp["L0_attn_Wv"], dtype=f)[:, h * DH:(h + 1) * DH]),
        "wb": np.ascontiguousarray(np.asarray(inp["L0_attn_Wb"], dtype=f)[:, h:h + 1]),
        "convq": conv_tile(inp["L0_attn_conv_q"]),
        "convk": conv_tile(inp["L0_attn_conv_k"]),
        "convv": conv_tile(inp["L0_attn_conv_v"]),
    }


def kernel(**inputs):
    shared = _prep_shared(inputs)
    nc = build_nc()
    in_maps = []
    for c in range(8):
        m = dict(shared)
        m.update(_prep_core(inputs, c))
        in_maps.append(m)
    res = run_bass_kernel_spmd(nc, in_maps, core_ids=list(range(8)))
    out = np.empty((NB, TSEQ, D), dtype=np.float32)
    for c in range(8):
        b, s = c // 4, c % 4
        out[b, s * TCORE:(s + 1) * TCORE, :] = res.results[c]["outT"].T
    return out


# version 09
# speedup vs baseline: 1.0781x; 1.0257x over previous
import numpy as np

try:
    import concourse.bass as bass
    import concourse.mybir as mybir
    from concourse.bass_utils import run_bass_kernel_spmd
except ImportError:
    import sys
    sys.path.insert(0, "/opt/trn_rl_repo")
    import concourse.bass as bass
    import concourse.mybir as mybir
    from concourse.bass_utils import run_bass_kernel_spmd

F32 = mybir.dt.float32
BF16 = mybir.dt.bfloat16
FP16 = mybir.dt.float16
AF = mybir.ActivationFunctionType
ALU = mybir.AluOpType

D = 1024
HID = 4096
DH = 256
H = 4
NB = 2
TSEQ = 8192
TCORE = 2048
TT = 256
CH = 128
GCH = 8
GT = CH * GCH
NKD = D // 128
NOH = HID // 128
NT = TCORE // TT
NG = TSEQ // GT
NSLICE = TSEQ // TCORE
EPS_LN = 1e-5
REPL_GROUPS = [[0, 1, 2, 3], [4, 5, 6, 7]]
EPS_L2 = 1e-6
EPS_RMS = 1e-5

SEM_NAMES = ("LDC", "LDX0", "LDX1", "LDR", "LDG", "LDA", "LDO", "LDL", "LDW0", "LDW1", "WS0", "WS1",
             "WL0", "WL1", "WF", "PE", "VE", "AC", "GP", "STR", "STXA", "STXB", "STOA", "STOB", "STU", "CC")


class SBAlloc:
    def __init__(self, nc, base, limit):
        self.nc = nc
        self.cur = base
        self.limit = limit

    def alloc(self, name, shape, dtype, align=64):
        per = int(np.prod(shape[1:])) * mybir.dt.size(dtype)
        off = (self.cur + align - 1) // align * align
        t = self.nc.alloc_sbuf_tensor_at(name, list(shape), dtype, offset=off)
        self.cur = off + per
        assert self.cur <= self.limit, f"SBUF overflow at {name}: {self.cur} > {self.limit}"
        return t


def build_nc():
    nc = bass.Bass("TRN2", target_bir_lowering=False)

    def din(name, shape):
        return nc.dram_tensor(name, list(shape), F32, kind="ExternalInput").ap()

    x0T = din("x0T", [D, TCORE])
    w12t = din("w12t", [8, NKD, 128, 1024])
    h1t = din("h1t", [16, NOH, 128, 256])
    h2t = din("h2t", [16, NOH, 128, 256])
    w3t = din("w3t", [4, NOH, 128, 256])
    w1s_i = din("w1s", [128, NOH])
    w2s_i = din("w2s", [128, NOH])
    h1s_i = din("h1s", [128, NOH])
    h2s_i = din("h2s", [128, NOH])
    w3s_i = din("w3s", [128, NKD])
    wq_i = din("wq", [D, DH])
    wk_i = din("wk", [D, DH])
    wv_i = din("wv", [D, DH])
    wb_i = din("wb", [D, 1])
    convq_i = din("convq", [128, 8])
    convk_i = din("convk", [128, 8])
    convv_i = din("convv", [128, 8])
    wo_i = din("wo", [D, D])
    grms_i = din("grms", [128, 2])
    lng_i = din("lngt", [128, NKD])
    lnb_i = din("lnbt", [128, NKD])
    outT = nc.dram_tensor("outT", [D, TCORE], F32, kind="ExternalOutput").ap()

    resid = nc.dram_tensor("resid", [D, TCORE], F32).ap()
    HC = TCORE // 2
    HS = TSEQ // 2
    xg_in = [nc.dram_tensor(f"xg_in{i}", [D, HC], F32).ap() for i in range(2)]
    xg_out = [nc.dram_tensor(f"xg_out{i}", [NSLICE * D, HC], F32).ap() for i in range(2)]
    og_in = [nc.dram_tensor(f"og_in{i}", [DH, HS], F32).ap() for i in range(2)]
    og_out = [nc.dram_tensor(f"og_out{i}", [H * DH, HS], F32).ap() for i in range(2)]
    wsc = nc.dram_tensor("wsc", [44, 128, 8192], FP16).ap()

    ARENA = 205 * 1024
    arena_base = nc.sbuf_base
    nc.alloc_sbuf_tensor("arena", [128, ARENA], mybir.dt.uint8)
    arena_limit = arena_base + ARENA
    A = SBAlloc(nc, arena_base, arena_limit)
    ones_col = A.alloc("ones_col", [128, 1], F32)
    ones_row = A.alloc("ones_row", [1, 128], F32)
    iota_t = A.alloc("iota_t", [128, 128], F32)
    ident = A.alloc("ident", [128, 128], F32)
    m_trilneg = A.alloc("m_trilneg", [128, 128], F32)
    m_triu = A.alloc("m_triu", [128, 128], F32)
    s_w1 = A.alloc("s_w1", [128, NOH], F32)
    s_w2 = A.alloc("s_w2", [128, NOH], F32)
    s_h1 = A.alloc("s_h1", [128, NOH], F32)
    s_h2 = A.alloc("s_h2", [128, NOH], F32)
    s_w3 = A.alloc("s_w3", [128, NKD], F32)
    s_w3h = A.alloc("s_w3h", [128, NKD], F32)
    c_q = A.alloc("c_q", [128, 8], F32)
    c_k = A.alloc("c_k", [128, 8], F32)
    c_v = A.alloc("c_v", [128, 8], F32)
    g_rms = A.alloc("g_rms", [128, 2], F32)
    ln_g = A.alloc("ln_g", [128, NKD], F32)
    ln_b = A.alloc("ln_b", [128, NKD], F32)
    row_mu = A.alloc("row_mu", [1, TT], F32)
    row_m2 = A.alloc("row_m2", [1, TT], F32)
    row_t = A.alloc("row_t", [1, TT], F32)
    row_var = A.alloc("row_var", [1, TT], F32)
    row_sd = A.alloc("row_sd", [1, TT], F32)
    row_ri = A.alloc("row_ri", [1, TT], F32)
    row_a = A.alloc("row_a", [1, TT], F32)
    row_l2 = A.alloc("row_l2", [1, 512], F32)
    row_l2b = A.alloc("row_l2b", [1, 512], F32)
    x_tile2 = [A.alloc(f"x_tile{i}", [128, NKD * TT], F32) for i in range(2)]
    r2b2 = [A.alloc(f"r2b{i}", [128, NKD * TT], F32) for i in range(2)]
    xlnb_tile = A.alloc("xlnb_tile", [128, NKD * TT], F32)
    sq_tile = A.alloc("sq_tile", [128, 2 * TT], F32)
    z_tile = A.alloc("z_tile", [128, TT], F32)
    e_tile = A.alloc("e_tile", [128, TT], F32)
    silu_t = A.alloc("silu_t", [128, TT], F32)
    hf_t = [A.alloc(f"hf_t{i}", [128, TT], F32) for i in range(2)]
    Rb = A.alloc("Rb", [128, TT], F32)
    Ab = A.alloc("Ab", [128, TT], F32)
    og_yp2 = [A.alloc(f"og_yp{i}", [128, NKD * TT], F32) for i in range(2)]
    const_vals = [EPS_LN, EPS_L2, 0.5, 1.0 / DH]
    const_ts = []
    for i, v in enumerate(sorted(set(const_vals))):
        tcv = A.alloc(f"cst{i}", [128, 1], F32)
        nc.const_aps.register(F32, v, tcv[:, :])
        const_ts.append((tcv, v))
    wo_buf = [A.alloc(f"wo_buf{i}", [128, D], F32) for i in range(2)]

    perm_end = A.cur
    Fz = SBAlloc(nc, perm_end, arena_limit)
    xsp2 = [[Fz.alloc(f"xsp{q}_{p}", [128, NKD * TT], FP16) for p in range(2)] for q in range(2)]
    wbuf = [Fz.alloc(f"wbuf{p}", [128, 8192], FP16) for p in range(2)]
    hspA = [Fz.alloc(f"hspA{p}", [128, NOH * TT], FP16) for p in range(2)]
    hspB = [Fz.alloc(f"hspB{p}", [128, NOH * TT], FP16) for p in range(2)]

    P2 = SBAlloc(nc, perm_end, arena_limit)
    xg_sb = P2.alloc("xg_sb", [128, NKD * GT], F32)
    u_sb = P2.alloc("u_sb", [128, 2 * (GT + 3)], F32)
    qT_sb = P2.alloc("qT_sb", [128, 2 * GT], F32)
    kT_sb = P2.alloc("kT_sb", [128, 2 * GT], F32)
    vT_sb = P2.alloc("vT_sb", [128, 2 * GT], F32)
    wq_sb = P2.alloc("wq_sb", [128, NKD * 256], F32)
    wk_sb = P2.alloc("wk_sb", [128, NKD * 256], F32)
    wv_sb = P2.alloc("wv_sb", [128, NKD * 256], F32)
    wb_sb = P2.alloc("wb_sb", [128, NKD], F32)
    beta_sb = P2.alloc("beta_sb", [128, GCH], F32)
    bh_sb = P2.alloc("bh_sb", [128, 1], F32)
    S_sb = P2.alloc("S_sb", [128, 512], F32)
    NSLOT = 4
    P_pp = [[P2.alloc(f"P_pp{ln}_{i}", [128, 128], F32) for i in range(2)] for ln in range(2)]
    T_pp = [[P2.alloc(f"T_pp{ln}_{i}", [128, 128], F32) for i in range(2)] for ln in range(2)]
    X_a = [P2.alloc(f"X_a{i}", [128, 128], F32) for i in range(NSLOT)]
    W1Tn_a = [P2.alloc(f"W1Tn_a{i}", [128, 256], F32) for i in range(NSLOT)]
    Bv_a = [P2.alloc(f"Bv_a{i}", [128, 256], F32) for i in range(NSLOT)]
    LT_a = [P2.alloc(f"LT_a{i}", [128, 128], F32) for i in range(NSLOT)]
    Ktm_a = [P2.alloc(f"Ktm_a{i}", [128, 256], F32) for i in range(NSLOT)]
    Bkn_t2 = [P2.alloc(f"Bkn_t{ln}", [128, 256], F32) for ln in range(2)]
    U_sb = P2.alloc("U_sb", [128, 256], F32)
    o_sb = P2.alloc("o_sb", [128, 256], F32)
    on_sb = P2.alloc("on_sb", [128, 256], F32)
    ossq = P2.alloc("ossq", [128, 1], F32)
    orr = P2.alloc("orr", [128, 1], F32)
    orinv = P2.alloc("orinv", [128, 1], F32)
    oT_sb = P2.alloc("oT_sb", [128, 256], F32)
    utail = P2.alloc("utail", [128, 18], F32)

    pb = [nc.alloc_psum_tensor(f"pb{i}", [128, 512], F32) for i in range(8)]

    sems = {s: nc.alloc_semaphore(s) for s in SEM_NAMES}

    def emit(me, eng_handle):
        C = {k: 0 for k in SEM_NAMES}
        bank_free = [None] * 8
        rr = [0]

        def op(engine, build, waits=(), incs=(), drain=True):
            assert len(incs) <= 1, "one semaphore update per instruction"
            for s, n in incs:
                C[s] += n
            if me == engine:
                for s, v in waits:
                    if v > 0:
                        eng_handle.wait_ge(sems[s], v)
                ins = build(eng_handle)
                for s, n in incs:
                    ins = ins.then_inc(sems[s], n)
                if drain and engine in ("vector", "scalar", "gpsimd"):
                    eng_handle.drain()

        def waitonly(engine, waits):
            if me == engine:
                for s, v in waits:
                    if v > 0:
                        eng_handle.wait_ge(sems[s], v)

        def get_bank(extra=()):
            b = rr[0] % 8
            rr[0] += 1
            w = list(extra)
            if bank_free[b] is not None:
                w.append(bank_free[b])
            return b, w

        def rel_bank(b, sem):
            bank_free[b] = (sem, C[sem])

        for dst, src in [(s_w1, w1s_i), (s_w2, w2s_i), (s_h1, h1s_i), (s_h2, h2s_i),
                         (s_w3, w3s_i), (c_q, convq_i), (c_k, convk_i), (c_v, convv_i),
                         (g_rms, grms_i), (ln_g, lng_i), (ln_b, lnb_i)]:
            op("sync", lambda e, d=dst, s=src: e.dma_start(out=d[:, :], in_=s),
               incs=[("LDC", 16)])
        ld_consts = C["LDC"]

        op("gpsimd", lambda e: e.iota(iota_t[:, :], pattern=[[1, 128]], base=0,
                                      channel_multiplier=-1,
                                      allow_small_or_imprecise_dtypes=True),
           incs=[("GP", 1)])
        gp_iota = C["GP"]
        op("vector", lambda e: e.tensor_scalar(m_trilneg[:, :], iota_t[:, :], 0.0, -1.0,
                                               ALU.is_lt, ALU.mult),
           waits=[("GP", gp_iota)], incs=[("VE", 1)])
        op("vector", lambda e: e.tensor_single_scalar(m_triu[:, :], iota_t[:, :], 0.0, ALU.is_ge),
           incs=[("VE", 1)])
        op("vector", lambda e: e.tensor_single_scalar(ident[:, :], iota_t[:, :], 0.0, ALU.is_equal),
           incs=[("VE", 1)])
        op("vector", lambda e: e.memset(ones_col[:, :], 1.0), incs=[("VE", 1)])
        op("vector", lambda e: e.memset(ones_row[:, :], 1.0), incs=[("VE", 1)])
        for tcv, v in const_ts:
            op("vector", lambda e, tcv=tcv, v=v: e.memset(tcv[:, :], v), incs=[("VE", 1)])
        op("vector", lambda e: e.tensor_scalar_mul(s_w3h[:, :], s_w3[:, :], 0.5),
           waits=[("LDC", ld_consts)], incs=[("VE", 1)])
        ve_setup = C["VE"]

        S = {
            "pe_sq": 0,
            "ac_rows": 0,
            "pe_rows": 0,
            "ve_silu": 0,
            "ve_hf": [0, 0],
            "st_r": 0, "st_xlnb": ("STXA", 0), "st_out": 0, "st_og": ("STOA", 0),
            "ve_x_read": 0,
            "pe_og_read": 0,
            "ve_yp_read": 0,
            "ve_xlnb_read": 0,
            "pe_l2": 0,
            "ac_osq": 0,
            "ve_orr": 0,
            "pe_scan_q": 0,
            "pe_chain_prev": [0, 0],
            "pe_bkn_prev": [0, 0],
            "pe_u_prev": 0,
            "tail_pe": [0, 0, 0, 0],
            "ve_mt": [0, 0, 0, 0],
            "ve_indep": [0, 0, 0, 0],
            "pe_xg": 0,
            "ve_wo_rd": 0,
        }

        def ln_stats(src, src_ready):
            b1, w1_ = get_bank(list(src_ready) + [("VE", ve_setup)])
            b2, w2_ = get_bank()
            pe_mm2 = {}
            for kc in range(NKD):
                sqs = slice((kc % 2) * TT, (kc % 2) * TT + TT)
                w = []
                if kc == 0:
                    w = list(src_ready) + [("PE", S["pe_sq"])]
                elif kc >= 2:
                    w = [("PE", pe_mm2[kc - 2])]
                op("scalar", lambda e, kc=kc, sqs=sqs: e.activation(
                    sq_tile[:, sqs], src[:, kc * TT:(kc + 1) * TT], AF.Square),
                   waits=w, incs=[("AC", 1)])
                ac_kc = C["AC"]
                op("tensor", lambda e, kc=kc, b=b1: e.matmul(
                    pb[b][0:1, :TT], ones_col[:, :], src[:, kc * TT:(kc + 1) * TT],
                    start=(kc == 0), stop=(kc == NKD - 1)),
                   waits=(w1_ if kc == 0 else ()),
                   incs=([("PE", 1)] if kc == NKD - 1 else ()))
                if kc == NKD - 1:
                    pe_s1 = C["PE"]
                op("tensor", lambda e, kc=kc, b=b2, sqs=sqs: e.matmul(
                    pb[b][0:1, :TT], ones_col[:, :], sq_tile[:, sqs],
                    start=(kc == 0), stop=(kc == NKD - 1)),
                   waits=(list(w2_) if kc == 0 else []) + [("AC", ac_kc)],
                   incs=[("PE", 1)])
                pe_mm2[kc] = C["PE"]
            pe_s2 = C["PE"]
            S["pe_sq"] = pe_s2
            op("vector", lambda e, b=b1: e.tensor_scalar_mul(row_mu[:, :], pb[b][0:1, :TT], 1.0 / D),
               waits=[("PE", pe_s1), ("AC", S["ac_rows"]), ("PE", S["pe_rows"])],
               incs=[("VE", 1)])
            op("vector", lambda e, b=b2: e.tensor_scalar_mul(row_m2[:, :], pb[b][0:1, :TT], 1.0 / D),
               waits=[("PE", pe_s2)], incs=[("VE", 1)])
            rel_bank(b1, "VE")
            rel_bank(b2, "VE")
            op("vector", lambda e: e.tensor_mul(row_t[:, :], row_mu[:, :], row_mu[:, :]),
               incs=[("VE", 1)])
            op("vector", lambda e: e.tensor_sub(row_var[:, :], row_m2[:, :], row_t[:, :]),
               incs=[("VE", 1)])
            ve_var = C["VE"]
            op("scalar", lambda e: e.activation(row_sd[:, :], row_var[:, :], AF.Sqrt, bias=EPS_LN),
               waits=[("VE", ve_var)], incs=[("AC", 1)])
            ac_sd = C["AC"]
            S["ac_rows"] = ac_sd
            op("vector", lambda e: e.reciprocal(row_ri[:, :], row_sd[:, :]),
               waits=[("AC", ac_sd)], incs=[("VE", 1)])
            op("vector", lambda e: e.tensor_mul(row_a[:, :], row_mu[:, :], row_ri[:, :]),
               incs=[("VE", 1)])
            ve_rows = C["VE"]
            b3, w3_ = get_bank([("VE", ve_rows)])
            op("tensor", lambda e, b=b3: e.matmul(pb[b][:, :TT], ones_row[:, :], row_ri[0:1, :TT],
                                                  start=True, stop=True),
               waits=w3_, incs=[("PE", 1)])
            pe_b1 = C["PE"]
            b4, w4_ = get_bank()
            op("tensor", lambda e, b=b4: e.matmul(pb[b][:, :TT], ones_row[:, :], row_a[0:1, :TT],
                                                  start=True, stop=True),
               waits=w4_, incs=[("PE", 1)])
            pe_b2 = C["PE"]
            S["pe_rows"] = pe_b2
            op("vector", lambda e, b=b3: e.tensor_copy(Rb[:, :], pb[b][:, :TT]),
               waits=[("PE", pe_b1)], incs=[("VE", 1)])
            rel_bank(b3, "VE")
            op("vector", lambda e, b=b4: e.tensor_copy(Ab[:, :], pb[b][:, :TT]),
               waits=[("PE", pe_b2)], incs=[("VE", 1)])
            rel_bank(b4, "VE")
            return C["VE"]

        def ln_split(src, ve_rb, xsp, pe_guard=0):
            for kc in range(NKD):
                cs = slice(kc * TT, (kc + 1) * TT)
                op("vector", lambda e, cs=cs: e.tensor_mul(z_tile[:, :], src[:, cs], Rb[:, :]),
                   waits=([("VE", ve_rb), ("PE", pe_guard)] if kc == 0 else ()),
                   incs=[("VE", 1)])
                op("vector", lambda e: e.tensor_sub(z_tile[:, :], z_tile[:, :], Ab[:, :]),
                   incs=[("VE", 1)])
                op("vector", lambda e, cs=cs, xsp=xsp: e.tensor_copy(xsp[0][:, cs], z_tile[:, :]),
                   incs=[("VE", 1)])
                op("vector", lambda e, cs=cs, xsp=xsp: e.tensor_sub(xsp[1][:, cs], z_tile[:, :], xsp[0][:, cs]),
                   incs=[("VE", 1)])
            return C["VE"]

        wl_state = {"idx": 0}
        NGRP_W = 44

        def wload(src_ap, nk, sub):
            idx = wl_state["idx"]
            gidx = idx % NGRP_W
            par = idx % 2
            sem = "WL0" if par == 0 else "WL1"
            wssem = "WS0" if par == 0 else "WS1"
            waits = []
            if idx >= 2:
                waits.append(("WF", idx - 1))
                if idx - 2 < NGRP_W:
                    waits.append((wssem, 16 * (idx // 2)))
            if idx < NGRP_W:
                op("gpsimd", lambda e: e.dma_start(
                    out=wbuf[par].ap().rearrange("p (k j) -> p k j", k=nk)[:, :, :sub],
                    in_=src_ap.rearrange("k p j -> p k j")),
                   waits=waits, incs=[(sem, 16)], drain=False)
                op("sync", lambda e: e.dma_start(
                    out=wsc[gidx], in_=wbuf[par][:, :]),
                   waits=[(sem, 16 * (idx // 2 + 1))], incs=[(wssem, 16)])
            else:
                gs = "WS0" if gidx % 2 == 0 else "WS1"
                w2 = (gs, 16 * (gidx // 2 + 1))
                if w2 not in waits and not any(s == gs and v >= w2[1] for s, v in waits):
                    waits.append(w2)
                op("gpsimd", lambda e: e.dma_start(
                    out=wbuf[par][:, :], in_=wsc[gidx]),
                   waits=waits, incs=[(sem, 16)], drain=False)
            wl_state["idx"] += 1
            return par, (sem, 16 * (idx // 2 + 1))

        def ffn(ve_split, in1_tile, out_tile, st_out_guard, xsp, inject=None):
            inj = inject or {}

            def fire(k):
                for fn in inj.get(k, ()):
                    fn()
            for og in range(8):
                par, wlv = wload(w12t[og], NKD, 1024)
                for ocp in range(4):
                    oc = og * 4 + ocp
                    bA, wA = get_bank()
                    if ocp == 0:
                        wA += [wlv, ("VE", ve_split)]
                    for kc in range(NKD):
                        for p in range(2):
                            last = (kc == NKD - 1 and p == 1)
                            op("tensor", lambda e, kc=kc, p=p, b=bA, ocp=ocp, par=par: e.matmul(
                                pb[b][:, :TT],
                                wbuf[par][:, kc * 1024 + ocp * 128: kc * 1024 + ocp * 128 + 128],
                                xsp[p][:, kc * TT:(kc + 1) * TT],
                                start=(kc == 0 and p == 0), stop=last),
                               waits=(wA if (kc == 0 and p == 0) else ()),
                               incs=([("PE", 1)] if last else ()))
                    peA = C["PE"]
                    bB, wB = get_bank()
                    for kc in range(NKD):
                        for p in range(2):
                            last = (kc == NKD - 1 and p == 1)
                            op("tensor", lambda e, kc=kc, p=p, b=bB, ocp=ocp, par=par: e.matmul(
                                pb[b][:, :TT],
                                wbuf[par][:, kc * 1024 + 512 + ocp * 128: kc * 1024 + 512 + ocp * 128 + 128],
                                xsp[p][:, kc * TT:(kc + 1) * TT],
                                start=(kc == 0 and p == 0), stop=last),
                               waits=(wB if (kc == 0 and p == 0) else ()),
                               incs=([("PE", 1)] if last else ()))
                    peB = C["PE"]
                    op("scalar", lambda e, b=bA, oc=oc: e.activation(
                        silu_t[:, :], pb[b][:, :TT], AF.Silu, scale=s_w1[:, oc:oc + 1]),
                       waits=[("PE", peA), ("VE", S["ve_silu"])], incs=[("AC", 1)])
                    rel_bank(bA, "AC")
                    ac_silu = C["AC"]
                    ht = hf_t[oc % 2]
                    op("vector", lambda e, b=bB, oc=oc, ht=ht: e.scalar_tensor_tensor(
                        ht[:, :], pb[b][:, :TT], s_w2[:, oc:oc + 1], silu_t[:, :],
                        ALU.mult, ALU.mult),
                       waits=[("PE", peB), ("AC", ac_silu)], incs=[("VE", 1)])
                    S["ve_silu"] = C["VE"]
                    rel_bank(bB, "VE")
                    if ocp == 3:
                        op("vector", lambda e: e.engine_nop(), incs=[("WF", 1)], drain=False)
                    cs = slice(oc * TT, (oc + 1) * TT)
                    op("vector", lambda e, cs=cs, ht=ht: e.tensor_copy(hspA[0][:, cs], ht[:, :]),
                       incs=[("VE", 1)])
                    op("vector", lambda e, cs=cs, ht=ht: e.tensor_sub(hspA[1][:, cs], ht[:, :], hspA[0][:, cs]),
                       incs=[("VE", 1)])
                    S["ve_hf"][oc % 2] = C["VE"]
            ve_hA = C["VE"]
            pe_stageA = C["PE"]
            fire("A_end")

            def hidden_stage(wt_ap, s_t, src_sp, dst_sp, ve_src, mid, end):
                for og in range(16):
                    if og == 8:
                        fire(mid)
                    par, wlv = wload(wt_ap[og], NOH, 256)
                    for ocj in range(2):
                        oc = og * 2 + ocj
                        bk, wk_ = get_bank()
                        if ocj == 0:
                            wk_ += [wlv, ("VE", ve_src)]
                        for kc in range(NOH):
                            for p in range(2):
                                last = (kc == NOH - 1 and p == 1)
                                op("tensor", lambda e, kc=kc, p=p, b=bk, ocj=ocj, par=par, sp=src_sp: e.matmul(
                                    pb[b][:, :TT],
                                    wbuf[par][:, kc * 256 + ocj * 128: kc * 256 + ocj * 128 + 128],
                                    sp[p][:, kc * TT:(kc + 1) * TT],
                                    start=(kc == 0 and p == 0), stop=last),
                                   waits=(wk_ if (kc == 0 and p == 0) else ()),
                                   incs=([("PE", 1)] if last else ()))
                        peg = C["PE"]
                        ht = hf_t[oc % 2]
                        op("scalar", lambda e, b=bk, oc=oc, ht=ht, s_t=s_t: e.activation(
                            ht[:, :], pb[b][:, :TT], AF.Gelu, scale=s_t[:, oc:oc + 1]),
                           waits=[("PE", peg), ("VE", S["ve_hf"][oc % 2])], incs=[("AC", 1)])
                        rel_bank(bk, "AC")
                        ac_g = C["AC"]
                        cs = slice(oc * TT, (oc + 1) * TT)
                        op("vector", lambda e, cs=cs, ht=ht, sp=dst_sp: e.tensor_copy(sp[0][:, cs], ht[:, :]),
                           waits=[("AC", ac_g)], incs=[("VE", 1)])
                        op("vector", lambda e, cs=cs, ht=ht, sp=dst_sp: e.tensor_sub(sp[1][:, cs], ht[:, :], sp[0][:, cs]),
                           incs=[("VE", 1)])
                        S["ve_hf"][oc % 2] = C["VE"]
                        if ocj == 1:
                            op("vector", lambda e: e.engine_nop(), incs=[("WF", 1)], drain=False)
                fire(end)
                return C["VE"]

            ve_hB = hidden_stage(h1t, s_h1, hspA, hspB, ve_hA, "B_mid", "B_end")
            ve_hC = hidden_stage(h2t, s_h2, hspB, hspA, ve_hB, "C_mid", "C_end")
            ffn.pe_stageA = pe_stageA

            for og in range(4):
                par, wlv = wload(w3t[og], NOH, 256)
                for ocj in range(2):
                    oc = og * 2 + ocj
                    bk, wk_ = get_bank()
                    if ocj == 0:
                        wk_ += [wlv, ("VE", ve_hC)]
                    for kc in range(NOH):
                        for p in range(2):
                            last = (kc == NOH - 1 and p == 1)
                            op("tensor", lambda e, kc=kc, p=p, b=bk, ocj=ocj, par=par: e.matmul(
                                pb[b][:, :TT],
                                wbuf[par][:, kc * 256 + ocj * 128: kc * 256 + ocj * 128 + 128],
                                hspA[p][:, kc * TT:(kc + 1) * TT],
                                start=(kc == 0 and p == 0), stop=last),
                               waits=(wk_ if (kc == 0 and p == 0) else ()),
                               incs=([("PE", 1)] if last else ()))
                    peg = C["PE"]
                    cs = slice(oc * TT, (oc + 1) * TT)
                    w = [("PE", peg)]
                    if oc == 0:
                        w.append(st_out_guard())
                    op("vector", lambda e, b=bk, oc=oc, cs=cs: e.scalar_tensor_tensor(
                        out_tile[:, cs], pb[b][:, :TT], s_w3h[:, oc:oc + 1], in1_tile[:, cs],
                        ALU.mult, ALU.add),
                       waits=w, incs=[("VE", 1)])
                    rel_bank(bk, "VE")
                    if ocj == 1:
                        op("vector", lambda e: e.engine_nop(), incs=[("WF", 1)], drain=False)
            return C["VE"]

        stxa_full = [16 * max(1, NT // 2)]
        ve_split_t = {}
        pe_xsp_par = [0, 0]
        ve_x_par = [0, 0]
        pend1 = {}
        rb1 = {}

        def p1_load_stats(t):
            par = t % 2
            cols = slice(t * TT, (t + 1) * TT)
            xt = x_tile2[par]
            sem = "LDX0" if par == 0 else "LDX1"
            op("sync", lambda e, xt=xt, cols=cols: e.dma_start(
                out=xt.ap().rearrange("p (k c) -> p k c", k=NKD),
                in_=x0T[:, cols].rearrange("(k p) c -> p k c", p=128)),
               waits=[("VE", ve_x_par[par])], incs=[(sem, 16)])
            rb1[t] = ln_stats(xt, [(sem, C[sem])])

        def p1_splits(t):
            par = t % 2
            ve_split_t[t] = ln_split(x_tile2[par], rb1.pop(t), xsp2[par], pe_xsp_par[par])

        def p1_lnb(t):
            r_tile = r2b2[0]
            ve_r = pend1[t][0]
            ve_rb2 = ln_stats(r_tile, [("VE", ve_r)])
            for kc in range(NKD):
                cs = slice(kc * TT, (kc + 1) * TT)
                w = []
                if kc == 0:
                    w = [("VE", ve_rb2), S["st_xlnb"]]
                op("vector", lambda e, cs=cs, r_tile=r_tile: e.tensor_mul(z_tile[:, :], r_tile[:, cs], Rb[:, :]),
                   waits=w, incs=[("VE", 1)])
                op("vector", lambda e, cs=cs: e.tensor_sub(xlnb_tile[:, cs], z_tile[:, :], Ab[:, :]),
                   incs=[("VE", 1)])
            pend1[t] = (ve_r, C["VE"])

        def p1_stores(t):
            cols = slice(t * TT, (t + 1) * TT)
            ve_r, ve_xlnb = pend1.pop(t)
            op("sync", lambda e, cols=cols: e.dma_start(
                out=resid[:, cols].rearrange("(k p) c -> p k c", p=128),
                in_=r2b2[0].ap().rearrange("p (k c) -> p k c", k=NKD)),
               waits=[("VE", ve_r)], incs=[("STR", 16)])
            S["st_r"] = C["STR"]
            half = (t * TT) // HC
            colh = (t * TT) % HC
            sxs = "STXA" if half == 0 else "STXB"
            op("sync", lambda e, half=half, colh=colh: e.dma_start(
                out=xg_in[half][:, colh:colh + TT].rearrange("(k p) c -> p k c", p=128),
                in_=xlnb_tile.ap().rearrange("p (k c) -> p k c", k=NKD)),
               waits=[("VE", ve_xlnb)], incs=[(sxs, 16)])
            S["st_xlnb"] = (sxs, C[sxs])

        p1_load_stats(0)
        p1_splits(0)
        for t in range(NT):
            par = t % 2
            inj = {}
            if t + 1 < NT:
                inj["A_end"] = [lambda tt=t + 1: p1_load_stats(tt)]
                inj["B_end"] = [lambda tt=t + 1: p1_splits(tt)]
            if t - 1 in pend1:
                inj.setdefault("A_end", []).append(lambda tt=t - 1: p1_stores(tt))
            pend1[t] = (0, 0)
            ve_r = ffn(ve_split_t.pop(t), x_tile2[par], r2b2[0],
                       lambda: ("STR", S["st_r"]), xsp2[par], inject=inj)
            pe_xsp_par[par] = ffn.pe_stageA
            ve_x_par[par] = ve_r
            pend1[t] = (ve_r, None)
            p1_lnb(t)
            if t == NT // 2 and NT > 1:
                op("gpsimd", lambda e: e.collective_compute(
                    "AllGather", ALU.bypass, replica_groups=REPL_GROUPS,
                    ins=[xg_in[0]], outs=[xg_out[0]]),
                   waits=[("STXA", stxa_full[0])], incs=[("CC", 1)], drain=False)
        p1_stores(NT - 1)
        if NT <= 1:
            op("gpsimd", lambda e: e.collective_compute(
                "AllGather", ALU.bypass, replica_groups=REPL_GROUPS,
                ins=[xg_in[0]], outs=[xg_out[0]]),
               waits=[("STXA", stxa_full[0])], incs=[("CC", 1)], drain=False)
        op("gpsimd", lambda e: e.collective_compute(
            "AllGather", ALU.bypass, replica_groups=REPL_GROUPS,
            ins=[xg_in[1]], outs=[xg_out[1]]),
           waits=[("STXB", C["STXB"])], incs=[("CC", 1)], drain=False)
        cc1 = C["CC"]

        f_pe, f_ac, f_ve = C["PE"], C["AC"], C["VE"]
        aw = [("PE", f_pe), ("AC", f_ac), ("VE", f_ve)]
        for dst, src in [(wq_sb, wq_i), (wk_sb, wk_i), (wv_sb, wv_i)]:
            op("sync", lambda e, d=dst, s=src: e.dma_start(
                out=d.ap().rearrange("p (k j) -> p k j", k=NKD),
                in_=s.rearrange("(k p) j -> p k j", p=128)),
               waits=aw, incs=[("LDA", 16)])
            aw = []
        op("sync", lambda e: e.dma_start(
            out=wb_sb.ap().rearrange("p (k j) -> p k j", j=1),
            in_=wb_i.rearrange("(k p) j -> p k j", p=128),
            allow_slow_non_contiguous=True),
           incs=[("LDA", 16)])
        ld_attnw = C["LDA"]
        op("vector", lambda e: e.memset(S_sb[:, :], 0.0),
           waits=[("PE", f_pe), ("AC", f_ac)], incs=[("VE", 1)])
        op("vector", lambda e: e.memset(utail[:, :], 0.0), incs=[("VE", 1)])
        ve_supd = C["VE"]

        NPIECE = max(1, GT // 512)
        PW = GT // NPIECE
        for g in range(NG):
            gt0 = g * GT
            blk = gt0 // TCORE
            col0 = gt0 % TCORE
            hb = col0 // HC
            colh = col0 % HC
            cc_need = 1 if hb == 0 else 2
            ld_xg = []
            for kc in range(NKD):
                w = [("PE", S["pe_xg"]), ("CC", cc_need)] if kc == 0 else []
                op("sync", lambda e, kc=kc, hb=hb, colh=colh: e.dma_start(
                    out=xg_sb[:, kc * GT:(kc + 1) * GT],
                    in_=xg_out[hb][blk * D + kc * 128: blk * D + kc * 128 + 128,
                                   colh:colh + GT]),
                   waits=w, incs=[("LDG", 16)])
                ld_xg.append(C["LDG"])

            for j in range(GCH):
                bk, wk_ = get_bank()
                for kc in range(NKD):
                    w = (wk_ + [("LDG", ld_xg[-1]), ("LDA", ld_attnw)]) if kc == 0 else []
                    op("tensor", lambda e, kc=kc, j=j, b=bk: e.matmul(
                        pb[b][:, 0:1],
                        xg_sb[:, kc * GT + j * CH: kc * GT + j * CH + 128],
                        wb_sb[:, kc:kc + 1],
                        start=(kc == 0), stop=(kc == NKD - 1)),
                       waits=w, incs=([("PE", 1)] if kc == NKD - 1 else ()))
                peb = C["PE"]
                op("scalar", lambda e, b=bk: e.activation(
                    bh_sb[:, :], pb[b][:, 0:1], AF.Tanh, scale=0.5),
                   waits=[("PE", peb), ("VE", S.get("ve_bh", 0))], incs=[("AC", 1)])
                rel_bank(bk, "AC")
                acb = C["AC"]
                op("vector", lambda e, j=j: e.tensor_scalar(
                    beta_sb[:, j:j + 1], bh_sb[:, :], 0.5, 0.5, ALU.mult, ALU.add),
                   waits=[("AC", acb)], incs=[("VE", 1)])
                S["ve_bh"] = C["VE"]
            ve_beta = C["VE"]

            for wi, (wsb, csb, dst) in enumerate([(wq_sb, c_q, qT_sb),
                                                  (wk_sb, c_k, kT_sb),
                                                  (wv_sb, c_v, vT_sb)]):
                for c2 in range(2):
                    op("vector", lambda e, c2=c2, wi=wi: e.tensor_copy(
                        u_sb[:, c2 * (GT + 3): c2 * (GT + 3) + 3],
                        utail[:, (wi * 2 + c2) * 3: (wi * 2 + c2) * 3 + 3]),
                       incs=[("VE", 1)])
                for c2 in range(2):
                    for piece in range(NPIECE):
                        bk, wk_ = get_bank()
                        w = wk_ + [("LDG", ld_xg[-1]), ("LDA", ld_attnw)]
                        for kc in range(NKD):
                            op("tensor", lambda e, kc=kc, c2=c2, piece=piece, b=bk, wsb=wsb: e.matmul(
                                pb[b][:, :PW],
                                wsb[:, kc * 256 + c2 * 128: kc * 256 + c2 * 128 + 128],
                                xg_sb[:, kc * GT + piece * PW: kc * GT + piece * PW + PW],
                                start=(kc == 0), stop=(kc == NKD - 1)),
                               waits=(w if kc == 0 else ()),
                               incs=([("PE", 1)] if kc == NKD - 1 else ()))
                        pep = C["PE"]
                        op("vector", lambda e, c2=c2, piece=piece, b=bk: e.tensor_copy(
                            u_sb[:, c2 * (GT + 3) + 3 + piece * PW: c2 * (GT + 3) + 3 + piece * PW + PW],
                            pb[b][:, :PW]),
                           waits=[("PE", pep), ("AC", S.get("ac_u", 0))], incs=[("VE", 1)])
                        rel_bank(bk, "VE")
                for c2 in range(2):
                    op("vector", lambda e, c2=c2, wi=wi: e.tensor_copy(
                        utail[:, (wi * 2 + c2) * 3: (wi * 2 + c2) * 3 + 3],
                        u_sb[:, c2 * (GT + 3) + GT: c2 * (GT + 3) + GT + 3]),
                       incs=[("VE", 1)])
                for c2 in range(2):
                    ub = c2 * (GT + 3)
                    dcol = c2 * GT
                    w = [("PE", S["pe_scan_q"])] if (c2 == 0) else []
                    op("vector", lambda e, ub=ub, dcol=dcol, csb=csb, dst=dst, c2=c2: e.tensor_scalar_mul(
                        dst[:, dcol:dcol + GT], u_sb[:, ub:ub + GT], csb[:, c2 * 4:c2 * 4 + 1]),
                       waits=w, incs=[("VE", 1)])
                    for tap in range(1, 4):
                        op("vector", lambda e, ub=ub, dcol=dcol, csb=csb, dst=dst, c2=c2, tap=tap: e.scalar_tensor_tensor(
                            dst[:, dcol:dcol + GT], u_sb[:, ub + tap:ub + tap + GT],
                            csb[:, c2 * 4 + tap:c2 * 4 + tap + 1], dst[:, dcol:dcol + GT],
                            ALU.mult, ALU.add),
                           incs=[("VE", 1)])
                ve_conv = C["VE"]
                for c2 in range(2):
                    op("scalar", lambda e, c2=c2, dst=dst: e.activation(
                        dst[:, c2 * GT:(c2 + 1) * GT], dst[:, c2 * GT:(c2 + 1) * GT], AF.Silu),
                       waits=([("VE", ve_conv)] if c2 == 0 else ()), incs=[("AC", 1)])
            ac_silu_all = C["AC"]

            for wi, dst in [(0, qT_sb), (1, kT_sb)]:
                for c2 in range(2):
                    w = [("PE", S["pe_l2"])]
                    if wi == 0 and c2 == 0:
                        w.append(("AC", ac_silu_all))
                    op("scalar", lambda e, c2=c2, dst=dst: e.activation(
                        u_sb[:, c2 * (GT + 3) + 3: c2 * (GT + 3) + 3 + GT],
                        dst[:, c2 * GT:(c2 + 1) * GT], AF.Square),
                       waits=w, incs=[("AC", 1)])
                ac_sq2 = C["AC"]
                S["ac_u"] = ac_sq2
                for piece in range(NPIECE):
                    bk, wk_ = get_bank([("AC", ac_sq2)])
                    for c2 in range(2):
                        op("tensor", lambda e, c2=c2, piece=piece, b=bk: e.matmul(
                            pb[b][0:1, :PW], ones_col[:, :],
                            u_sb[:, c2 * (GT + 3) + 3 + piece * PW: c2 * (GT + 3) + 3 + piece * PW + PW],
                            start=(c2 == 0), stop=(c2 == 1)),
                           waits=(wk_ if c2 == 0 else ()),
                           incs=([("PE", 1)] if c2 == 1 else ()))
                    pel = C["PE"]
                    S["pe_l2"] = pel
                    op("scalar", lambda e, b=bk: e.activation(
                        row_l2[0:1, :PW], pb[b][0:1, :PW], AF.Sqrt, bias=EPS_L2),
                       waits=[("PE", pel), ("VE", S["ve_wo_rd"])], incs=[("AC", 1)])
                    rel_bank(bk, "AC")
                    acl = C["AC"]
                    op("vector", lambda e: e.reciprocal(row_l2b[0:1, :PW], row_l2[0:1, :PW]),
                       waits=[("AC", acl)], incs=[("VE", 1)])
                    vel = C["VE"]
                    S["ve_wo_rd"] = vel
                    b2, w2_ = get_bank([("VE", vel)])
                    op("tensor", lambda e, b=b2: e.matmul(
                        pb[b][:, :PW], ones_row[:, :], row_l2b[0:1, :PW],
                        start=True, stop=True),
                       waits=w2_, incs=[("PE", 1)])
                    pebc = C["PE"]
                    for c2 in range(2):
                        op("vector", lambda e, c2=c2, piece=piece, b=b2, dst=dst: e.tensor_mul(
                            dst[:, c2 * GT + piece * PW: c2 * GT + piece * PW + PW],
                            dst[:, c2 * GT + piece * PW: c2 * GT + piece * PW + PW],
                            pb[b][:, :PW]),
                           waits=([("PE", pebc)] if c2 == 0 else ()), incs=[("VE", 1)])
                    rel_bank(b2, "VE")
            ve_qk = C["VE"]

            SKEW = 2

            def indep_gen(j):
                lane = j % 2
                slot = j % NSLOT
                jb = j * CH
                bcol = beta_sb[:, j:j + 1]
                Ppp = P_pp[lane]
                Tpp = T_pp[lane]
                Bkn_t = Bkn_t2[lane]

                def qs(c2):
                    return qT_sb[:, c2 * GT + jb: c2 * GT + jb + 128]

                def ks(c2):
                    return kT_sb[:, c2 * GT + jb: c2 * GT + jb + 128]

                def vs(c2):
                    return vT_sb[:, c2 * GT + jb: c2 * GT + jb + 128]

                slot_guard = [("PE", S["tail_pe"][slot])]
                for c2 in range(2):
                    bk, wk_ = get_bank([("VE", ve_qk), ("AC", ac_silu_all)])
                    op("tensor", lambda e, b=bk, c2=c2: e.transpose(
                        pb[b][:, :128], vs(c2), ident[:, :]),
                       waits=wk_, incs=[("PE", 1)])
                    pet = C["PE"]
                    op("vector", lambda e, b=bk, c2=c2, slot=slot: e.tensor_scalar_mul(
                        Bv_a[slot][:, c2 * 128:(c2 + 1) * 128], pb[b][:, :128], bcol),
                       waits=[("PE", pet), ("VE", ve_beta)] + (slot_guard if c2 == 0 else []),
                       incs=[("VE", 1)])
                    rel_bank(bk, "VE")
                yield
                for c2 in range(2):
                    bk, wk_ = get_bank()
                    op("tensor", lambda e, b=bk, c2=c2: e.transpose(
                        pb[b][:, :128], ks(c2), ident[:, :]),
                       waits=wk_, incs=[("PE", 1)])
                    pet = C["PE"]
                    op("vector", lambda e, b=bk, c2=c2, slot=slot: e.tensor_copy(
                        Ktm_a[slot][:, c2 * 128:(c2 + 1) * 128], pb[b][:, :128]),
                       waits=[("PE", pet)], incs=[("VE", 1)])
                    rel_bank(bk, "VE")
                yield
                bk, wk_ = get_bank()
                for c2 in range(2):
                    op("tensor", lambda e, b=bk, c2=c2: e.matmul(
                        pb[b][:, :128], ks(c2), ks(c2), start=(c2 == 0), stop=(c2 == 1)),
                       waits=(wk_ if c2 == 0 else ()),
                       incs=([("PE", 1)] if c2 == 1 else ()))
                pea = C["PE"]
                op("vector", lambda e, b=bk: e.scalar_tensor_tensor(
                    Ppp[0][:, :], pb[b][:, :128], bcol, m_trilneg[:, :], ALU.mult, ALU.mult),
                   waits=[("PE", pea), ("VE", ve_beta), ("PE", S["pe_chain_prev"][lane])],
                   incs=[("VE", 1)])
                rel_bank(bk, "VE")
                ve_P = C["VE"]
                yield
                bk, wk_ = get_bank([("VE", ve_P)])
                op("tensor", lambda e, b=bk: e.transpose(pb[b][:, :128], Ppp[0][:, :], ident[:, :]),
                   waits=wk_, incs=[("PE", 1)])
                pet0 = C["PE"]
                op("vector", lambda e, b=bk: e.tensor_copy(Tpp[0][:, :], pb[b][:, :128]),
                   waits=[("PE", pet0)], incs=[("VE", 1)])
                rel_bank(bk, "VE")
                ve_T = C["VE"]
                op("vector", lambda e, slot=slot: e.tensor_add(
                    X_a[slot][:, :], Tpp[0][:, :], ident[:, :]), incs=[("VE", 1)])
                ve_X = C["VE"]
                yield
                for l in range(1, 7):
                    pprev, pcur = Ppp[(l - 1) % 2], Ppp[l % 2]
                    tprev, tcur = Tpp[(l - 1) % 2], Tpp[l % 2]
                    bk, wk_ = get_bank([("VE", ve_T), ("VE", ve_P)])
                    op("tensor", lambda e, b=bk, tp=tprev, pp=pprev: e.matmul(
                        pb[b][:, :128], tp[:, :], pp[:, :], start=True, stop=True),
                       waits=wk_, incs=[("PE", 1)])
                    pe1 = C["PE"]
                    bk2, wk2_ = get_bank()
                    op("tensor", lambda e, b=bk2, tp=tprev, pp=pprev: e.matmul(
                        pb[b][:, :128], pp[:, :], tp[:, :], start=True, stop=True),
                       waits=wk2_, incs=[("PE", 1)])
                    pe2 = C["PE"]
                    op("vector", lambda e, b=bk, pc=pcur: e.tensor_copy(pc[:, :], pb[b][:, :128]),
                       waits=[("PE", pe1)], incs=[("VE", 1)])
                    rel_bank(bk, "VE")
                    ve_P = C["VE"]
                    op("vector", lambda e, b=bk2, tc=tcur: e.tensor_copy(tc[:, :], pb[b][:, :128]),
                       waits=[("PE", pe2)], incs=[("VE", 1)])
                    rel_bank(bk2, "VE")
                    ve_T = C["VE"]
                    bk3, wk3_ = get_bank([("VE", ve_P), ("VE", ve_X)])
                    op("tensor", lambda e, b=bk3, pc=pcur, slot=slot: e.matmul(
                        pb[b][:, :128], pc[:, :], X_a[slot][:, :], start=True, stop=True),
                       waits=wk3_, incs=[("PE", 1)])
                    pex = C["PE"]
                    op("vector", lambda e, b=bk3, slot=slot: e.tensor_add(
                        X_a[slot][:, :], X_a[slot][:, :], pb[b][:, :128]),
                       waits=[("PE", pex)], incs=[("VE", 1)])
                    rel_bank(bk3, "VE")
                    ve_X = C["VE"]
                    yield
                S["pe_chain_prev"][lane] = C["PE"]
                S["ve_mt"][slot] = ve_X
                op("vector", lambda e, slot=slot, Bkn_t=Bkn_t: e.tensor_scalar(
                    Bkn_t[:, :], Ktm_a[slot][:, :], bcol, -1.0, ALU.mult, ALU.mult),
                   waits=[("PE", S["pe_bkn_prev"][lane])], incs=[("VE", 1)])
                ve_bkn = C["VE"]
                for c2 in range(2):
                    bk, wk_ = get_bank([("VE", ve_bkn), ("VE", ve_X)])
                    op("tensor", lambda e, b=bk, c2=c2, slot=slot, Bkn_t=Bkn_t: e.matmul(
                        pb[b][:, :128], Bkn_t[:, c2 * 128:(c2 + 1) * 128], X_a[slot][:, :],
                        start=True, stop=True),
                       waits=wk_, incs=[("PE", 1)])
                    pew = C["PE"]
                    if c2 == 1:
                        S["pe_bkn_prev"][lane] = pew
                    op("vector", lambda e, b=bk, c2=c2, slot=slot: e.tensor_copy(
                        W1Tn_a[slot][:, c2 * 128:(c2 + 1) * 128], pb[b][:, :128]),
                       waits=[("PE", pew)], incs=[("VE", 1)])
                    rel_bank(bk, "VE")
                yield
                bk, wk_ = get_bank()
                for c2 in range(2):
                    op("tensor", lambda e, b=bk, c2=c2: e.matmul(
                        pb[b][:, :128], ks(c2), qs(c2), start=(c2 == 0), stop=(c2 == 1)),
                       waits=(wk_ if c2 == 0 else ()),
                       incs=([("PE", 1)] if c2 == 1 else ()))
                pelt = C["PE"]
                op("vector", lambda e, b=bk, slot=slot: e.tensor_mul(
                    LT_a[slot][:, :], pb[b][:, :128], m_triu[:, :]),
                   waits=[("PE", pelt)], incs=[("VE", 1)])
                rel_bank(bk, "VE")
                S["ve_indep"][slot] = C["VE"]

            def run_indep_pair(ja, jb2):
                gens = [indep_gen(ja)]
                if jb2 is not None:
                    gens.append(indep_gen(jb2))
                done = [False] * len(gens)
                while not all(done):
                    for i, g in enumerate(gens):
                        if not done[i]:
                            try:
                                next(g)
                            except StopIteration:
                                done[i] = True

            def tail(j):
                slot = j % NSLOT
                jb = j * CH
                nonlocal ve_supd

                def qs(c2):
                    return qT_sb[:, c2 * GT + jb: c2 * GT + jb + 128]

                bk, wk_ = get_bank([("VE", S["ve_indep"][slot]), ("VE", ve_supd),
                                    ("PE", S["pe_u_prev"])])
                op("tensor", lambda e, b=bk, slot=slot: e.matmul(
                    pb[b][:, :256], X_a[slot][:, :], Bv_a[slot][:, :],
                    start=True, stop=False),
                   waits=wk_)
                for c2 in range(2):
                    op("tensor", lambda e, b=bk, c2=c2, slot=slot: e.matmul(
                        pb[b][:, :256], W1Tn_a[slot][:, c2 * 128:(c2 + 1) * 128],
                        S_sb[:, c2 * 256:(c2 + 1) * 256],
                        start=False, stop=(c2 == 1)),
                       incs=([("PE", 1)] if c2 == 1 else ()))
                peu = C["PE"]
                op("vector", lambda e, b=bk: e.tensor_copy(U_sb[:, :], pb[b][:, :256]),
                   waits=[("PE", peu)], incs=[("VE", 1)])
                rel_bank(bk, "VE")
                ve_U = C["VE"]
                bo, wo_ = get_bank()
                for c2 in range(2):
                    op("tensor", lambda e, b=bo, c2=c2: e.matmul(
                        pb[b][:, :256], qs(c2), S_sb[:, c2 * 256:(c2 + 1) * 256],
                        start=(c2 == 0), stop=False),
                       waits=(wo_ if c2 == 0 else ()))
                op("tensor", lambda e, b=bo, slot=slot: e.matmul(
                    pb[b][:, :256], LT_a[slot][:, :], U_sb[:, :], start=False, stop=True),
                   waits=[("VE", ve_U)], incs=[("PE", 1)])
                peo = C["PE"]
                for c2 in range(2):
                    bk, wk_ = get_bank()
                    op("tensor", lambda e, b=bk, c2=c2, slot=slot: e.matmul(
                        pb[b][:, :256], Ktm_a[slot][:, c2 * 128:(c2 + 1) * 128], U_sb[:, :],
                        start=True, stop=True),
                       waits=wk_, incs=[("PE", 1)])
                    pes = C["PE"]
                    if c2 == 1:
                        S["pe_u_prev"] = pes
                        S["tail_pe"][slot] = pes
                    op("vector", lambda e, b=bk, c2=c2: e.tensor_add(
                        S_sb[:, c2 * 256:(c2 + 1) * 256], S_sb[:, c2 * 256:(c2 + 1) * 256],
                        pb[b][:, :256]),
                       waits=[("PE", pes)], incs=[("VE", 1)])
                    rel_bank(bk, "VE")
                ve_supd = C["VE"]
                op("vector", lambda e, b=bo: e.tensor_copy(o_sb[:, :], pb[b][:, :256]),
                   waits=[("PE", peo), ("AC", S["ac_osq"])], incs=[("VE", 1)])
                rel_bank(bo, "VE")
                ve_o = C["VE"]
                op("scalar", lambda e: e.activation(on_sb[:, :], o_sb[:, :], AF.Square,
                                                    accum_out=ossq[:, :]),
                   waits=[("VE", ve_o), ("VE", S["ve_orr"])], incs=[("AC", 1)])
                op("scalar", lambda e: e.activation(orr[:, :], ossq[:, :], AF.Sqrt,
                                                    scale=1.0 / DH, bias=EPS_RMS),
                   incs=[("AC", 1)])
                ac_or = C["AC"]
                S["ac_osq"] = ac_or
                op("vector", lambda e: e.reciprocal(orinv[:, :], orr[:, :]),
                   waits=[("AC", ac_or)], incs=[("VE", 1)])
                S["ve_orr"] = C["VE"]
                op("vector", lambda e: e.tensor_scalar_mul(on_sb[:, :], o_sb[:, :], orinv[:, :]),
                   incs=[("VE", 1)])
                ve_on = C["VE"]
                for c2 in range(2):
                    bk, wk_ = get_bank([("VE", ve_on)])
                    op("tensor", lambda e, b=bk, c2=c2: e.transpose(
                        pb[b][:, :128], on_sb[:, c2 * 128:(c2 + 1) * 128], ident[:, :]),
                       waits=wk_, incs=[("PE", 1)])
                    peot = C["PE"]
                    w = [("PE", peot)]
                    if c2 == 0:
                        w.append(S["st_og"])
                    op("vector", lambda e, b=bk, c2=c2: e.tensor_scalar_mul(
                        oT_sb[:, c2 * 128:(c2 + 1) * 128], pb[b][:, :128], g_rms[:, c2:c2 + 1]),
                       waits=w, incs=[("VE", 1)])
                    rel_bank(bk, "VE")
                ve_ot = C["VE"]
                sc = gt0 + jb
                oh = sc // HS
                sco = sc % HS
                sos = "STOA" if oh == 0 else "STOB"
                op("sync", lambda e, oh=oh, sco=sco: e.dma_start(
                    out=og_in[oh][:, sco: sco + 128].rearrange("(c p) t -> p c t", p=128),
                    in_=oT_sb.ap().rearrange("p (c t) -> p c t", c=2)),
                   waits=[("VE", ve_ot)], incs=[(sos, 16)])
                S["st_og"] = (sos, C[sos])

            for jp in range(0, GCH, 2):
                run_indep_pair(jp, jp + 1 if jp + 1 < GCH else None)
                if jp >= SKEW:
                    tail(jp - 2)
                    tail(jp - 1)
            tail(GCH - 2)
            tail(GCH - 1)
            S["pe_scan_q"] = C["PE"]
            S["pe_xg"] = C["PE"]
            if g == max(0, NG // 2 - 1):
                op("gpsimd", lambda e: e.collective_compute(
                    "AllGather", ALU.bypass, replica_groups=REPL_GROUPS,
                    ins=[og_in[0]], outs=[og_out[0]]),
                   waits=[("STOA", C["STOA"])], incs=[("CC", 1)], drain=False)

        op("gpsimd", lambda e: e.collective_compute(
            "AllGather", ALU.bypass, replica_groups=REPL_GROUPS,
            ins=[og_in[1]], outs=[og_out[1]]),
           waits=[("STOB", C["STOB"])], incs=[("CC", 1)], drain=False)
        cc2 = C["CC"]

        f3_pe, f3_ac = C["PE"], C["AC"]
        if me == "sync":
            pid = nc.partition_id(engines=[mybir.EngineType.SP])
            slice_reg = eng_handle.to_reg(pid % 4)

        first3 = [[("PE", f3_pe), ("AC", f3_ac)]]
        pe_y_kc = {}
        st3 = {"ve_ogyp": [0, 0], "ve_r2b": [0, 0], "ve_xln": 0, "ve_x3": 0,
               "pe_og": [0, 0]}
        ve_split3 = {}
        pend3 = {}
        pst3 = {}

        def q_loads_y(t):
            par = t % 2
            cols = slice(t * TT, (t + 1) * TT)
            ogt = og_yp2[par]
            C["LDO"] += 16
            if me == "sync":
                for s_, v_ in (first3[0] + [("PE", st3["pe_og"][par]), ("VE", st3["ve_ogyp"][par])]):
                    if v_ > 0:
                        eng_handle.wait_ge(sems[s_], v_)
                for sl in range(4):
                    with eng_handle.If_cmp(slice_reg, sl, "IS_EQ"):
                        c0 = sl * TCORE + t * TT
                        oh = c0 // HS
                        c0h = c0 % HS
                        eng_handle.wait_ge(sems["CC"], cc2 - 1 if oh == 0 else cc2)
                        eng_handle.dma_start(
                            out=ogt.ap().rearrange("p (k c) -> p k c", k=NKD),
                            in_=og_out[oh][:, c0h:c0h + TT].rearrange("(k p) c -> p k c", p=128),
                        ).then_inc(sems["LDO"], 16)
                eng_handle.end_ifs()
            first3[0] = []
            ld_og = C["LDO"]
            halfL = (t * TT) // HC
            colL = (t * TT) % HC
            op("sync", lambda e, halfL=halfL, colL=colL: e.dma_start(
                out=xlnb_tile.ap().rearrange("p (k c) -> p k c", k=NKD),
                in_=xg_in[halfL][:, colL:colL + TT].rearrange("(k p) c -> p k c", p=128)),
               waits=[("VE", st3["ve_xln"])], incs=[("LDL", 16)])
            ld_xln = C["LDL"]
            op("sync", lambda e, cols=cols: e.dma_start(
                out=x_tile2[0].ap().rearrange("p (k c) -> p k c", k=NKD),
                in_=resid[:, cols].rearrange("(k p) c -> p k c", p=128)),
               waits=[("VE", st3["ve_x3"]), ("STR", S["st_r"])], incs=[("LDR", 16)])
            ld_r1 = C["LDR"]
            ld_wo = {}

            def load_wo(kc):
                sem = "LDW0" if kc % 2 == 0 else "LDW1"
                key = (kc - 2) % NKD if pe_y_kc else None
                w = [("PE", pe_y_kc.get(key, 0))] if key is not None else []
                op("sync", lambda e, kc=kc: e.dma_start(
                    out=wo_buf[kc % 2][:, :], in_=wo_i[kc * 128:(kc + 1) * 128, :]),
                   waits=w, incs=[(sem, 16)])
                ld_wo[kc] = (sem, C[sem])

            load_wo(0)
            load_wo(1)
            ybanks = []
            for oc in range(NKD):
                bk, wk_ = get_bank()
                ybanks.append((bk, wk_))
            for kc in range(NKD):
                for oc in range(NKD):
                    bk, wk_ = ybanks[oc]
                    w = []
                    if kc == 0:
                        w = list(wk_) + [("LDO", ld_og)]
                    if oc == 0:
                        w = w + [ld_wo[kc]]
                    op("tensor", lambda e, kc=kc, oc=oc, b=bk, ogt=ogt: e.matmul(
                        pb[b][:, :TT], wo_buf[kc % 2][:, oc * 128:(oc + 1) * 128],
                        ogt[:, kc * TT:(kc + 1) * TT],
                        start=(kc == 0), stop=(kc == NKD - 1)),
                       waits=w,
                       incs=([("PE", 1)] if oc == NKD - 1 else ()))
                pe_y_kc[kc] = C["PE"]
                if kc + 2 < NKD:
                    load_wo(kc + 2)
            pe_y = C["PE"]
            st3["pe_og"][par] = pe_y
            for oc in range(NKD):
                bk, _ = ybanks[oc]
                cs = slice(oc * TT, (oc + 1) * TT)
                w = [("PE", pe_y)]
                if oc == 0:
                    w += [("LDL", ld_xln)]
                op("vector", lambda e, b=bk, cs=cs, ogt=ogt: e.tensor_add(
                    ogt[:, cs], pb[b][:, :TT], xlnb_tile[:, cs]),
                   waits=w, incs=[("VE", 1)])
                rel_bank(bk, "VE")
            pst3[t] = {"ve_yp": C["VE"], "ld_r1": ld_r1}
            st3["ve_xln"] = C["VE"]

        def q_stats1(t):
            par = t % 2
            pst3[t]["ve_rb3"] = ln_stats(og_yp2[par], [("VE", pst3[t]["ve_yp"])])

        def q_zchain(t):
            par = t % 2
            ogt = og_yp2[par]
            r2t = r2b2[par]
            for kc in range(NKD):
                cs = slice(kc * TT, (kc + 1) * TT)
                w = []
                if kc == 0:
                    w = [("VE", pst3[t]["ve_rb3"])]
                op("vector", lambda e, cs=cs, ogt=ogt: e.tensor_mul(z_tile[:, :], ogt[:, cs], Rb[:, :]),
                   waits=w, incs=[("VE", 1)])
                op("vector", lambda e: e.tensor_sub(z_tile[:, :], z_tile[:, :], Ab[:, :]),
                   incs=[("VE", 1)])
                ve_z = C["VE"]
                op("scalar", lambda e, kc=kc: e.activation(
                    e_tile[:, :], z_tile[:, :], AF.Identity,
                    bias=ln_b[:, kc:kc + 1], scale=ln_g[:, kc:kc + 1]),
                   waits=[("VE", ve_z)], incs=[("AC", 1)])
                ac_z = C["AC"]
                w = [("AC", ac_z)]
                if kc == 0:
                    w += [("LDR", pst3[t]["ld_r1"]), ("VE", st3["ve_r2b"][par])]
                op("vector", lambda e, cs=cs, r2t=r2t: e.tensor_add(
                    r2t[:, cs], e_tile[:, :], x_tile2[0][:, cs]),
                   waits=w, incs=[("VE", 1)])
            ve_r2 = C["VE"]
            pst3[t]["ve_r2"] = ve_r2
            st3["ve_ogyp"][par] = ve_r2
            st3["ve_x3"] = ve_r2

        def q_stats2(t):
            par = t % 2
            pst3[t]["ve_rb4"] = ln_stats(r2b2[par], [("VE", pst3[t]["ve_r2"])])

        def q_splits(t):
            par = t % 2
            ve_split3[t] = ln_split(r2b2[par], pst3[t]["ve_rb4"], xsp2[par], pe_xsp_par[par])
            pst3.pop(t)

        def p3_store(t):
            cols = slice(t * TT, (t + 1) * TT)
            ve_out = pend3.pop(t)
            op("sync", lambda e, cols=cols: e.dma_start(
                out=outT[:, cols].rearrange("(k p) c -> p k c", p=128),
                in_=out3.ap().rearrange("p (k c) -> p k c", k=NKD)),
               waits=[("VE", ve_out)], incs=[("STU", 16)])
            S["st_out"] = C["STU"]

        out3 = x_tile2[1]
        q_loads_y(0)
        q_stats1(0)
        q_zchain(0)
        q_stats2(0)
        q_splits(0)
        for t in range(NT):
            par = t % 2
            inj = {}
            if t + 1 < NT:
                inj["A_end"] = [lambda tt=t + 1: q_loads_y(tt)]
                inj["B_mid"] = [lambda tt=t + 1: q_stats1(tt)]
                inj["B_end"] = [lambda tt=t + 1: q_zchain(tt)]
                inj["C_mid"] = [lambda tt=t + 1: q_stats2(tt)]
                inj["C_end"] = [lambda tt=t + 1: q_splits(tt)]
            if t - 1 in pend3:
                inj.setdefault("A_end", []).append(lambda tt=t - 1: p3_store(tt))
            ve_out = ffn(ve_split3.pop(t), r2b2[par], out3,
                         lambda: ("STU", S["st_out"]), xsp2[par], inject=inj)
            pe_xsp_par[par] = ffn.pe_stageA
            st3["ve_r2b"][par] = ve_out
            pend3[t] = ve_out
        p3_store(NT - 1)

        waitonly("sync", [("STU", C["STU"]), ("STR", C["STR"]), ("STXA", C["STXA"]),
                          ("STXB", C["STXB"]), ("STOA", C["STOA"]), ("STOB", C["STOB"])])
        return C

    counts = {}
    with nc.Block() as block:
        def mk(name):
            def f(eng):
                counts[name] = emit(name, eng)
            return f
        block.sync(mk("sync"))
        block.gpsimd(mk("gpsimd"))
        block.tensor(mk("tensor"))
        block.vector(mk("vector"))
        block.scalar(mk("scalar"))

    base = counts["sync"]
    for k, c in counts.items():
        assert c == base, f"counter divergence in {k}"
    return nc


def _prep_shared(inp):
    f = np.float32
    w1T = np.ascontiguousarray(np.asarray(inp["L0_ffn_w1_w"], dtype=f).T)
    w2T = np.ascontiguousarray(np.asarray(inp["L0_ffn_w2_w"], dtype=f).T)
    h1T = np.ascontiguousarray(np.asarray(inp["L0_ffn_h1_w"], dtype=f).T)
    h2T = np.ascontiguousarray(np.asarray(inp["L0_ffn_h2_w"], dtype=f).T)
    w3T = np.ascontiguousarray(np.asarray(inp["L0_ffn_w3_w"], dtype=f).T)

    w12t = np.empty((8, NKD, 128, 1024), f)
    w12t[:, :, :, :512] = w1T.reshape(NKD, 128, 8, 512).transpose(2, 0, 1, 3)
    w12t[:, :, :, 512:] = w2T.reshape(NKD, 128, 8, 512).transpose(2, 0, 1, 3)
    h1t = np.ascontiguousarray(h1T.reshape(NOH, 128, 16, 256).transpose(2, 0, 1, 3))
    h2t = np.ascontiguousarray(h2T.reshape(NOH, 128, 16, 256).transpose(2, 0, 1, 3))
    w3t = np.ascontiguousarray(w3T.reshape(NOH, 128, 4, 256).transpose(2, 0, 1, 3))

    tile_s = lambda s, n: np.ascontiguousarray(np.asarray(s, dtype=f).reshape(n, 128).T)
    return {
        "w12t": np.ascontiguousarray(w12t),
        "h1t": h1t, "h2t": h2t, "w3t": w3t,
        "w1s": tile_s(inp["L0_ffn_w1_s"], NOH),
        "w2s": tile_s(inp["L0_ffn_w2_s"], NOH),
        "h1s": tile_s(inp["L0_ffn_h1_s"], NOH),
        "h2s": tile_s(inp["L0_ffn_h2_s"], NOH),
        "w3s": tile_s(inp["L0_ffn_w3_s"], NKD),
        "wo": np.ascontiguousarray(np.asarray(inp["L0_attn_Wo"], dtype=f)),
        "grms": tile_s(inp["L0_attn_g_rms"], 2),
        "lngt": tile_s(inp["L0_attn_ln_g"], NKD),
        "lnbt": tile_s(inp["L0_attn_ln_b"], NKD),
    }


def _prep_core(inp, c):
    f = np.float32
    b, s = c // 4, c % 4
    h = c % 4
    x = np.asarray(inp["L0_attn_x"], dtype=f)
    x0T = np.ascontiguousarray(x[b, s * TCORE:(s + 1) * TCORE, :].T)
    conv_tile = lambda a: np.ascontiguousarray(
        np.asarray(a, dtype=f)[:, h * DH:(h + 1) * DH].T.reshape(2, 128, 4)
        .transpose(1, 0, 2).reshape(128, 8))
    return {
        "x0T": x0T,
        "wq": np.ascontiguousarray(np.asarray(inp["L0_attn_Wq"], dtype=f)[:, h * DH:(h + 1) * DH]),
        "wk": np.ascontiguousarray(np.asarray(inp["L0_attn_Wk"], dtype=f)[:, h * DH:(h + 1) * DH]),
        "wv": np.ascontiguousarray(np.asarray(inp["L0_attn_Wv"], dtype=f)[:, h * DH:(h + 1) * DH]),
        "wb": np.ascontiguousarray(np.asarray(inp["L0_attn_Wb"], dtype=f)[:, h:h + 1]),
        "convq": conv_tile(inp["L0_attn_conv_q"]),
        "convk": conv_tile(inp["L0_attn_conv_k"]),
        "convv": conv_tile(inp["L0_attn_conv_v"]),
    }


def kernel(**inputs):
    shared = _prep_shared(inputs)
    nc = build_nc()
    in_maps = []
    for c in range(8):
        m = dict(shared)
        m.update(_prep_core(inputs, c))
        in_maps.append(m)
    res = run_bass_kernel_spmd(nc, in_maps, core_ids=list(range(8)))
    out = np.empty((NB, TSEQ, D), dtype=np.float32)
    for c in range(8):
        b, s = c // 4, c % 4
        out[b, s * TCORE:(s + 1) * TCORE, :] = res.results[c]["outT"].T
    return out


# version 11
# speedup vs baseline: 1.0823x; 1.0039x over previous
import numpy as np

try:
    import concourse.bass as bass
    import concourse.mybir as mybir
    from concourse.bass_utils import run_bass_kernel_spmd
except ImportError:
    import sys
    sys.path.insert(0, "/opt/trn_rl_repo")
    import concourse.bass as bass
    import concourse.mybir as mybir
    from concourse.bass_utils import run_bass_kernel_spmd

F32 = mybir.dt.float32
BF16 = mybir.dt.bfloat16
FP16 = mybir.dt.float16
AF = mybir.ActivationFunctionType
ALU = mybir.AluOpType

D = 1024
HID = 4096
DH = 256
H = 4
NB = 2
TSEQ = 8192
TCORE = 2048
TT = 256
CH = 128
GCH = 8
GT = CH * GCH
NKD = D // 128
NOH = HID // 128
NT = TCORE // TT
NG = TSEQ // GT
NSLICE = TSEQ // TCORE
EPS_LN = 1e-5
REPL_GROUPS = [[0, 1, 2, 3], [4, 5, 6, 7]]
EPS_L2 = 1e-6
EPS_RMS = 1e-5

SEM_NAMES = ("LDC", "LDX0", "LDX1", "LDR", "LDG", "LDA", "LDO", "LDL", "LDW0", "LDW1", "WS0", "WS1",
             "WL0", "WL1", "WF", "PE", "VE", "AC", "GP", "STR", "STXA", "STXB", "STOA", "STOB", "STU", "CC")


class SBAlloc:
    def __init__(self, nc, base, limit):
        self.nc = nc
        self.cur = base
        self.limit = limit

    def alloc(self, name, shape, dtype, align=64):
        per = int(np.prod(shape[1:])) * mybir.dt.size(dtype)
        off = (self.cur + align - 1) // align * align
        t = self.nc.alloc_sbuf_tensor_at(name, list(shape), dtype, offset=off)
        self.cur = off + per
        assert self.cur <= self.limit, f"SBUF overflow at {name}: {self.cur} > {self.limit}"
        return t


def build_nc():
    nc = bass.Bass("TRN2", target_bir_lowering=False)

    def din(name, shape):
        return nc.dram_tensor(name, list(shape), F32, kind="ExternalInput").ap()

    x0T = din("x0T", [D, TCORE])
    w12t = din("w12t", [8, NKD, 128, 1024])
    h1t = din("h1t", [16, NOH, 128, 256])
    h2t = din("h2t", [16, NOH, 128, 256])
    w3t = din("w3t", [4, NOH, 128, 256])
    w1s_i = din("w1s", [128, NOH])
    w2s_i = din("w2s", [128, NOH])
    h1s_i = din("h1s", [128, NOH])
    h2s_i = din("h2s", [128, NOH])
    w3s_i = din("w3s", [128, NKD])
    wq_i = din("wq", [D, DH])
    wk_i = din("wk", [D, DH])
    wv_i = din("wv", [D, DH])
    wb_i = din("wb", [D, 1])
    convq_i = din("convq", [128, 8])
    convk_i = din("convk", [128, 8])
    convv_i = din("convv", [128, 8])
    wo_i = din("wo", [D, D])
    grms_i = din("grms", [128, 2])
    lng_i = din("lngt", [128, NKD])
    lnb_i = din("lnbt", [128, NKD])
    outT = nc.dram_tensor("outT", [D, TCORE], F32, kind="ExternalOutput").ap()

    resid = nc.dram_tensor("resid", [D, TCORE], F32).ap()
    HC = TCORE // 2
    HS = TSEQ // 2
    xg_in = [nc.dram_tensor(f"xg_in{i}", [D, HC], F32).ap() for i in range(2)]
    xg_out = [nc.dram_tensor(f"xg_out{i}", [NSLICE * D, HC], F32).ap() for i in range(2)]
    og_in = [nc.dram_tensor(f"og_in{i}", [DH, HS], F32).ap() for i in range(2)]
    og_out = [nc.dram_tensor(f"og_out{i}", [H * DH, HS], F32).ap() for i in range(2)]
    wsc = nc.dram_tensor("wsc", [44, 128, 8192], FP16).ap()

    ARENA = 205 * 1024
    arena_base = nc.sbuf_base
    nc.alloc_sbuf_tensor("arena", [128, ARENA], mybir.dt.uint8)
    arena_limit = arena_base + ARENA
    A = SBAlloc(nc, arena_base, arena_limit)
    ones_col = A.alloc("ones_col", [128, 1], F32)
    ones_row = A.alloc("ones_row", [1, 128], F32)
    iota_t = A.alloc("iota_t", [128, 128], F32)
    ident = A.alloc("ident", [128, 128], F32)
    m_trilneg = A.alloc("m_trilneg", [128, 128], F32)
    m_triu = A.alloc("m_triu", [128, 128], F32)
    s_w1 = A.alloc("s_w1", [128, NOH], F32)
    s_w2 = A.alloc("s_w2", [128, NOH], F32)
    s_h1 = A.alloc("s_h1", [128, NOH], F32)
    s_h2 = A.alloc("s_h2", [128, NOH], F32)
    s_w3 = A.alloc("s_w3", [128, NKD], F32)
    s_w3h = A.alloc("s_w3h", [128, NKD], F32)
    c_q = A.alloc("c_q", [128, 8], F32)
    c_k = A.alloc("c_k", [128, 8], F32)
    c_v = A.alloc("c_v", [128, 8], F32)
    g_rms = A.alloc("g_rms", [128, 2], F32)
    ln_g = A.alloc("ln_g", [128, NKD], F32)
    ln_b = A.alloc("ln_b", [128, NKD], F32)
    row_mu = A.alloc("row_mu", [1, TT], F32)
    row_m2 = A.alloc("row_m2", [1, TT], F32)
    row_t = A.alloc("row_t", [1, TT], F32)
    row_var = A.alloc("row_var", [1, TT], F32)
    row_sd = A.alloc("row_sd", [1, TT], F32)
    row_ri = A.alloc("row_ri", [1, TT], F32)
    row_a = A.alloc("row_a", [1, TT], F32)
    row_l2 = A.alloc("row_l2", [1, 512], F32)
    row_l2b = A.alloc("row_l2b", [1, 512], F32)
    x_tile2 = [A.alloc(f"x_tile{i}", [128, NKD * TT], F32) for i in range(2)]
    r2b2 = [A.alloc(f"r2b{i}", [128, NKD * TT], F32) for i in range(2)]
    xlnb_tile = A.alloc("xlnb_tile", [128, NKD * TT], F32)
    sq_tile = A.alloc("sq_tile", [128, 2 * TT], F32)
    z_tile = A.alloc("z_tile", [128, TT], F32)
    e_tile = A.alloc("e_tile", [128, TT], F32)
    silu_t = A.alloc("silu_t", [128, TT], F32)
    hf_t = [A.alloc(f"hf_t{i}", [128, TT], F32) for i in range(2)]
    Rb = A.alloc("Rb", [128, TT], F32)
    Ab = A.alloc("Ab", [128, TT], F32)
    og_yp2 = [A.alloc(f"og_yp{i}", [128, NKD * TT], F32) for i in range(2)]
    const_vals = [EPS_LN, EPS_L2, 0.5, 1.0 / DH]
    const_ts = []
    for i, v in enumerate(sorted(set(const_vals))):
        tcv = A.alloc(f"cst{i}", [128, 1], F32)
        nc.const_aps.register(F32, v, tcv[:, :])
        const_ts.append((tcv, v))
    wo_buf = [A.alloc(f"wo_buf{i}", [128, D], F32) for i in range(2)]

    perm_end = A.cur
    Fz = SBAlloc(nc, perm_end, arena_limit)
    xsp2 = [[Fz.alloc(f"xsp{q}_{p}", [128, NKD * TT], FP16) for p in range(2)] for q in range(2)]
    wbuf = [Fz.alloc(f"wbuf{p}", [128, 8192], FP16) for p in range(2)]
    hspA = [Fz.alloc(f"hspA{p}", [128, NOH * TT], FP16) for p in range(2)]
    hspB = [Fz.alloc(f"hspB{p}", [128, NOH * TT], FP16) for p in range(2)]

    P2 = SBAlloc(nc, perm_end, arena_limit)
    xg_sb = P2.alloc("xg_sb", [128, NKD * GT], F32)
    u_sb = P2.alloc("u_sb", [128, 2 * (GT + 3)], F32)
    qT_sb = P2.alloc("qT_sb", [128, 2 * GT], F32)
    kT_sb = P2.alloc("kT_sb", [128, 2 * GT], F32)
    vT_sb = P2.alloc("vT_sb", [128, 2 * GT], F32)
    wq_sb = P2.alloc("wq_sb", [128, NKD * 256], F32)
    wk_sb = P2.alloc("wk_sb", [128, NKD * 256], F32)
    wv_sb = P2.alloc("wv_sb", [128, NKD * 256], F32)
    wb_sb = P2.alloc("wb_sb", [128, NKD], F32)
    beta_sb = P2.alloc("beta_sb", [128, GCH], F32)
    bh_sb = P2.alloc("bh_sb", [128, 1], F32)
    S_sb = P2.alloc("S_sb", [128, 512], F32)
    NSLOT = 4
    P_pp = [[P2.alloc(f"P_pp{ln}_{i}", [128, 128], F32) for i in range(2)] for ln in range(2)]
    T_pp = [[P2.alloc(f"T_pp{ln}_{i}", [128, 128], F32) for i in range(2)] for ln in range(2)]
    X_a = [P2.alloc(f"X_a{i}", [128, 128], F32) for i in range(NSLOT)]
    W1Tn_a = [P2.alloc(f"W1Tn_a{i}", [128, 256], F32) for i in range(NSLOT)]
    Bv_a = [P2.alloc(f"Bv_a{i}", [128, 256], F32) for i in range(NSLOT)]
    LT_a = [P2.alloc(f"LT_a{i}", [128, 128], F32) for i in range(NSLOT)]
    Ktm_a = [P2.alloc(f"Ktm_a{i}", [128, 256], F32) for i in range(NSLOT)]
    Bkn_t2 = [P2.alloc(f"Bkn_t{ln}", [128, 256], F32) for ln in range(2)]
    U_sb = P2.alloc("U_sb", [128, 256], F32)
    o_sb = P2.alloc("o_sb", [128, 256], F32)
    on_sb = P2.alloc("on_sb", [128, 256], F32)
    ossq = P2.alloc("ossq", [128, 1], F32)
    orr = P2.alloc("orr", [128, 1], F32)
    orinv = P2.alloc("orinv", [128, 1], F32)
    oT_sb = P2.alloc("oT_sb", [128, 256], F32)
    utail = P2.alloc("utail", [128, 18], F32)

    pb = [nc.alloc_psum_tensor(f"pb{i}", [128, 512], F32) for i in range(8)]

    sems = {s: nc.alloc_semaphore(s) for s in SEM_NAMES}

    def emit(me, eng_handle):
        C = {k: 0 for k in SEM_NAMES}
        bank_free = [None] * 8
        rr = [0]

        rr_slow = [0]

        def op(engine, build, waits=(), incs=(), drain=True):
            assert len(incs) <= 1, "one semaphore update per instruction"
            for s, n in incs:
                C[s] += n
            if me == engine:
                for s, v in waits:
                    if v > 0:
                        eng_handle.wait_ge(sems[s], v)
                ins = build(eng_handle)
                for s, n in incs:
                    ins = ins.then_inc(sems[s], n)
                if drain and engine in ("vector", "scalar", "gpsimd"):
                    eng_handle.drain()

        def waitonly(engine, waits):
            if me == engine:
                for s, v in waits:
                    if v > 0:
                        eng_handle.wait_ge(sems[s], v)

        def get_bank(extra=()):
            b = rr[0] % 6
            rr[0] += 1
            w = list(extra)
            if bank_free[b] is not None:
                w.append(bank_free[b])
            return b, w

        def get_bank_slow(extra=()):
            b = 6 + rr_slow[0] % 2
            rr_slow[0] += 1
            w = list(extra)
            if bank_free[b] is not None:
                w.append(bank_free[b])
            return b, w

        def rel_bank(b, sem):
            bank_free[b] = (sem, C[sem])

        for dst, src in [(s_w1, w1s_i), (s_w2, w2s_i), (s_h1, h1s_i), (s_h2, h2s_i),
                         (s_w3, w3s_i), (c_q, convq_i), (c_k, convk_i), (c_v, convv_i),
                         (g_rms, grms_i), (ln_g, lng_i), (ln_b, lnb_i)]:
            op("sync", lambda e, d=dst, s=src: e.dma_start(out=d[:, :], in_=s),
               incs=[("LDC", 16)])
        ld_consts = C["LDC"]

        op("gpsimd", lambda e: e.iota(iota_t[:, :], pattern=[[1, 128]], base=0,
                                      channel_multiplier=-1,
                                      allow_small_or_imprecise_dtypes=True),
           incs=[("GP", 1)])
        gp_iota = C["GP"]
        op("vector", lambda e: e.tensor_scalar(m_trilneg[:, :], iota_t[:, :], 0.0, -1.0,
                                               ALU.is_lt, ALU.mult),
           waits=[("GP", gp_iota)], incs=[("VE", 1)])
        op("vector", lambda e: e.tensor_single_scalar(m_triu[:, :], iota_t[:, :], 0.0, ALU.is_ge),
           incs=[("VE", 1)])
        op("vector", lambda e: e.tensor_single_scalar(ident[:, :], iota_t[:, :], 0.0, ALU.is_equal),
           incs=[("VE", 1)])
        op("vector", lambda e: e.memset(ones_col[:, :], 1.0), incs=[("VE", 1)])
        op("vector", lambda e: e.memset(ones_row[:, :], 1.0), incs=[("VE", 1)])
        for tcv, v in const_ts:
            op("vector", lambda e, tcv=tcv, v=v: e.memset(tcv[:, :], v), incs=[("VE", 1)])
        op("vector", lambda e: e.tensor_scalar_mul(s_w3h[:, :], s_w3[:, :], 0.5),
           waits=[("LDC", ld_consts)], incs=[("VE", 1)])
        ve_setup = C["VE"]

        S = {
            "pe_sq": 0,
            "ac_rows": 0,
            "pe_rows": 0,
            "ve_silu": 0,
            "ve_hf": [0, 0],
            "st_r": 0, "st_xlnb": ("STXA", 0), "st_out": 0, "st_og": ("STOA", 0),
            "ve_x_read": 0,
            "pe_og_read": 0,
            "ve_yp_read": 0,
            "ve_xlnb_read": 0,
            "pe_l2": 0,
            "ac_osq": 0,
            "ve_orr": 0,
            "pe_scan_q": 0,
            "pe_chain_prev": [0, 0],
            "pe_bkn_prev": [0, 0],
            "pe_u_prev": 0,
            "tail_pe": [0, 0, 0, 0],
            "ve_mt": [0, 0, 0, 0],
            "ve_indep": [0, 0, 0, 0],
            "pe_xg": 0,
            "ve_wo_rd": 0,
        }

        def ln_stats(src, src_ready):
            b1, w1_ = get_bank_slow(list(src_ready) + [("VE", ve_setup)])
            b2, w2_ = get_bank_slow()
            pe_mm2 = {}
            for kc in range(NKD):
                sqs = slice((kc % 2) * TT, (kc % 2) * TT + TT)
                w = []
                if kc == 0:
                    w = list(src_ready) + [("PE", S["pe_sq"])]
                elif kc >= 2:
                    w = [("PE", pe_mm2[kc - 2])]
                op("scalar", lambda e, kc=kc, sqs=sqs: e.activation(
                    sq_tile[:, sqs], src[:, kc * TT:(kc + 1) * TT], AF.Square),
                   waits=w, incs=[("AC", 1)])
                ac_kc = C["AC"]
                op("tensor", lambda e, kc=kc, b=b1: e.matmul(
                    pb[b][0:1, :TT], ones_col[:, :], src[:, kc * TT:(kc + 1) * TT],
                    start=(kc == 0), stop=(kc == NKD - 1)),
                   waits=(w1_ if kc == 0 else ()),
                   incs=([("PE", 1)] if kc == NKD - 1 else ()))
                if kc == NKD - 1:
                    pe_s1 = C["PE"]
                op("tensor", lambda e, kc=kc, b=b2, sqs=sqs: e.matmul(
                    pb[b][0:1, :TT], ones_col[:, :], sq_tile[:, sqs],
                    start=(kc == 0), stop=(kc == NKD - 1)),
                   waits=(list(w2_) if kc == 0 else []) + [("AC", ac_kc)],
                   incs=[("PE", 1)])
                pe_mm2[kc] = C["PE"]
            pe_s2 = C["PE"]
            S["pe_sq"] = pe_s2
            op("vector", lambda e, b=b1: e.tensor_scalar_mul(row_mu[:, :], pb[b][0:1, :TT], 1.0 / D),
               waits=[("PE", pe_s1), ("AC", S["ac_rows"]), ("PE", S["pe_rows"])],
               incs=[("VE", 1)])
            op("vector", lambda e, b=b2: e.tensor_scalar_mul(row_m2[:, :], pb[b][0:1, :TT], 1.0 / D),
               waits=[("PE", pe_s2)], incs=[("VE", 1)])
            rel_bank(b1, "VE")
            rel_bank(b2, "VE")
            op("vector", lambda e: e.tensor_mul(row_t[:, :], row_mu[:, :], row_mu[:, :]),
               incs=[("VE", 1)])
            op("vector", lambda e: e.tensor_sub(row_var[:, :], row_m2[:, :], row_t[:, :]),
               incs=[("VE", 1)])
            ve_var = C["VE"]
            op("scalar", lambda e: e.activation(row_sd[:, :], row_var[:, :], AF.Sqrt, bias=EPS_LN),
               waits=[("VE", ve_var)], incs=[("AC", 1)])
            ac_sd = C["AC"]
            S["ac_rows"] = ac_sd
            op("vector", lambda e: e.reciprocal(row_ri[:, :], row_sd[:, :]),
               waits=[("AC", ac_sd)], incs=[("VE", 1)])
            op("vector", lambda e: e.tensor_mul(row_a[:, :], row_mu[:, :], row_ri[:, :]),
               incs=[("VE", 1)])
            ve_rows = C["VE"]
            b3, w3_ = get_bank_slow([("VE", ve_rows)])
            op("tensor", lambda e, b=b3: e.matmul(pb[b][:, :TT], ones_row[:, :], row_ri[0:1, :TT],
                                                  start=True, stop=True),
               waits=w3_, incs=[("PE", 1)])
            pe_b1 = C["PE"]
            b4, w4_ = get_bank_slow()
            op("tensor", lambda e, b=b4: e.matmul(pb[b][:, :TT], ones_row[:, :], row_a[0:1, :TT],
                                                  start=True, stop=True),
               waits=w4_, incs=[("PE", 1)])
            pe_b2 = C["PE"]
            S["pe_rows"] = pe_b2
            op("vector", lambda e, b=b3: e.tensor_copy(Rb[:, :], pb[b][:, :TT]),
               waits=[("PE", pe_b1)], incs=[("VE", 1)])
            rel_bank(b3, "VE")
            op("vector", lambda e, b=b4: e.tensor_copy(Ab[:, :], pb[b][:, :TT]),
               waits=[("PE", pe_b2)], incs=[("VE", 1)])
            rel_bank(b4, "VE")
            return C["VE"]

        def ln_split(src, ve_rb, xsp, pe_guard=0):
            for kc in range(NKD):
                cs = slice(kc * TT, (kc + 1) * TT)
                op("vector", lambda e, cs=cs: e.tensor_mul(z_tile[:, :], src[:, cs], Rb[:, :]),
                   waits=([("VE", ve_rb), ("PE", pe_guard)] if kc == 0 else ()),
                   incs=[("VE", 1)])
                op("vector", lambda e: e.tensor_sub(z_tile[:, :], z_tile[:, :], Ab[:, :]),
                   incs=[("VE", 1)])
                op("vector", lambda e, cs=cs, xsp=xsp: e.tensor_copy(xsp[0][:, cs], z_tile[:, :]),
                   incs=[("VE", 1)])
                op("vector", lambda e, cs=cs, xsp=xsp: e.tensor_sub(xsp[1][:, cs], z_tile[:, :], xsp[0][:, cs]),
                   incs=[("VE", 1)])
            return C["VE"]

        wl_state = {"idx": 0, "pe_done": {}}
        NGRP_W = 44

        def wload(src_ap, nk, sub):
            idx = wl_state["idx"]
            gidx = idx % NGRP_W
            par = idx % 2
            sem = "WL0" if par == 0 else "WL1"
            wssem = "WS0" if par == 0 else "WS1"
            waits = []
            if idx >= 2:
                waits.append(("PE", wl_state["pe_done"].get(idx - 2, 0)))
                if idx - 2 < NGRP_W:
                    waits.append((wssem, 16 * (idx // 2)))
            if idx < NGRP_W:
                op("gpsimd", lambda e: e.dma_start(
                    out=wbuf[par].ap().rearrange("p (k j) -> p k j", k=nk)[:, :, :sub],
                    in_=src_ap.rearrange("k p j -> p k j")),
                   waits=waits, incs=[(sem, 16)], drain=False)
                op("sync", lambda e: e.dma_start(
                    out=wsc[gidx], in_=wbuf[par][:, :]),
                   waits=[(sem, 16 * (idx // 2 + 1))], incs=[(wssem, 16)])
            else:
                gs = "WS0" if gidx % 2 == 0 else "WS1"
                w2 = (gs, 16 * (gidx // 2 + 1))
                if w2 not in waits and not any(s == gs and v >= w2[1] for s, v in waits):
                    waits.append(w2)
                op("gpsimd", lambda e: e.dma_start(
                    out=wbuf[par][:, :], in_=wsc[gidx]),
                   waits=waits, incs=[(sem, 16)], drain=False)
            wl_state["idx"] += 1
            wl_state["last_idx"] = idx
            return par, (sem, 16 * (idx // 2 + 1))

        def ffn(ve_split, in1_tile, out_tile, st_out_guard, xsp, inject=None):
            inj = inject or {}

            def fire(k):
                for fn in inj.get(k, ()):
                    fn()
            for og in range(8):
                par, wlv = wload(w12t[og], NKD, 1024)
                widx = wl_state["last_idx"]
                for ocp in range(4):
                    oc = og * 4 + ocp
                    bA, wA = get_bank()
                    if ocp == 0:
                        wA += [wlv, ("VE", ve_split)]
                    for kc in range(NKD):
                        for p in range(2):
                            last = (kc == NKD - 1 and p == 1)
                            op("tensor", lambda e, kc=kc, p=p, b=bA, ocp=ocp, par=par: e.matmul(
                                pb[b][:, :TT],
                                wbuf[par][:, kc * 1024 + ocp * 128: kc * 1024 + ocp * 128 + 128],
                                xsp[p][:, kc * TT:(kc + 1) * TT],
                                start=(kc == 0 and p == 0), stop=last),
                               waits=(wA if (kc == 0 and p == 0) else ()),
                               incs=([("PE", 1)] if last else ()))
                    peA = C["PE"]
                    bB, wB = get_bank()
                    for kc in range(NKD):
                        for p in range(2):
                            last = (kc == NKD - 1 and p == 1)
                            op("tensor", lambda e, kc=kc, p=p, b=bB, ocp=ocp, par=par: e.matmul(
                                pb[b][:, :TT],
                                wbuf[par][:, kc * 1024 + 512 + ocp * 128: kc * 1024 + 512 + ocp * 128 + 128],
                                xsp[p][:, kc * TT:(kc + 1) * TT],
                                start=(kc == 0 and p == 0), stop=last),
                               waits=(wB if (kc == 0 and p == 0) else ()),
                               incs=([("PE", 1)] if last else ()))
                    peB = C["PE"]
                    op("scalar", lambda e, b=bA, oc=oc: e.activation(
                        silu_t[:, :], pb[b][:, :TT], AF.Silu, scale=s_w1[:, oc:oc + 1]),
                       waits=[("PE", peA), ("VE", S["ve_silu"])], incs=[("AC", 1)])
                    rel_bank(bA, "AC")
                    ac_silu = C["AC"]
                    ht = hf_t[oc % 2]
                    op("vector", lambda e, b=bB, oc=oc, ht=ht: e.scalar_tensor_tensor(
                        ht[:, :], pb[b][:, :TT], s_w2[:, oc:oc + 1], silu_t[:, :],
                        ALU.mult, ALU.mult),
                       waits=[("PE", peB), ("AC", ac_silu)], incs=[("VE", 1)])
                    S["ve_silu"] = C["VE"]
                    rel_bank(bB, "VE")
                    if ocp == 3:
                        wl_state["pe_done"][widx] = peB
                    cs = slice(oc * TT, (oc + 1) * TT)
                    op("vector", lambda e, cs=cs, ht=ht: e.tensor_copy(hspA[0][:, cs], ht[:, :]),
                       incs=[("VE", 1)])
                    op("vector", lambda e, cs=cs, ht=ht: e.tensor_sub(hspA[1][:, cs], ht[:, :], hspA[0][:, cs]),
                       incs=[("VE", 1)])
                    S["ve_hf"][oc % 2] = C["VE"]
            ve_hA = C["VE"]
            pe_stageA = C["PE"]
            fire("A_end")

            def hidden_stage(wt_ap, s_t, src_sp, dst_sp, ve_src, mid, end):
                for og in range(16):
                    if og == 8:
                        fire(mid)
                    par, wlv = wload(wt_ap[og], NOH, 256)
                    widx = wl_state["last_idx"]
                    for ocj in range(2):
                        oc = og * 2 + ocj
                        bk, wk_ = get_bank()
                        if ocj == 0:
                            wk_ += [wlv, ("VE", ve_src)]
                        for kc in range(NOH):
                            for p in range(2):
                                last = (kc == NOH - 1 and p == 1)
                                op("tensor", lambda e, kc=kc, p=p, b=bk, ocj=ocj, par=par, sp=src_sp: e.matmul(
                                    pb[b][:, :TT],
                                    wbuf[par][:, kc * 256 + ocj * 128: kc * 256 + ocj * 128 + 128],
                                    sp[p][:, kc * TT:(kc + 1) * TT],
                                    start=(kc == 0 and p == 0), stop=last),
                                   waits=(wk_ if (kc == 0 and p == 0) else ()),
                                   incs=([("PE", 1)] if last else ()))
                        peg = C["PE"]
                        ht = hf_t[oc % 2]
                        op("scalar", lambda e, b=bk, oc=oc, ht=ht, s_t=s_t: e.activation(
                            ht[:, :], pb[b][:, :TT], AF.Gelu, scale=s_t[:, oc:oc + 1]),
                           waits=[("PE", peg), ("VE", S["ve_hf"][oc % 2])], incs=[("AC", 1)])
                        rel_bank(bk, "AC")
                        ac_g = C["AC"]
                        cs = slice(oc * TT, (oc + 1) * TT)
                        op("vector", lambda e, cs=cs, ht=ht, sp=dst_sp: e.tensor_copy(sp[0][:, cs], ht[:, :]),
                           waits=[("AC", ac_g)], incs=[("VE", 1)])
                        op("vector", lambda e, cs=cs, ht=ht, sp=dst_sp: e.tensor_sub(sp[1][:, cs], ht[:, :], sp[0][:, cs]),
                           incs=[("VE", 1)])
                        S["ve_hf"][oc % 2] = C["VE"]
                        if ocj == 1:
                            wl_state["pe_done"][widx] = peg
                fire(end)
                return C["VE"]

            ve_hB = hidden_stage(h1t, s_h1, hspA, hspB, ve_hA, "B_mid", "B_end")
            ve_hC = hidden_stage(h2t, s_h2, hspB, hspA, ve_hB, "C_mid", "C_end")
            ffn.pe_stageA = pe_stageA

            for og in range(4):
                par, wlv = wload(w3t[og], NOH, 256)
                widx = wl_state["last_idx"]
                for ocj in range(2):
                    oc = og * 2 + ocj
                    bk, wk_ = get_bank()
                    if ocj == 0:
                        wk_ += [wlv, ("VE", ve_hC)]
                    for kc in range(NOH):
                        for p in range(2):
                            last = (kc == NOH - 1 and p == 1)
                            op("tensor", lambda e, kc=kc, p=p, b=bk, ocj=ocj, par=par: e.matmul(
                                pb[b][:, :TT],
                                wbuf[par][:, kc * 256 + ocj * 128: kc * 256 + ocj * 128 + 128],
                                hspA[p][:, kc * TT:(kc + 1) * TT],
                                start=(kc == 0 and p == 0), stop=last),
                               waits=(wk_ if (kc == 0 and p == 0) else ()),
                               incs=([("PE", 1)] if last else ()))
                    peg = C["PE"]
                    cs = slice(oc * TT, (oc + 1) * TT)
                    w = [("PE", peg)]
                    if oc == 0:
                        w.append(st_out_guard())
                    op("vector", lambda e, b=bk, oc=oc, cs=cs: e.scalar_tensor_tensor(
                        out_tile[:, cs], pb[b][:, :TT], s_w3h[:, oc:oc + 1], in1_tile[:, cs],
                        ALU.mult, ALU.add),
                       waits=w, incs=[("VE", 1)])
                    rel_bank(bk, "VE")
                    if ocj == 1:
                        wl_state["pe_done"][widx] = peg
            return C["VE"]

        stxa_full = [16 * max(1, NT // 2)]
        ve_split_t = {}
        pe_xsp_par = [0, 0]
        ve_x_par = [0, 0]
        pend1 = {}
        rb1 = {}

        def p1_load_stats(t):
            par = t % 2
            cols = slice(t * TT, (t + 1) * TT)
            xt = x_tile2[par]
            sem = "LDX0" if par == 0 else "LDX1"
            op("sync", lambda e, xt=xt, cols=cols: e.dma_start(
                out=xt.ap().rearrange("p (k c) -> p k c", k=NKD),
                in_=x0T[:, cols].rearrange("(k p) c -> p k c", p=128)),
               waits=[("VE", ve_x_par[par])], incs=[(sem, 16)])
            rb1[t] = ln_stats(xt, [(sem, C[sem])])

        def p1_splits(t):
            par = t % 2
            ve_split_t[t] = ln_split(x_tile2[par], rb1.pop(t), xsp2[par], pe_xsp_par[par])

        def p1_lnb(t):
            r_tile = r2b2[0]
            ve_r = pend1[t][0]
            ve_rb2 = ln_stats(r_tile, [("VE", ve_r)])
            for kc in range(NKD):
                cs = slice(kc * TT, (kc + 1) * TT)
                w = []
                if kc == 0:
                    w = [("VE", ve_rb2), S["st_xlnb"]]
                op("vector", lambda e, cs=cs, r_tile=r_tile: e.tensor_mul(z_tile[:, :], r_tile[:, cs], Rb[:, :]),
                   waits=w, incs=[("VE", 1)])
                op("vector", lambda e, cs=cs: e.tensor_sub(xlnb_tile[:, cs], z_tile[:, :], Ab[:, :]),
                   incs=[("VE", 1)])
            pend1[t] = (ve_r, C["VE"])

        def p1_stores(t):
            cols = slice(t * TT, (t + 1) * TT)
            ve_r, ve_xlnb = pend1.pop(t)
            op("sync", lambda e, cols=cols: e.dma_start(
                out=resid[:, cols].rearrange("(k p) c -> p k c", p=128),
                in_=r2b2[0].ap().rearrange("p (k c) -> p k c", k=NKD)),
               waits=[("VE", ve_r)], incs=[("STR", 16)])
            S["st_r"] = C["STR"]
            half = (t * TT) // HC
            colh = (t * TT) % HC
            sxs = "STXA" if half == 0 else "STXB"
            op("sync", lambda e, half=half, colh=colh: e.dma_start(
                out=xg_in[half][:, colh:colh + TT].rearrange("(k p) c -> p k c", p=128),
                in_=xlnb_tile.ap().rearrange("p (k c) -> p k c", k=NKD)),
               waits=[("VE", ve_xlnb)], incs=[(sxs, 16)])
            S["st_xlnb"] = (sxs, C[sxs])

        p1_load_stats(0)
        p1_splits(0)
        for t in range(NT):
            par = t % 2
            inj = {}
            if t + 1 < NT:
                inj["A_end"] = [lambda tt=t + 1: p1_load_stats(tt)]
                inj["B_end"] = [lambda tt=t + 1: p1_splits(tt)]
            if t - 1 in pend1:
                inj.setdefault("A_end", []).append(lambda tt=t - 1: p1_stores(tt))
            pend1[t] = (0, 0)
            ve_r = ffn(ve_split_t.pop(t), x_tile2[par], r2b2[0],
                       lambda: ("STR", S["st_r"]), xsp2[par], inject=inj)
            pe_xsp_par[par] = ffn.pe_stageA
            ve_x_par[par] = ve_r
            pend1[t] = (ve_r, None)
            p1_lnb(t)
            if t == NT // 2 and NT > 1:
                op("gpsimd", lambda e: e.collective_compute(
                    "AllGather", ALU.bypass, replica_groups=REPL_GROUPS,
                    ins=[xg_in[0]], outs=[xg_out[0]]),
                   waits=[("STXA", stxa_full[0])], incs=[("CC", 1)], drain=False)
        p1_stores(NT - 1)
        if NT <= 1:
            op("gpsimd", lambda e: e.collective_compute(
                "AllGather", ALU.bypass, replica_groups=REPL_GROUPS,
                ins=[xg_in[0]], outs=[xg_out[0]]),
               waits=[("STXA", stxa_full[0])], incs=[("CC", 1)], drain=False)
        op("gpsimd", lambda e: e.collective_compute(
            "AllGather", ALU.bypass, replica_groups=REPL_GROUPS,
            ins=[xg_in[1]], outs=[xg_out[1]]),
           waits=[("STXB", C["STXB"])], incs=[("CC", 1)], drain=False)
        cc1 = C["CC"]

        f_pe, f_ac, f_ve = C["PE"], C["AC"], C["VE"]
        aw = [("PE", f_pe), ("AC", f_ac), ("VE", f_ve)]
        for dst, src in [(wq_sb, wq_i), (wk_sb, wk_i), (wv_sb, wv_i)]:
            op("sync", lambda e, d=dst, s=src: e.dma_start(
                out=d.ap().rearrange("p (k j) -> p k j", k=NKD),
                in_=s.rearrange("(k p) j -> p k j", p=128)),
               waits=aw, incs=[("LDA", 16)])
            aw = []
        op("sync", lambda e: e.dma_start(
            out=wb_sb.ap().rearrange("p (k j) -> p k j", j=1),
            in_=wb_i.rearrange("(k p) j -> p k j", p=128),
            allow_slow_non_contiguous=True),
           incs=[("LDA", 16)])
        ld_attnw = C["LDA"]
        op("vector", lambda e: e.memset(S_sb[:, :], 0.0),
           waits=[("PE", f_pe), ("AC", f_ac)], incs=[("VE", 1)])
        op("vector", lambda e: e.memset(utail[:, :], 0.0), incs=[("VE", 1)])
        ve_supd = C["VE"]

        NPIECE = max(1, GT // 512)
        PW = GT // NPIECE
        for g in range(NG):
            gt0 = g * GT
            blk = gt0 // TCORE
            col0 = gt0 % TCORE
            hb = col0 // HC
            colh = col0 % HC
            cc_need = 1 if hb == 0 else 2
            ld_xg = []
            for kc in range(NKD):
                w = [("PE", S["pe_xg"]), ("CC", cc_need)] if kc == 0 else []
                op("sync", lambda e, kc=kc, hb=hb, colh=colh: e.dma_start(
                    out=xg_sb[:, kc * GT:(kc + 1) * GT],
                    in_=xg_out[hb][blk * D + kc * 128: blk * D + kc * 128 + 128,
                                   colh:colh + GT]),
                   waits=w, incs=[("LDG", 16)])
                ld_xg.append(C["LDG"])

            for j in range(GCH):
                bk, wk_ = get_bank()
                for kc in range(NKD):
                    w = (wk_ + [("LDG", ld_xg[-1]), ("LDA", ld_attnw)]) if kc == 0 else []
                    op("tensor", lambda e, kc=kc, j=j, b=bk: e.matmul(
                        pb[b][:, 0:1],
                        xg_sb[:, kc * GT + j * CH: kc * GT + j * CH + 128],
                        wb_sb[:, kc:kc + 1],
                        start=(kc == 0), stop=(kc == NKD - 1)),
                       waits=w, incs=([("PE", 1)] if kc == NKD - 1 else ()))
                peb = C["PE"]
                op("scalar", lambda e, b=bk: e.activation(
                    bh_sb[:, :], pb[b][:, 0:1], AF.Tanh, scale=0.5),
                   waits=[("PE", peb), ("VE", S.get("ve_bh", 0))], incs=[("AC", 1)])
                rel_bank(bk, "AC")
                acb = C["AC"]
                op("vector", lambda e, j=j: e.tensor_scalar(
                    beta_sb[:, j:j + 1], bh_sb[:, :], 0.5, 0.5, ALU.mult, ALU.add),
                   waits=[("AC", acb)], incs=[("VE", 1)])
                S["ve_bh"] = C["VE"]
            ve_beta = C["VE"]

            for wi, (wsb, csb, dst) in enumerate([(wq_sb, c_q, qT_sb),
                                                  (wk_sb, c_k, kT_sb),
                                                  (wv_sb, c_v, vT_sb)]):
                for c2 in range(2):
                    op("vector", lambda e, c2=c2, wi=wi: e.tensor_copy(
                        u_sb[:, c2 * (GT + 3): c2 * (GT + 3) + 3],
                        utail[:, (wi * 2 + c2) * 3: (wi * 2 + c2) * 3 + 3]),
                       incs=[("VE", 1)])
                for c2 in range(2):
                    for piece in range(NPIECE):
                        bk, wk_ = get_bank()
                        w = wk_ + [("LDG", ld_xg[-1]), ("LDA", ld_attnw)]
                        for kc in range(NKD):
                            op("tensor", lambda e, kc=kc, c2=c2, piece=piece, b=bk, wsb=wsb: e.matmul(
                                pb[b][:, :PW],
                                wsb[:, kc * 256 + c2 * 128: kc * 256 + c2 * 128 + 128],
                                xg_sb[:, kc * GT + piece * PW: kc * GT + piece * PW + PW],
                                start=(kc == 0), stop=(kc == NKD - 1)),
                               waits=(w if kc == 0 else ()),
                               incs=([("PE", 1)] if kc == NKD - 1 else ()))
                        pep = C["PE"]
                        op("vector", lambda e, c2=c2, piece=piece, b=bk: e.tensor_copy(
                            u_sb[:, c2 * (GT + 3) + 3 + piece * PW: c2 * (GT + 3) + 3 + piece * PW + PW],
                            pb[b][:, :PW]),
                           waits=[("PE", pep), ("AC", S.get("ac_u", 0))], incs=[("VE", 1)])
                        rel_bank(bk, "VE")
                for c2 in range(2):
                    op("vector", lambda e, c2=c2, wi=wi: e.tensor_copy(
                        utail[:, (wi * 2 + c2) * 3: (wi * 2 + c2) * 3 + 3],
                        u_sb[:, c2 * (GT + 3) + GT: c2 * (GT + 3) + GT + 3]),
                       incs=[("VE", 1)])
                for c2 in range(2):
                    ub = c2 * (GT + 3)
                    dcol = c2 * GT
                    w = [("PE", S["pe_scan_q"])] if (c2 == 0) else []
                    op("vector", lambda e, ub=ub, dcol=dcol, csb=csb, dst=dst, c2=c2: e.tensor_scalar_mul(
                        dst[:, dcol:dcol + GT], u_sb[:, ub:ub + GT], csb[:, c2 * 4:c2 * 4 + 1]),
                       waits=w, incs=[("VE", 1)])
                    for tap in range(1, 4):
                        op("vector", lambda e, ub=ub, dcol=dcol, csb=csb, dst=dst, c2=c2, tap=tap: e.scalar_tensor_tensor(
                            dst[:, dcol:dcol + GT], u_sb[:, ub + tap:ub + tap + GT],
                            csb[:, c2 * 4 + tap:c2 * 4 + tap + 1], dst[:, dcol:dcol + GT],
                            ALU.mult, ALU.add),
                           incs=[("VE", 1)])
                ve_conv = C["VE"]
                for c2 in range(2):
                    op("scalar", lambda e, c2=c2, dst=dst: e.activation(
                        dst[:, c2 * GT:(c2 + 1) * GT], dst[:, c2 * GT:(c2 + 1) * GT], AF.Silu),
                       waits=([("VE", ve_conv)] if c2 == 0 else ()), incs=[("AC", 1)])
            ac_silu_all = C["AC"]

            for wi, dst in [(0, qT_sb), (1, kT_sb)]:
                for c2 in range(2):
                    w = [("PE", S["pe_l2"])]
                    if wi == 0 and c2 == 0:
                        w.append(("AC", ac_silu_all))
                    op("scalar", lambda e, c2=c2, dst=dst: e.activation(
                        u_sb[:, c2 * (GT + 3) + 3: c2 * (GT + 3) + 3 + GT],
                        dst[:, c2 * GT:(c2 + 1) * GT], AF.Square),
                       waits=w, incs=[("AC", 1)])
                ac_sq2 = C["AC"]
                S["ac_u"] = ac_sq2
                for piece in range(NPIECE):
                    bk, wk_ = get_bank_slow([("AC", ac_sq2)])
                    for c2 in range(2):
                        op("tensor", lambda e, c2=c2, piece=piece, b=bk: e.matmul(
                            pb[b][0:1, :PW], ones_col[:, :],
                            u_sb[:, c2 * (GT + 3) + 3 + piece * PW: c2 * (GT + 3) + 3 + piece * PW + PW],
                            start=(c2 == 0), stop=(c2 == 1)),
                           waits=(wk_ if c2 == 0 else ()),
                           incs=([("PE", 1)] if c2 == 1 else ()))
                    pel = C["PE"]
                    S["pe_l2"] = pel
                    op("scalar", lambda e, b=bk: e.activation(
                        row_l2[0:1, :PW], pb[b][0:1, :PW], AF.Sqrt, bias=EPS_L2),
                       waits=[("PE", pel), ("VE", S["ve_wo_rd"])], incs=[("AC", 1)])
                    rel_bank(bk, "AC")
                    acl = C["AC"]
                    op("vector", lambda e: e.reciprocal(row_l2b[0:1, :PW], row_l2[0:1, :PW]),
                       waits=[("AC", acl)], incs=[("VE", 1)])
                    vel = C["VE"]
                    S["ve_wo_rd"] = vel
                    b2, w2_ = get_bank_slow([("VE", vel)])
                    op("tensor", lambda e, b=b2: e.matmul(
                        pb[b][:, :PW], ones_row[:, :], row_l2b[0:1, :PW],
                        start=True, stop=True),
                       waits=w2_, incs=[("PE", 1)])
                    pebc = C["PE"]
                    for c2 in range(2):
                        op("vector", lambda e, c2=c2, piece=piece, b=b2, dst=dst: e.tensor_mul(
                            dst[:, c2 * GT + piece * PW: c2 * GT + piece * PW + PW],
                            dst[:, c2 * GT + piece * PW: c2 * GT + piece * PW + PW],
                            pb[b][:, :PW]),
                           waits=([("PE", pebc)] if c2 == 0 else ()), incs=[("VE", 1)])
                    rel_bank(b2, "VE")
            ve_qk = C["VE"]

            SKEW = 2

            def indep_gen(j):
                lane = j % 2
                slot = j % NSLOT
                jb = j * CH
                bcol = beta_sb[:, j:j + 1]
                Ppp = P_pp[lane]
                Tpp = T_pp[lane]
                Bkn_t = Bkn_t2[lane]

                def qs(c2):
                    return qT_sb[:, c2 * GT + jb: c2 * GT + jb + 128]

                def ks(c2):
                    return kT_sb[:, c2 * GT + jb: c2 * GT + jb + 128]

                def vs(c2):
                    return vT_sb[:, c2 * GT + jb: c2 * GT + jb + 128]

                slot_guard = [("PE", S["tail_pe"][slot])]
                for c2 in range(2):
                    bk, wk_ = get_bank([("VE", ve_qk), ("AC", ac_silu_all)])
                    op("tensor", lambda e, b=bk, c2=c2: e.transpose(
                        pb[b][:, :128], vs(c2), ident[:, :]),
                       waits=wk_, incs=[("PE", 1)])
                    pet = C["PE"]
                    op("vector", lambda e, b=bk, c2=c2, slot=slot: e.tensor_scalar_mul(
                        Bv_a[slot][:, c2 * 128:(c2 + 1) * 128], pb[b][:, :128], bcol),
                       waits=[("PE", pet), ("VE", ve_beta)] + (slot_guard if c2 == 0 else []),
                       incs=[("VE", 1)])
                    rel_bank(bk, "VE")
                yield
                for c2 in range(2):
                    bk, wk_ = get_bank()
                    op("tensor", lambda e, b=bk, c2=c2: e.transpose(
                        pb[b][:, :128], ks(c2), ident[:, :]),
                       waits=wk_, incs=[("PE", 1)])
                    pet = C["PE"]
                    op("vector", lambda e, b=bk, c2=c2, slot=slot: e.tensor_copy(
                        Ktm_a[slot][:, c2 * 128:(c2 + 1) * 128], pb[b][:, :128]),
                       waits=[("PE", pet)], incs=[("VE", 1)])
                    rel_bank(bk, "VE")
                yield
                bk, wk_ = get_bank()
                for c2 in range(2):
                    op("tensor", lambda e, b=bk, c2=c2: e.matmul(
                        pb[b][:, :128], ks(c2), ks(c2), start=(c2 == 0), stop=(c2 == 1)),
                       waits=(wk_ if c2 == 0 else ()),
                       incs=([("PE", 1)] if c2 == 1 else ()))
                pea = C["PE"]
                op("vector", lambda e, b=bk: e.scalar_tensor_tensor(
                    Ppp[0][:, :], pb[b][:, :128], bcol, m_trilneg[:, :], ALU.mult, ALU.mult),
                   waits=[("PE", pea), ("VE", ve_beta), ("PE", S["pe_chain_prev"][lane])],
                   incs=[("VE", 1)])
                rel_bank(bk, "VE")
                ve_P = C["VE"]
                yield
                bk, wk_ = get_bank([("VE", ve_P)])
                op("tensor", lambda e, b=bk: e.transpose(pb[b][:, :128], Ppp[0][:, :], ident[:, :]),
                   waits=wk_, incs=[("PE", 1)])
                pet0 = C["PE"]
                op("vector", lambda e, b=bk: e.tensor_copy(Tpp[0][:, :], pb[b][:, :128]),
                   waits=[("PE", pet0)], incs=[("VE", 1)])
                rel_bank(bk, "VE")
                ve_T = C["VE"]
                op("vector", lambda e, slot=slot: e.tensor_add(
                    X_a[slot][:, :], Tpp[0][:, :], ident[:, :]), incs=[("VE", 1)])
                ve_X = C["VE"]
                yield
                for l in range(1, 7):
                    pprev, pcur = Ppp[(l - 1) % 2], Ppp[l % 2]
                    tprev, tcur = Tpp[(l - 1) % 2], Tpp[l % 2]
                    bk, wk_ = get_bank([("VE", ve_T), ("VE", ve_P)])
                    op("tensor", lambda e, b=bk, tp=tprev, pp=pprev: e.matmul(
                        pb[b][:, :128], tp[:, :], pp[:, :], start=True, stop=True),
                       waits=wk_, incs=[("PE", 1)])
                    pe1 = C["PE"]
                    bk2, wk2_ = get_bank()
                    op("tensor", lambda e, b=bk2, tp=tprev, pp=pprev: e.matmul(
                        pb[b][:, :128], pp[:, :], tp[:, :], start=True, stop=True),
                       waits=wk2_, incs=[("PE", 1)])
                    pe2 = C["PE"]
                    op("vector", lambda e, b=bk, pc=pcur: e.tensor_copy(pc[:, :], pb[b][:, :128]),
                       waits=[("PE", pe1)], incs=[("VE", 1)])
                    rel_bank(bk, "VE")
                    ve_P = C["VE"]
                    op("vector", lambda e, b=bk2, tc=tcur: e.tensor_copy(tc[:, :], pb[b][:, :128]),
                       waits=[("PE", pe2)], incs=[("VE", 1)])
                    rel_bank(bk2, "VE")
                    ve_T = C["VE"]
                    bk3, wk3_ = get_bank([("VE", ve_P), ("VE", ve_X)])
                    op("tensor", lambda e, b=bk3, pc=pcur, slot=slot: e.matmul(
                        pb[b][:, :128], pc[:, :], X_a[slot][:, :], start=True, stop=True),
                       waits=wk3_, incs=[("PE", 1)])
                    pex = C["PE"]
                    op("vector", lambda e, b=bk3, slot=slot: e.tensor_add(
                        X_a[slot][:, :], X_a[slot][:, :], pb[b][:, :128]),
                       waits=[("PE", pex)], incs=[("VE", 1)])
                    rel_bank(bk3, "VE")
                    ve_X = C["VE"]
                    yield
                S["pe_chain_prev"][lane] = C["PE"]
                S["ve_mt"][slot] = ve_X
                op("vector", lambda e, slot=slot, Bkn_t=Bkn_t: e.tensor_scalar(
                    Bkn_t[:, :], Ktm_a[slot][:, :], bcol, -1.0, ALU.mult, ALU.mult),
                   waits=[("PE", S["pe_bkn_prev"][lane])], incs=[("VE", 1)])
                ve_bkn = C["VE"]
                for c2 in range(2):
                    bk, wk_ = get_bank([("VE", ve_bkn), ("VE", ve_X)])
                    op("tensor", lambda e, b=bk, c2=c2, slot=slot, Bkn_t=Bkn_t: e.matmul(
                        pb[b][:, :128], Bkn_t[:, c2 * 128:(c2 + 1) * 128], X_a[slot][:, :],
                        start=True, stop=True),
                       waits=wk_, incs=[("PE", 1)])
                    pew = C["PE"]
                    if c2 == 1:
                        S["pe_bkn_prev"][lane] = pew
                    op("vector", lambda e, b=bk, c2=c2, slot=slot: e.tensor_copy(
                        W1Tn_a[slot][:, c2 * 128:(c2 + 1) * 128], pb[b][:, :128]),
                       waits=[("PE", pew)], incs=[("VE", 1)])
                    rel_bank(bk, "VE")
                yield
                bk, wk_ = get_bank()
                for c2 in range(2):
                    op("tensor", lambda e, b=bk, c2=c2: e.matmul(
                        pb[b][:, :128], ks(c2), qs(c2), start=(c2 == 0), stop=(c2 == 1)),
                       waits=(wk_ if c2 == 0 else ()),
                       incs=([("PE", 1)] if c2 == 1 else ()))
                pelt = C["PE"]
                op("vector", lambda e, b=bk, slot=slot: e.tensor_mul(
                    LT_a[slot][:, :], pb[b][:, :128], m_triu[:, :]),
                   waits=[("PE", pelt)], incs=[("VE", 1)])
                rel_bank(bk, "VE")
                S["ve_indep"][slot] = C["VE"]

            def run_indep_pair(ja, jb2):
                gens = [indep_gen(ja)]
                if jb2 is not None:
                    gens.append(indep_gen(jb2))
                done = [False] * len(gens)
                while not all(done):
                    for i, g in enumerate(gens):
                        if not done[i]:
                            try:
                                next(g)
                            except StopIteration:
                                done[i] = True

            def tail(j):
                slot = j % NSLOT
                jb = j * CH
                nonlocal ve_supd

                def qs(c2):
                    return qT_sb[:, c2 * GT + jb: c2 * GT + jb + 128]

                bk, wk_ = get_bank([("VE", S["ve_indep"][slot]), ("VE", ve_supd),
                                    ("PE", S["pe_u_prev"])])
                op("tensor", lambda e, b=bk, slot=slot: e.matmul(
                    pb[b][:, :256], X_a[slot][:, :], Bv_a[slot][:, :],
                    start=True, stop=False),
                   waits=wk_)
                for c2 in range(2):
                    op("tensor", lambda e, b=bk, c2=c2, slot=slot: e.matmul(
                        pb[b][:, :256], W1Tn_a[slot][:, c2 * 128:(c2 + 1) * 128],
                        S_sb[:, c2 * 256:(c2 + 1) * 256],
                        start=False, stop=(c2 == 1)),
                       incs=([("PE", 1)] if c2 == 1 else ()))
                peu = C["PE"]
                op("vector", lambda e, b=bk: e.tensor_copy(U_sb[:, :], pb[b][:, :256]),
                   waits=[("PE", peu)], incs=[("VE", 1)])
                rel_bank(bk, "VE")
                ve_U = C["VE"]
                bo, wo_ = get_bank()
                for c2 in range(2):
                    op("tensor", lambda e, b=bo, c2=c2: e.matmul(
                        pb[b][:, :256], qs(c2), S_sb[:, c2 * 256:(c2 + 1) * 256],
                        start=(c2 == 0), stop=False),
                       waits=(wo_ if c2 == 0 else ()))
                op("tensor", lambda e, b=bo, slot=slot: e.matmul(
                    pb[b][:, :256], LT_a[slot][:, :], U_sb[:, :], start=False, stop=True),
                   waits=[("VE", ve_U)], incs=[("PE", 1)])
                peo = C["PE"]
                for c2 in range(2):
                    bk, wk_ = get_bank()
                    op("tensor", lambda e, b=bk, c2=c2, slot=slot: e.matmul(
                        pb[b][:, :256], Ktm_a[slot][:, c2 * 128:(c2 + 1) * 128], U_sb[:, :],
                        start=True, stop=True),
                       waits=wk_, incs=[("PE", 1)])
                    pes = C["PE"]
                    if c2 == 1:
                        S["pe_u_prev"] = pes
                        S["tail_pe"][slot] = pes
                    op("vector", lambda e, b=bk, c2=c2: e.tensor_add(
                        S_sb[:, c2 * 256:(c2 + 1) * 256], S_sb[:, c2 * 256:(c2 + 1) * 256],
                        pb[b][:, :256]),
                       waits=[("PE", pes)], incs=[("VE", 1)])
                    rel_bank(bk, "VE")
                ve_supd = C["VE"]
                op("vector", lambda e, b=bo: e.tensor_copy(o_sb[:, :], pb[b][:, :256]),
                   waits=[("PE", peo), ("AC", S["ac_osq"])], incs=[("VE", 1)])
                rel_bank(bo, "VE")
                ve_o = C["VE"]
                op("scalar", lambda e: e.activation(on_sb[:, :], o_sb[:, :], AF.Square,
                                                    accum_out=ossq[:, :]),
                   waits=[("VE", ve_o), ("VE", S["ve_orr"])], incs=[("AC", 1)])
                op("scalar", lambda e: e.activation(orr[:, :], ossq[:, :], AF.Sqrt,
                                                    scale=1.0 / DH, bias=EPS_RMS),
                   incs=[("AC", 1)])
                ac_or = C["AC"]
                S["ac_osq"] = ac_or
                op("vector", lambda e: e.reciprocal(orinv[:, :], orr[:, :]),
                   waits=[("AC", ac_or)], incs=[("VE", 1)])
                S["ve_orr"] = C["VE"]
                op("vector", lambda e: e.tensor_scalar_mul(on_sb[:, :], o_sb[:, :], orinv[:, :]),
                   incs=[("VE", 1)])
                ve_on = C["VE"]
                for c2 in range(2):
                    bk, wk_ = get_bank([("VE", ve_on)])
                    op("tensor", lambda e, b=bk, c2=c2: e.transpose(
                        pb[b][:, :128], on_sb[:, c2 * 128:(c2 + 1) * 128], ident[:, :]),
                       waits=wk_, incs=[("PE", 1)])
                    peot = C["PE"]
                    w = [("PE", peot)]
                    if c2 == 0:
                        w.append(S["st_og"])
                    op("vector", lambda e, b=bk, c2=c2: e.tensor_scalar_mul(
                        oT_sb[:, c2 * 128:(c2 + 1) * 128], pb[b][:, :128], g_rms[:, c2:c2 + 1]),
                       waits=w, incs=[("VE", 1)])
                    rel_bank(bk, "VE")
                ve_ot = C["VE"]
                sc = gt0 + jb
                oh = sc // HS
                sco = sc % HS
                sos = "STOA" if oh == 0 else "STOB"
                op("sync", lambda e, oh=oh, sco=sco: e.dma_start(
                    out=og_in[oh][:, sco: sco + 128].rearrange("(c p) t -> p c t", p=128),
                    in_=oT_sb.ap().rearrange("p (c t) -> p c t", c=2)),
                   waits=[("VE", ve_ot)], incs=[(sos, 16)])
                S["st_og"] = (sos, C[sos])

            for jp in range(0, GCH, 2):
                run_indep_pair(jp, jp + 1 if jp + 1 < GCH else None)
                if jp >= SKEW:
                    tail(jp - 2)
                    tail(jp - 1)
            tail(GCH - 2)
            tail(GCH - 1)
            S["pe_scan_q"] = C["PE"]
            S["pe_xg"] = C["PE"]
            if g == max(0, NG // 2 - 1):
                op("gpsimd", lambda e: e.collective_compute(
                    "AllGather", ALU.bypass, replica_groups=REPL_GROUPS,
                    ins=[og_in[0]], outs=[og_out[0]]),
                   waits=[("STOA", C["STOA"])], incs=[("CC", 1)], drain=False)

        op("gpsimd", lambda e: e.collective_compute(
            "AllGather", ALU.bypass, replica_groups=REPL_GROUPS,
            ins=[og_in[1]], outs=[og_out[1]]),
           waits=[("STOB", C["STOB"])], incs=[("CC", 1)], drain=False)
        cc2 = C["CC"]

        f3_pe, f3_ac = C["PE"], C["AC"]
        if me == "sync":
            pid = nc.partition_id(engines=[mybir.EngineType.SP])
            slice_reg = eng_handle.to_reg(pid % 4)

        first3 = [[("PE", f3_pe), ("AC", f3_ac)]]
        pe_y_kc = {}
        st3 = {"ve_ogyp": [0, 0], "ve_r2b": [0, 0], "ve_xln": 0, "ve_x3": 0,
               "pe_og": [0, 0]}
        ve_split3 = {}
        pend3 = {}
        pst3 = {}

        def q_loads_y(t):
            par = t % 2
            cols = slice(t * TT, (t + 1) * TT)
            ogt = og_yp2[par]
            C["LDO"] += 16
            if me == "sync":
                for s_, v_ in (first3[0] + [("PE", st3["pe_og"][par]), ("VE", st3["ve_ogyp"][par])]):
                    if v_ > 0:
                        eng_handle.wait_ge(sems[s_], v_)
                for sl in range(4):
                    with eng_handle.If_cmp(slice_reg, sl, "IS_EQ"):
                        c0 = sl * TCORE + t * TT
                        oh = c0 // HS
                        c0h = c0 % HS
                        eng_handle.wait_ge(sems["CC"], cc2 - 1 if oh == 0 else cc2)
                        eng_handle.dma_start(
                            out=ogt.ap().rearrange("p (k c) -> p k c", k=NKD),
                            in_=og_out[oh][:, c0h:c0h + TT].rearrange("(k p) c -> p k c", p=128),
                        ).then_inc(sems["LDO"], 16)
                eng_handle.end_ifs()
            first3[0] = []
            ld_og = C["LDO"]
            halfL = (t * TT) // HC
            colL = (t * TT) % HC
            op("sync", lambda e, halfL=halfL, colL=colL: e.dma_start(
                out=xlnb_tile.ap().rearrange("p (k c) -> p k c", k=NKD),
                in_=xg_in[halfL][:, colL:colL + TT].rearrange("(k p) c -> p k c", p=128)),
               waits=[("VE", st3["ve_xln"])], incs=[("LDL", 16)])
            ld_xln = C["LDL"]
            op("sync", lambda e, cols=cols: e.dma_start(
                out=x_tile2[0].ap().rearrange("p (k c) -> p k c", k=NKD),
                in_=resid[:, cols].rearrange("(k p) c -> p k c", p=128)),
               waits=[("VE", st3["ve_x3"]), ("STR", S["st_r"])], incs=[("LDR", 16)])
            ld_r1 = C["LDR"]
            ld_wo = {}

            def load_wo(kc):
                sem = "LDW0" if kc % 2 == 0 else "LDW1"
                key = (kc - 2) % NKD if pe_y_kc else None
                w = [("PE", pe_y_kc.get(key, 0))] if key is not None else []
                op("sync", lambda e, kc=kc: e.dma_start(
                    out=wo_buf[kc % 2][:, :], in_=wo_i[kc * 128:(kc + 1) * 128, :]),
                   waits=w, incs=[(sem, 16)])
                ld_wo[kc] = (sem, C[sem])

            load_wo(0)
            load_wo(1)
            ybanks = []
            for oc in range(NKD):
                bk, wk_ = get_bank() if oc < 6 else get_bank_slow()
                ybanks.append((bk, wk_))
            for kc in range(NKD):
                for oc in range(NKD):
                    bk, wk_ = ybanks[oc]
                    w = []
                    if kc == 0:
                        w = list(wk_) + [("LDO", ld_og)]
                    if oc == 0:
                        w = w + [ld_wo[kc]]
                    op("tensor", lambda e, kc=kc, oc=oc, b=bk, ogt=ogt: e.matmul(
                        pb[b][:, :TT], wo_buf[kc % 2][:, oc * 128:(oc + 1) * 128],
                        ogt[:, kc * TT:(kc + 1) * TT],
                        start=(kc == 0), stop=(kc == NKD - 1)),
                       waits=w,
                       incs=([("PE", 1)] if oc == NKD - 1 else ()))
                pe_y_kc[kc] = C["PE"]
                if kc + 2 < NKD:
                    load_wo(kc + 2)
            pe_y = C["PE"]
            st3["pe_og"][par] = pe_y
            for oc in range(NKD):
                bk, _ = ybanks[oc]
                cs = slice(oc * TT, (oc + 1) * TT)
                w = [("PE", pe_y)]
                if oc == 0:
                    w += [("LDL", ld_xln)]
                op("vector", lambda e, b=bk, cs=cs, ogt=ogt: e.tensor_add(
                    ogt[:, cs], pb[b][:, :TT], xlnb_tile[:, cs]),
                   waits=w, incs=[("VE", 1)])
                rel_bank(bk, "VE")
            pst3[t] = {"ve_yp": C["VE"], "ld_r1": ld_r1}
            st3["ve_xln"] = C["VE"]

        def q_stats1(t):
            par = t % 2
            pst3[t]["ve_rb3"] = ln_stats(og_yp2[par], [("VE", pst3[t]["ve_yp"])])

        def q_zchain(t):
            par = t % 2
            ogt = og_yp2[par]
            r2t = r2b2[par]
            for kc in range(NKD):
                cs = slice(kc * TT, (kc + 1) * TT)
                w = []
                if kc == 0:
                    w = [("VE", pst3[t]["ve_rb3"])]
                op("vector", lambda e, cs=cs, ogt=ogt: e.tensor_mul(z_tile[:, :], ogt[:, cs], Rb[:, :]),
                   waits=w, incs=[("VE", 1)])
                op("vector", lambda e: e.tensor_sub(z_tile[:, :], z_tile[:, :], Ab[:, :]),
                   incs=[("VE", 1)])
                ve_z = C["VE"]
                op("scalar", lambda e, kc=kc: e.activation(
                    e_tile[:, :], z_tile[:, :], AF.Identity,
                    bias=ln_b[:, kc:kc + 1], scale=ln_g[:, kc:kc + 1]),
                   waits=[("VE", ve_z)], incs=[("AC", 1)])
                ac_z = C["AC"]
                w = [("AC", ac_z)]
                if kc == 0:
                    w += [("LDR", pst3[t]["ld_r1"]), ("VE", st3["ve_r2b"][par])]
                op("vector", lambda e, cs=cs, r2t=r2t: e.tensor_add(
                    r2t[:, cs], e_tile[:, :], x_tile2[0][:, cs]),
                   waits=w, incs=[("VE", 1)])
            ve_r2 = C["VE"]
            pst3[t]["ve_r2"] = ve_r2
            st3["ve_ogyp"][par] = ve_r2
            st3["ve_x3"] = ve_r2

        def q_stats2(t):
            par = t % 2
            pst3[t]["ve_rb4"] = ln_stats(r2b2[par], [("VE", pst3[t]["ve_r2"])])

        def q_splits(t):
            par = t % 2
            ve_split3[t] = ln_split(r2b2[par], pst3[t]["ve_rb4"], xsp2[par], pe_xsp_par[par])
            pst3.pop(t)

        def p3_store(t):
            cols = slice(t * TT, (t + 1) * TT)
            ve_out = pend3.pop(t)
            op("sync", lambda e, cols=cols: e.dma_start(
                out=outT[:, cols].rearrange("(k p) c -> p k c", p=128),
                in_=out3.ap().rearrange("p (k c) -> p k c", k=NKD)),
               waits=[("VE", ve_out)], incs=[("STU", 16)])
            S["st_out"] = C["STU"]

        out3 = x_tile2[1]
        q_loads_y(0)
        q_stats1(0)
        q_zchain(0)
        q_stats2(0)
        q_splits(0)
        for t in range(NT):
            par = t % 2
            inj = {}
            if t + 1 < NT:
                inj["A_end"] = [lambda tt=t + 1: q_loads_y(tt)]
                inj["B_mid"] = [lambda tt=t + 1: q_stats1(tt)]
                inj["B_end"] = [lambda tt=t + 1: q_zchain(tt)]
                inj["C_mid"] = [lambda tt=t + 1: q_stats2(tt)]
                inj["C_end"] = [lambda tt=t + 1: q_splits(tt)]
            if t - 1 in pend3:
                inj.setdefault("A_end", []).append(lambda tt=t - 1: p3_store(tt))
            ve_out = ffn(ve_split3.pop(t), r2b2[par], out3,
                         lambda: ("STU", S["st_out"]), xsp2[par], inject=inj)
            pe_xsp_par[par] = ffn.pe_stageA
            st3["ve_r2b"][par] = ve_out
            pend3[t] = ve_out
        p3_store(NT - 1)

        waitonly("sync", [("STU", C["STU"]), ("STR", C["STR"]), ("STXA", C["STXA"]),
                          ("STXB", C["STXB"]), ("STOA", C["STOA"]), ("STOB", C["STOB"])])
        return C

    counts = {}
    with nc.Block() as block:
        def mk(name):
            def f(eng):
                counts[name] = emit(name, eng)
            return f
        block.sync(mk("sync"))
        block.gpsimd(mk("gpsimd"))
        block.tensor(mk("tensor"))
        block.vector(mk("vector"))
        block.scalar(mk("scalar"))

    base = counts["sync"]
    for k, c in counts.items():
        assert c == base, f"counter divergence in {k}"
    return nc


def _prep_shared(inp):
    f = np.float32
    w1T = np.ascontiguousarray(np.asarray(inp["L0_ffn_w1_w"], dtype=f).T)
    w2T = np.ascontiguousarray(np.asarray(inp["L0_ffn_w2_w"], dtype=f).T)
    h1T = np.ascontiguousarray(np.asarray(inp["L0_ffn_h1_w"], dtype=f).T)
    h2T = np.ascontiguousarray(np.asarray(inp["L0_ffn_h2_w"], dtype=f).T)
    w3T = np.ascontiguousarray(np.asarray(inp["L0_ffn_w3_w"], dtype=f).T)

    w12t = np.empty((8, NKD, 128, 1024), f)
    w12t[:, :, :, :512] = w1T.reshape(NKD, 128, 8, 512).transpose(2, 0, 1, 3)
    w12t[:, :, :, 512:] = w2T.reshape(NKD, 128, 8, 512).transpose(2, 0, 1, 3)
    h1t = np.ascontiguousarray(h1T.reshape(NOH, 128, 16, 256).transpose(2, 0, 1, 3))
    h2t = np.ascontiguousarray(h2T.reshape(NOH, 128, 16, 256).transpose(2, 0, 1, 3))
    w3t = np.ascontiguousarray(w3T.reshape(NOH, 128, 4, 256).transpose(2, 0, 1, 3))

    tile_s = lambda s, n: np.ascontiguousarray(np.asarray(s, dtype=f).reshape(n, 128).T)
    return {
        "w12t": np.ascontiguousarray(w12t),
        "h1t": h1t, "h2t": h2t, "w3t": w3t,
        "w1s": tile_s(inp["L0_ffn_w1_s"], NOH),
        "w2s": tile_s(inp["L0_ffn_w2_s"], NOH),
        "h1s": tile_s(inp["L0_ffn_h1_s"], NOH),
        "h2s": tile_s(inp["L0_ffn_h2_s"], NOH),
        "w3s": tile_s(inp["L0_ffn_w3_s"], NKD),
        "wo": np.ascontiguousarray(np.asarray(inp["L0_attn_Wo"], dtype=f)),
        "grms": tile_s(inp["L0_attn_g_rms"], 2),
        "lngt": tile_s(inp["L0_attn_ln_g"], NKD),
        "lnbt": tile_s(inp["L0_attn_ln_b"], NKD),
    }


def _prep_core(inp, c):
    f = np.float32
    b, s = c // 4, c % 4
    h = c % 4
    x = np.asarray(inp["L0_attn_x"], dtype=f)
    x0T = np.ascontiguousarray(x[b, s * TCORE:(s + 1) * TCORE, :].T)
    conv_tile = lambda a: np.ascontiguousarray(
        np.asarray(a, dtype=f)[:, h * DH:(h + 1) * DH].T.reshape(2, 128, 4)
        .transpose(1, 0, 2).reshape(128, 8))
    return {
        "x0T": x0T,
        "wq": np.ascontiguousarray(np.asarray(inp["L0_attn_Wq"], dtype=f)[:, h * DH:(h + 1) * DH]),
        "wk": np.ascontiguousarray(np.asarray(inp["L0_attn_Wk"], dtype=f)[:, h * DH:(h + 1) * DH]),
        "wv": np.ascontiguousarray(np.asarray(inp["L0_attn_Wv"], dtype=f)[:, h * DH:(h + 1) * DH]),
        "wb": np.ascontiguousarray(np.asarray(inp["L0_attn_Wb"], dtype=f)[:, h:h + 1]),
        "convq": conv_tile(inp["L0_attn_conv_q"]),
        "convk": conv_tile(inp["L0_attn_conv_k"]),
        "convv": conv_tile(inp["L0_attn_conv_v"]),
    }


def kernel(**inputs):
    shared = _prep_shared(inputs)
    nc = build_nc()
    in_maps = []
    for c in range(8):
        m = dict(shared)
        m.update(_prep_core(inputs, c))
        in_maps.append(m)
    res = run_bass_kernel_spmd(nc, in_maps, core_ids=list(range(8)))
    out = np.empty((NB, TSEQ, D), dtype=np.float32)
    for c in range(8):
        b, s = c // 4, c % 4
        out[b, s * TCORE:(s + 1) * TCORE, :] = res.results[c]["outT"].T
    return out


# version 12
# speedup vs baseline: 1.1366x; 1.0502x over previous
import numpy as np

try:
    import concourse.bass as bass
    import concourse.mybir as mybir
    from concourse.bass_utils import run_bass_kernel_spmd
except ImportError:
    import sys
    sys.path.insert(0, "/opt/trn_rl_repo")
    import concourse.bass as bass
    import concourse.mybir as mybir
    from concourse.bass_utils import run_bass_kernel_spmd

F32 = mybir.dt.float32
BF16 = mybir.dt.bfloat16
FP16 = mybir.dt.float16
AF = mybir.ActivationFunctionType
ALU = mybir.AluOpType

D = 1024
HID = 4096
DH = 256
H = 4
NB = 2
TSEQ = 8192
TCORE = 2048
TT = 256
CH = 128
GCH = 8
GT = CH * GCH
NKD = D // 128
NOH = HID // 128
NT = TCORE // TT
NG = TSEQ // GT
NSLICE = TSEQ // TCORE
EPS_LN = 1e-5
REPL_GROUPS = [[0, 1, 2, 3], [4, 5, 6, 7]]
EPS_L2 = 1e-6
EPS_RMS = 1e-5

SEM_NAMES = ("LDC", "LDX0", "LDX1", "LDR", "LDG", "LDA", "LDO", "LDL", "LDW0", "LDW1", "WS0", "WS1",
             "WL0", "WL1", "WF", "PE", "VE", "AC", "GP", "STR", "STXA", "STXB", "STOA", "STOB", "STU", "CC")


class SBAlloc:
    def __init__(self, nc, base, limit):
        self.nc = nc
        self.cur = base
        self.limit = limit

    def alloc(self, name, shape, dtype, align=64):
        per = int(np.prod(shape[1:])) * mybir.dt.size(dtype)
        off = (self.cur + align - 1) // align * align
        t = self.nc.alloc_sbuf_tensor_at(name, list(shape), dtype, offset=off)
        self.cur = off + per
        assert self.cur <= self.limit, f"SBUF overflow at {name}: {self.cur} > {self.limit}"
        return t


def build_nc():
    nc = bass.Bass("TRN2", target_bir_lowering=False)

    def din(name, shape):
        return nc.dram_tensor(name, list(shape), F32, kind="ExternalInput").ap()

    x0T = din("x0T", [D, TCORE])
    w12t = din("w12t", [8, NKD, 128, 1024])
    h1t = din("h1t", [16, NOH, 128, 256])
    h2t = din("h2t", [16, NOH, 128, 256])
    w3t = din("w3t", [4, NOH, 128, 256])
    w1s_i = din("w1s", [128, NOH])
    w2s_i = din("w2s", [128, NOH])
    h1s_i = din("h1s", [128, NOH])
    h2s_i = din("h2s", [128, NOH])
    w3s_i = din("w3s", [128, NKD])
    wq_i = din("wq", [D, DH])
    wk_i = din("wk", [D, DH])
    wv_i = din("wv", [D, DH])
    wb_i = din("wb", [D, 1])
    convq_i = din("convq", [128, 8])
    convk_i = din("convk", [128, 8])
    convv_i = din("convv", [128, 8])
    wo_i = din("wo", [D, D])
    grms_i = din("grms", [128, 2])
    lng_i = din("lngt", [128, NKD])
    lnb_i = din("lnbt", [128, NKD])
    outT = nc.dram_tensor("outT", [D, TCORE], F32, kind="ExternalOutput").ap()

    resid = nc.dram_tensor("resid", [D, TCORE], F32).ap()
    HC = TCORE // 2
    HS = TSEQ // 2
    xg_in = [nc.dram_tensor(f"xg_in{i}", [D, HC], F32).ap() for i in range(2)]
    xg_out = [nc.dram_tensor(f"xg_out{i}", [NSLICE * D, HC], F32).ap() for i in range(2)]
    og_in = [nc.dram_tensor(f"og_in{i}", [DH, HS], F32).ap() for i in range(2)]
    og_out = [nc.dram_tensor(f"og_out{i}", [H * DH, HS], F32).ap() for i in range(2)]
    wsc = nc.dram_tensor("wsc", [44, 128, 8192], FP16).ap()

    ARENA = 205 * 1024
    arena_base = nc.sbuf_base
    nc.alloc_sbuf_tensor("arena", [128, ARENA], mybir.dt.uint8)
    arena_limit = arena_base + ARENA
    A = SBAlloc(nc, arena_base, arena_limit)
    ones_col = A.alloc("ones_col", [128, 1], F32)
    ones_row = A.alloc("ones_row", [1, 128], F32)
    iota_t = A.alloc("iota_t", [128, 128], F32)
    ident = A.alloc("ident", [128, 128], F32)
    m_trilneg = A.alloc("m_trilneg", [128, 128], F32)
    m_triu = A.alloc("m_triu", [128, 128], F32)
    s_w1 = A.alloc("s_w1", [128, NOH], F32)
    s_w2 = A.alloc("s_w2", [128, NOH], F32)
    s_h1 = A.alloc("s_h1", [128, NOH], F32)
    s_h2 = A.alloc("s_h2", [128, NOH], F32)
    s_w3 = A.alloc("s_w3", [128, NKD], F32)
    s_w3h = A.alloc("s_w3h", [128, NKD], F32)
    c_q = A.alloc("c_q", [128, 8], F32)
    c_k = A.alloc("c_k", [128, 8], F32)
    c_v = A.alloc("c_v", [128, 8], F32)
    g_rms = A.alloc("g_rms", [128, 2], F32)
    ln_g = A.alloc("ln_g", [128, NKD], F32)
    ln_b = A.alloc("ln_b", [128, NKD], F32)
    row_mu = A.alloc("row_mu", [1, TT], F32)
    row_m2 = A.alloc("row_m2", [1, TT], F32)
    row_t = A.alloc("row_t", [1, TT], F32)
    row_var = A.alloc("row_var", [1, TT], F32)
    row_sd = A.alloc("row_sd", [1, TT], F32)
    row_ri = A.alloc("row_ri", [1, TT], F32)
    row_a = A.alloc("row_a", [1, TT], F32)
    row_l2 = A.alloc("row_l2", [1, 512], F32)
    row_l2b = A.alloc("row_l2b", [1, 512], F32)
    x_tile2 = [A.alloc(f"x_tile{i}", [128, NKD * TT], F32) for i in range(2)]
    r2b2 = [A.alloc(f"r2b{i}", [128, NKD * TT], F32) for i in range(2)]
    xlnb_tile = A.alloc("xlnb_tile", [128, NKD * TT], F32)
    sq_tile = A.alloc("sq_tile", [128, 2 * TT], F32)
    z_tile = A.alloc("z_tile", [128, TT], F32)
    e_tile = A.alloc("e_tile", [128, TT], F32)
    silu_t = A.alloc("silu_t", [128, TT], F32)
    hf_t = [A.alloc(f"hf_t{i}", [128, TT], F32) for i in range(2)]
    Rb = A.alloc("Rb", [128, TT], F32)
    Ab = A.alloc("Ab", [128, TT], F32)
    og_yp2 = [A.alloc(f"og_yp{i}", [128, NKD * TT], F32) for i in range(2)]
    const_vals = [EPS_LN, EPS_L2, 0.5, 1.0 / DH]
    const_ts = []
    for i, v in enumerate(sorted(set(const_vals))):
        tcv = A.alloc(f"cst{i}", [128, 1], F32)
        nc.const_aps.register(F32, v, tcv[:, :])
        const_ts.append((tcv, v))
    wo_buf = [A.alloc(f"wo_buf{i}", [128, D], F32) for i in range(2)]

    perm_end = A.cur
    Fz = SBAlloc(nc, perm_end, arena_limit)
    xsp2 = [[Fz.alloc(f"xsp{q}_{p}", [128, NKD * TT], FP16) for p in range(2)] for q in range(2)]
    wbuf = [Fz.alloc(f"wbuf{p}", [128, 8192], FP16) for p in range(2)]
    hspA = [Fz.alloc(f"hspA{p}", [128, NOH * TT], FP16) for p in range(2)]
    hspB = [Fz.alloc(f"hspB{p}", [128, NOH * TT], FP16) for p in range(2)]

    P2 = SBAlloc(nc, perm_end, arena_limit)
    xg_sb = P2.alloc("xg_sb", [128, NKD * GT], F32)
    u_sb = P2.alloc("u_sb", [128, 2 * (GT + 3)], F32)
    qT_sb = P2.alloc("qT_sb", [128, 2 * GT], F32)
    kT_sb = P2.alloc("kT_sb", [128, 2 * GT], F32)
    vT_sb = P2.alloc("vT_sb", [128, 2 * GT], F32)
    wq_sb = P2.alloc("wq_sb", [128, NKD * 256], F32)
    wk_sb = P2.alloc("wk_sb", [128, NKD * 256], F32)
    wv_sb = P2.alloc("wv_sb", [128, NKD * 256], F32)
    wb_sb = P2.alloc("wb_sb", [128, NKD], F32)
    beta_sb = P2.alloc("beta_sb", [128, GCH], F32)
    bh_sb = P2.alloc("bh_sb", [128, 1], F32)
    S_sb = P2.alloc("S_sb", [128, 512], F32)
    NSLOT = 4
    P_pp = [[P2.alloc(f"P_pp{ln}_{i}", [128, 128], F32) for i in range(2)] for ln in range(2)]
    T_pp = [[P2.alloc(f"T_pp{ln}_{i}", [128, 128], F32) for i in range(2)] for ln in range(2)]
    X_a = [P2.alloc(f"X_a{i}", [128, 128], F32) for i in range(NSLOT)]
    W1Tn_a = [P2.alloc(f"W1Tn_a{i}", [128, 256], F32) for i in range(NSLOT)]
    Bv_a = [P2.alloc(f"Bv_a{i}", [128, 256], F32) for i in range(NSLOT)]
    LT_a = [P2.alloc(f"LT_a{i}", [128, 128], F32) for i in range(NSLOT)]
    Ktm_a = [P2.alloc(f"Ktm_a{i}", [128, 256], F32) for i in range(NSLOT)]
    Bkn_t2 = [P2.alloc(f"Bkn_t{ln}", [128, 256], F32) for ln in range(2)]
    U_sb = P2.alloc("U_sb", [128, 256], F32)
    o_sb = P2.alloc("o_sb", [128, 256], F32)
    on_sb = P2.alloc("on_sb", [128, 256], F32)
    ossq = P2.alloc("ossq", [128, 1], F32)
    orr = P2.alloc("orr", [128, 1], F32)
    orinv = P2.alloc("orinv", [128, 1], F32)
    oT_sb = P2.alloc("oT_sb", [128, 256], F32)
    utail = P2.alloc("utail", [128, 18], F32)

    pb = [nc.alloc_psum_tensor(f"pb{i}", [128, 512], F32) for i in range(8)]

    sems = {s: nc.alloc_semaphore(s) for s in SEM_NAMES}

    def emit(me, eng_handle):
        C = {k: 0 for k in SEM_NAMES}
        bank_free = [None] * 8
        rr = [0]

        rr_slow = [0]

        def op(engine, build, waits=(), incs=(), drain=True):
            assert len(incs) <= 1, "one semaphore update per instruction"
            for s, n in incs:
                C[s] += n
            if me == engine:
                for s, v in waits:
                    if v > 0:
                        eng_handle.wait_ge(sems[s], v)
                ins = build(eng_handle)
                for s, n in incs:
                    ins = ins.then_inc(sems[s], n)
                if drain and engine in ("vector", "scalar", "gpsimd"):
                    eng_handle.drain()

        def waitonly(engine, waits):
            if me == engine:
                for s, v in waits:
                    if v > 0:
                        eng_handle.wait_ge(sems[s], v)

        def get_bank(extra=()):
            b = rr[0] % 6
            rr[0] += 1
            w = list(extra)
            if bank_free[b] is not None:
                w.append(bank_free[b])
            return b, w

        def get_bank_slow(extra=()):
            b = 6 + rr_slow[0] % 2
            rr_slow[0] += 1
            w = list(extra)
            if bank_free[b] is not None:
                w.append(bank_free[b])
            return b, w

        def rel_bank(b, sem):
            bank_free[b] = (sem, C[sem])

        for dst, src in [(s_w1, w1s_i), (s_w2, w2s_i), (s_h1, h1s_i), (s_h2, h2s_i),
                         (s_w3, w3s_i), (c_q, convq_i), (c_k, convk_i), (c_v, convv_i),
                         (g_rms, grms_i), (ln_g, lng_i), (ln_b, lnb_i)]:
            op("sync", lambda e, d=dst, s=src: e.dma_start(out=d[:, :], in_=s),
               incs=[("LDC", 16)])
        ld_consts = C["LDC"]

        op("gpsimd", lambda e: e.iota(iota_t[:, :], pattern=[[1, 128]], base=0,
                                      channel_multiplier=-1,
                                      allow_small_or_imprecise_dtypes=True),
           incs=[("GP", 1)])
        gp_iota = C["GP"]
        op("vector", lambda e: e.tensor_scalar(m_trilneg[:, :], iota_t[:, :], 0.0, -1.0,
                                               ALU.is_lt, ALU.mult),
           waits=[("GP", gp_iota)], incs=[("VE", 1)])
        op("vector", lambda e: e.tensor_single_scalar(m_triu[:, :], iota_t[:, :], 0.0, ALU.is_ge),
           incs=[("VE", 1)])
        op("vector", lambda e: e.tensor_single_scalar(ident[:, :], iota_t[:, :], 0.0, ALU.is_equal),
           incs=[("VE", 1)])
        op("vector", lambda e: e.memset(ones_col[:, :], 1.0), incs=[("VE", 1)])
        op("vector", lambda e: e.memset(ones_row[:, :], 1.0), incs=[("VE", 1)])
        for tcv, v in const_ts:
            op("vector", lambda e, tcv=tcv, v=v: e.memset(tcv[:, :], v), incs=[("VE", 1)])
        op("vector", lambda e: e.tensor_scalar_mul(s_w3h[:, :], s_w3[:, :], 0.5),
           waits=[("LDC", ld_consts)], incs=[("VE", 1)])
        ve_setup = C["VE"]

        S = {
            "pe_sq": 0,
            "ac_rows": 0,
            "pe_rows": 0,
            "ve_silu": 0,
            "ve_hf": [0, 0],
            "st_r": 0, "st_xlnb": ("STXA", 0), "st_out": 0, "st_og": ("STOA", 0),
            "ve_x_read": 0,
            "pe_og_read": 0,
            "ve_yp_read": 0,
            "ve_xlnb_read": 0,
            "pe_l2": 0,
            "ac_osq": 0,
            "ve_orr": 0,
            "pe_scan_q": 0,
            "pe_chain_prev": [0, 0],
            "pe_bkn_prev": [0, 0],
            "pe_u_prev": 0,
            "tail_pe": [0, 0, 0, 0],
            "ve_mt": [0, 0, 0, 0],
            "ve_indep": [0, 0, 0, 0],
            "pe_xg": 0,
            "ve_wo_rd": 0,
        }

        def ln_stats(src, src_ready):
            b1, w1_ = get_bank_slow(list(src_ready) + [("VE", ve_setup)])
            b2, w2_ = get_bank_slow()
            pe_mm2 = {}
            for kc in range(NKD):
                sqs = slice((kc % 2) * TT, (kc % 2) * TT + TT)
                w = []
                if kc == 0:
                    w = list(src_ready) + [("PE", S["pe_sq"])]
                elif kc >= 2:
                    w = [("PE", pe_mm2[kc - 2])]
                op("scalar", lambda e, kc=kc, sqs=sqs: e.activation(
                    sq_tile[:, sqs], src[:, kc * TT:(kc + 1) * TT], AF.Square),
                   waits=w, incs=[("AC", 1)])
                ac_kc = C["AC"]
                op("tensor", lambda e, kc=kc, b=b1: e.matmul(
                    pb[b][0:1, :TT], ones_col[:, :], src[:, kc * TT:(kc + 1) * TT],
                    start=(kc == 0), stop=(kc == NKD - 1)),
                   waits=(w1_ if kc == 0 else ()),
                   incs=([("PE", 1)] if kc == NKD - 1 else ()))
                if kc == NKD - 1:
                    pe_s1 = C["PE"]
                op("tensor", lambda e, kc=kc, b=b2, sqs=sqs: e.matmul(
                    pb[b][0:1, :TT], ones_col[:, :], sq_tile[:, sqs],
                    start=(kc == 0), stop=(kc == NKD - 1)),
                   waits=(list(w2_) if kc == 0 else []) + [("AC", ac_kc)],
                   incs=[("PE", 1)])
                pe_mm2[kc] = C["PE"]
            pe_s2 = C["PE"]
            S["pe_sq"] = pe_s2
            op("vector", lambda e, b=b1: e.tensor_scalar_mul(row_mu[:, :], pb[b][0:1, :TT], 1.0 / D),
               waits=[("PE", pe_s1), ("AC", S["ac_rows"]), ("PE", S["pe_rows"])],
               incs=[("VE", 1)])
            op("vector", lambda e, b=b2: e.tensor_scalar_mul(row_m2[:, :], pb[b][0:1, :TT], 1.0 / D),
               waits=[("PE", pe_s2)], incs=[("VE", 1)])
            rel_bank(b1, "VE")
            rel_bank(b2, "VE")
            op("vector", lambda e: e.tensor_mul(row_t[:, :], row_mu[:, :], row_mu[:, :]),
               incs=[("VE", 1)])
            op("vector", lambda e: e.tensor_sub(row_var[:, :], row_m2[:, :], row_t[:, :]),
               incs=[("VE", 1)])
            ve_var = C["VE"]
            op("scalar", lambda e: e.activation(row_sd[:, :], row_var[:, :], AF.Sqrt, bias=EPS_LN),
               waits=[("VE", ve_var)], incs=[("AC", 1)])
            ac_sd = C["AC"]
            S["ac_rows"] = ac_sd
            op("vector", lambda e: e.reciprocal(row_ri[:, :], row_sd[:, :]),
               waits=[("AC", ac_sd)], incs=[("VE", 1)])
            op("vector", lambda e: e.tensor_mul(row_a[:, :], row_mu[:, :], row_ri[:, :]),
               incs=[("VE", 1)])
            ve_rows = C["VE"]
            b3, w3_ = get_bank_slow([("VE", ve_rows)])
            op("tensor", lambda e, b=b3: e.matmul(pb[b][:, :TT], ones_row[:, :], row_ri[0:1, :TT],
                                                  start=True, stop=True),
               waits=w3_, incs=[("PE", 1)])
            pe_b1 = C["PE"]
            b4, w4_ = get_bank_slow()
            op("tensor", lambda e, b=b4: e.matmul(pb[b][:, :TT], ones_row[:, :], row_a[0:1, :TT],
                                                  start=True, stop=True),
               waits=w4_, incs=[("PE", 1)])
            pe_b2 = C["PE"]
            S["pe_rows"] = pe_b2
            op("vector", lambda e, b=b3: e.tensor_copy(Rb[:, :], pb[b][:, :TT]),
               waits=[("PE", pe_b1)], incs=[("VE", 1)])
            rel_bank(b3, "VE")
            op("vector", lambda e, b=b4: e.tensor_copy(Ab[:, :], pb[b][:, :TT]),
               waits=[("PE", pe_b2)], incs=[("VE", 1)])
            rel_bank(b4, "VE")
            return C["VE"]

        def ln_split_kc(src, ve_rb, xsp, pe_guard, kc):
            cs = slice(kc * TT, (kc + 1) * TT)
            op("vector", lambda e, cs=cs: e.tensor_mul(z_tile[:, :], src[:, cs], Rb[:, :]),
               waits=([("VE", ve_rb), ("PE", pe_guard)] if kc == 0 else ()),
               incs=[("VE", 1)])
            op("vector", lambda e: e.tensor_sub(z_tile[:, :], z_tile[:, :], Ab[:, :]),
               incs=[("VE", 1)])
            op("vector", lambda e, cs=cs, xsp=xsp: e.tensor_copy(xsp[0][:, cs], z_tile[:, :]),
               incs=[("VE", 1)])
            op("vector", lambda e, cs=cs, xsp=xsp: e.tensor_sub(xsp[1][:, cs], z_tile[:, :], xsp[0][:, cs]),
               incs=[("VE", 1)])
            return C["VE"]

        def ln_split(src, ve_rb, xsp, pe_guard=0):
            for kc in range(NKD):
                ln_split_kc(src, ve_rb, xsp, pe_guard, kc)
            return C["VE"]

        wl_state = {"idx": 0, "pe_done": {}}
        NGRP_W = 44

        def wload(src_ap, nk, sub):
            idx = wl_state["idx"]
            gidx = idx % NGRP_W
            par = idx % 2
            sem = "WL0" if par == 0 else "WL1"
            wssem = "WS0" if par == 0 else "WS1"
            waits = []
            if idx >= 2:
                waits.append(("PE", wl_state["pe_done"].get(idx - 2, 0)))
                if idx - 2 < NGRP_W:
                    waits.append((wssem, 16 * (idx // 2)))
            if idx < NGRP_W:
                op("gpsimd", lambda e: e.dma_start(
                    out=wbuf[par].ap().rearrange("p (k j) -> p k j", k=nk)[:, :, :sub],
                    in_=src_ap.rearrange("k p j -> p k j")),
                   waits=waits, incs=[(sem, 16)], drain=False)
                op("sync", lambda e: e.dma_start(
                    out=wsc[gidx], in_=wbuf[par][:, :]),
                   waits=[(sem, 16 * (idx // 2 + 1))], incs=[(wssem, 16)])
            else:
                gs = "WS0" if gidx % 2 == 0 else "WS1"
                w2 = (gs, 16 * (gidx // 2 + 1))
                if w2 not in waits and not any(s == gs and v >= w2[1] for s, v in waits):
                    waits.append(w2)
                op("gpsimd", lambda e: e.dma_start(
                    out=wbuf[par][:, :], in_=wsc[gidx]),
                   waits=waits, incs=[(sem, 16)], drain=False)
            wl_state["idx"] += 1
            wl_state["last_idx"] = idx
            return par, (sem, 16 * (idx // 2 + 1))

        def ffn(ve_split, in1_tile, out_tile, st_out_guard, xsp, inject=None):
            inj = inject or {}

            def fire(k):
                for fn in inj.get(k, ()):
                    fn()
            for og in range(8):
                par, wlv = wload(w12t[og], NKD, 1024)
                widx = wl_state["last_idx"]
                for ocp in range(4):
                    oc = og * 4 + ocp
                    bA, wA = get_bank()
                    if ocp == 0:
                        wA += [wlv, ("VE", ve_split)]
                    for kc in range(NKD):
                        for p in range(2):
                            last = (kc == NKD - 1 and p == 1)
                            op("tensor", lambda e, kc=kc, p=p, b=bA, ocp=ocp, par=par: e.matmul(
                                pb[b][:, :TT],
                                wbuf[par][:, kc * 1024 + ocp * 128: kc * 1024 + ocp * 128 + 128],
                                xsp[p][:, kc * TT:(kc + 1) * TT],
                                start=(kc == 0 and p == 0), stop=last),
                               waits=(wA if (kc == 0 and p == 0) else ()),
                               incs=([("PE", 1)] if last else ()))
                    peA = C["PE"]
                    bB, wB = get_bank()
                    for kc in range(NKD):
                        for p in range(2):
                            last = (kc == NKD - 1 and p == 1)
                            op("tensor", lambda e, kc=kc, p=p, b=bB, ocp=ocp, par=par: e.matmul(
                                pb[b][:, :TT],
                                wbuf[par][:, kc * 1024 + 512 + ocp * 128: kc * 1024 + 512 + ocp * 128 + 128],
                                xsp[p][:, kc * TT:(kc + 1) * TT],
                                start=(kc == 0 and p == 0), stop=last),
                               waits=(wB if (kc == 0 and p == 0) else ()),
                               incs=([("PE", 1)] if last else ()))
                    peB = C["PE"]
                    op("scalar", lambda e, b=bA, oc=oc: e.activation(
                        silu_t[:, :], pb[b][:, :TT], AF.Silu, scale=s_w1[:, oc:oc + 1]),
                       waits=[("PE", peA), ("VE", S["ve_silu"])], incs=[("AC", 1)])
                    rel_bank(bA, "AC")
                    ac_silu = C["AC"]
                    ht = hf_t[oc % 2]
                    op("vector", lambda e, b=bB, oc=oc, ht=ht: e.scalar_tensor_tensor(
                        ht[:, :], pb[b][:, :TT], s_w2[:, oc:oc + 1], silu_t[:, :],
                        ALU.mult, ALU.mult),
                       waits=[("PE", peB), ("AC", ac_silu)], incs=[("VE", 1)])
                    S["ve_silu"] = C["VE"]
                    rel_bank(bB, "VE")
                    if ocp == 3:
                        wl_state["pe_done"][widx] = peB
                    cs = slice(oc * TT, (oc + 1) * TT)
                    op("vector", lambda e, cs=cs, ht=ht: e.tensor_copy(hspA[0][:, cs], ht[:, :]),
                       incs=[("VE", 1)])
                    op("vector", lambda e, cs=cs, ht=ht: e.tensor_sub(hspA[1][:, cs], ht[:, :], hspA[0][:, cs]),
                       incs=[("VE", 1)])
                    S["ve_hf"][oc % 2] = C["VE"]
            ve_hA = C["VE"]
            pe_stageA = C["PE"]
            fire("A_end")

            def hidden_stage(wt_ap, s_t, src_sp, dst_sp, ve_src, mid, end):
                stg = mid[0]
                for og in range(16):
                    if og == 8:
                        fire(mid)
                    fire(stg + str(og))
                    par, wlv = wload(wt_ap[og], NOH, 256)
                    widx = wl_state["last_idx"]
                    for ocj in range(2):
                        oc = og * 2 + ocj
                        bk, wk_ = get_bank()
                        if ocj == 0:
                            wk_ += [wlv, ("VE", ve_src)]
                        for kc in range(NOH):
                            for p in range(2):
                                last = (kc == NOH - 1 and p == 1)
                                op("tensor", lambda e, kc=kc, p=p, b=bk, ocj=ocj, par=par, sp=src_sp: e.matmul(
                                    pb[b][:, :TT],
                                    wbuf[par][:, kc * 256 + ocj * 128: kc * 256 + ocj * 128 + 128],
                                    sp[p][:, kc * TT:(kc + 1) * TT],
                                    start=(kc == 0 and p == 0), stop=last),
                                   waits=(wk_ if (kc == 0 and p == 0) else ()),
                                   incs=([("PE", 1)] if last else ()))
                        peg = C["PE"]
                        ht = hf_t[oc % 2]
                        op("scalar", lambda e, b=bk, oc=oc, ht=ht, s_t=s_t: e.activation(
                            ht[:, :], pb[b][:, :TT], AF.Gelu, scale=s_t[:, oc:oc + 1]),
                           waits=[("PE", peg), ("VE", S["ve_hf"][oc % 2])], incs=[("AC", 1)])
                        rel_bank(bk, "AC")
                        ac_g = C["AC"]
                        cs = slice(oc * TT, (oc + 1) * TT)
                        op("vector", lambda e, cs=cs, ht=ht, sp=dst_sp: e.tensor_copy(sp[0][:, cs], ht[:, :]),
                           waits=[("AC", ac_g)], incs=[("VE", 1)])
                        op("vector", lambda e, cs=cs, ht=ht, sp=dst_sp: e.tensor_sub(sp[1][:, cs], ht[:, :], sp[0][:, cs]),
                           incs=[("VE", 1)])
                        S["ve_hf"][oc % 2] = C["VE"]
                        if ocj == 1:
                            wl_state["pe_done"][widx] = peg
                fire(end)
                return C["VE"]

            ve_hB = hidden_stage(h1t, s_h1, hspA, hspB, ve_hA, "B_mid", "B_end")
            ve_hC = hidden_stage(h2t, s_h2, hspB, hspA, ve_hB, "C_mid", "C_end")
            ffn.pe_stageA = pe_stageA

            for og in range(4):
                par, wlv = wload(w3t[og], NOH, 256)
                widx = wl_state["last_idx"]
                for ocj in range(2):
                    oc = og * 2 + ocj
                    bk, wk_ = get_bank()
                    if ocj == 0:
                        wk_ += [wlv, ("VE", ve_hC)]
                    for kc in range(NOH):
                        for p in range(2):
                            last = (kc == NOH - 1 and p == 1)
                            op("tensor", lambda e, kc=kc, p=p, b=bk, ocj=ocj, par=par: e.matmul(
                                pb[b][:, :TT],
                                wbuf[par][:, kc * 256 + ocj * 128: kc * 256 + ocj * 128 + 128],
                                hspA[p][:, kc * TT:(kc + 1) * TT],
                                start=(kc == 0 and p == 0), stop=last),
                               waits=(wk_ if (kc == 0 and p == 0) else ()),
                               incs=([("PE", 1)] if last else ()))
                    peg = C["PE"]
                    cs = slice(oc * TT, (oc + 1) * TT)
                    w = [("PE", peg)]
                    if oc == 0:
                        w.append(st_out_guard())
                    op("vector", lambda e, b=bk, oc=oc, cs=cs: e.scalar_tensor_tensor(
                        out_tile[:, cs], pb[b][:, :TT], s_w3h[:, oc:oc + 1], in1_tile[:, cs],
                        ALU.mult, ALU.add),
                       waits=w, incs=[("VE", 1)])
                    rel_bank(bk, "VE")
                    if ocj == 1:
                        wl_state["pe_done"][widx] = peg
            return C["VE"]

        stxa_full = [16 * max(1, NT // 2)]
        ve_split_t = {}
        pe_xsp_par = [0, 0]
        ve_x_par = [0, 0]
        pend1 = {}
        rb1 = {}

        def p1_load_stats(t):
            par = t % 2
            cols = slice(t * TT, (t + 1) * TT)
            xt = x_tile2[par]
            sem = "LDX0" if par == 0 else "LDX1"
            op("sync", lambda e, xt=xt, cols=cols: e.dma_start(
                out=xt.ap().rearrange("p (k c) -> p k c", k=NKD),
                in_=x0T[:, cols].rearrange("(k p) c -> p k c", p=128)),
               waits=[("VE", ve_x_par[par])], incs=[(sem, 16)])
            rb1[t] = ln_stats(xt, [(sem, C[sem])])

        def p1_splits_kc(t, kc):
            par = t % 2
            ve_split_t[t] = ln_split_kc(x_tile2[par], rb1[t], xsp2[par], pe_xsp_par[par], kc)
            if kc == NKD - 1:
                rb1.pop(t)

        def p1_lnb(t):
            r_tile = r2b2[0]
            ve_r = pend1[t][0]
            ve_rb2 = ln_stats(r_tile, [("VE", ve_r)])
            for kc in range(NKD):
                cs = slice(kc * TT, (kc + 1) * TT)
                w = []
                if kc == 0:
                    w = [("VE", ve_rb2), S["st_xlnb"]]
                op("vector", lambda e, cs=cs, r_tile=r_tile: e.tensor_mul(z_tile[:, :], r_tile[:, cs], Rb[:, :]),
                   waits=w, incs=[("VE", 1)])
                op("vector", lambda e, cs=cs: e.tensor_sub(xlnb_tile[:, cs], z_tile[:, :], Ab[:, :]),
                   incs=[("VE", 1)])
            pend1[t] = (ve_r, C["VE"])

        def p1_stores(t):
            cols = slice(t * TT, (t + 1) * TT)
            ve_r, ve_xlnb = pend1.pop(t)
            op("sync", lambda e, cols=cols: e.dma_start(
                out=resid[:, cols].rearrange("(k p) c -> p k c", p=128),
                in_=r2b2[0].ap().rearrange("p (k c) -> p k c", k=NKD)),
               waits=[("VE", ve_r)], incs=[("STR", 16)])
            S["st_r"] = C["STR"]
            half = (t * TT) // HC
            colh = (t * TT) % HC
            sxs = "STXA" if half == 0 else "STXB"
            op("sync", lambda e, half=half, colh=colh: e.dma_start(
                out=xg_in[half][:, colh:colh + TT].rearrange("(k p) c -> p k c", p=128),
                in_=xlnb_tile.ap().rearrange("p (k c) -> p k c", k=NKD)),
               waits=[("VE", ve_xlnb)], incs=[(sxs, 16)])
            S["st_xlnb"] = (sxs, C[sxs])

        p1_load_stats(0)
        for kc in range(NKD):
            p1_splits_kc(0, kc)
        for t in range(NT):
            par = t % 2
            inj = {}
            if t + 1 < NT:
                inj["A_end"] = [lambda tt=t + 1: p1_load_stats(tt)]
                for kc in range(NKD):
                    inj.setdefault("B" + str(2 * kc), []).append(
                        lambda tt=t + 1, kc=kc: p1_splits_kc(tt, kc))
            if t - 1 in pend1:
                inj.setdefault("A_end", []).append(lambda tt=t - 1: p1_stores(tt))
            pend1[t] = (0, 0)
            ve_r = ffn(ve_split_t.pop(t), x_tile2[par], r2b2[0],
                       lambda: ("STR", S["st_r"]), xsp2[par], inject=inj)
            pe_xsp_par[par] = ffn.pe_stageA
            ve_x_par[par] = ve_r
            pend1[t] = (ve_r, None)
            p1_lnb(t)
            if t == NT // 2 and NT > 1:
                op("gpsimd", lambda e: e.collective_compute(
                    "AllGather", ALU.bypass, replica_groups=REPL_GROUPS,
                    ins=[xg_in[0]], outs=[xg_out[0]]),
                   waits=[("STXA", stxa_full[0])], incs=[("CC", 1)], drain=False)
        p1_stores(NT - 1)
        if NT <= 1:
            op("gpsimd", lambda e: e.collective_compute(
                "AllGather", ALU.bypass, replica_groups=REPL_GROUPS,
                ins=[xg_in[0]], outs=[xg_out[0]]),
               waits=[("STXA", stxa_full[0])], incs=[("CC", 1)], drain=False)
        op("gpsimd", lambda e: e.collective_compute(
            "AllGather", ALU.bypass, replica_groups=REPL_GROUPS,
            ins=[xg_in[1]], outs=[xg_out[1]]),
           waits=[("STXB", C["STXB"])], incs=[("CC", 1)], drain=False)
        cc1 = C["CC"]

        f_pe, f_ac, f_ve = C["PE"], C["AC"], C["VE"]
        aw = [("PE", f_pe), ("AC", f_ac), ("VE", f_ve)]
        for dst, src in [(wq_sb, wq_i), (wk_sb, wk_i), (wv_sb, wv_i)]:
            op("sync", lambda e, d=dst, s=src: e.dma_start(
                out=d.ap().rearrange("p (k j) -> p k j", k=NKD),
                in_=s.rearrange("(k p) j -> p k j", p=128)),
               waits=aw, incs=[("LDA", 16)])
            aw = []
        op("sync", lambda e: e.dma_start(
            out=wb_sb.ap().rearrange("p (k j) -> p k j", j=1),
            in_=wb_i.rearrange("(k p) j -> p k j", p=128),
            allow_slow_non_contiguous=True),
           incs=[("LDA", 16)])
        ld_attnw = C["LDA"]
        op("vector", lambda e: e.memset(S_sb[:, :], 0.0),
           waits=[("PE", f_pe), ("AC", f_ac)], incs=[("VE", 1)])
        op("vector", lambda e: e.memset(utail[:, :], 0.0), incs=[("VE", 1)])
        ve_supd = C["VE"]

        NPIECE = max(1, GT // 512)
        PW = GT // NPIECE
        for g in range(NG):
            gt0 = g * GT
            blk = gt0 // TCORE
            col0 = gt0 % TCORE
            hb = col0 // HC
            colh = col0 % HC
            cc_need = 1 if hb == 0 else 2
            ld_xg = []
            for kc in range(NKD):
                w = [("PE", S["pe_xg"]), ("CC", cc_need)] if kc == 0 else []
                op("sync", lambda e, kc=kc, hb=hb, colh=colh: e.dma_start(
                    out=xg_sb[:, kc * GT:(kc + 1) * GT],
                    in_=xg_out[hb][blk * D + kc * 128: blk * D + kc * 128 + 128,
                                   colh:colh + GT]),
                   waits=w, incs=[("LDG", 16)])
                ld_xg.append(C["LDG"])

            for j in range(GCH):
                bk, wk_ = get_bank()
                for kc in range(NKD):
                    w = (wk_ + [("LDG", ld_xg[-1]), ("LDA", ld_attnw)]) if kc == 0 else []
                    op("tensor", lambda e, kc=kc, j=j, b=bk: e.matmul(
                        pb[b][:, 0:1],
                        xg_sb[:, kc * GT + j * CH: kc * GT + j * CH + 128],
                        wb_sb[:, kc:kc + 1],
                        start=(kc == 0), stop=(kc == NKD - 1)),
                       waits=w, incs=([("PE", 1)] if kc == NKD - 1 else ()))
                peb = C["PE"]
                op("scalar", lambda e, b=bk: e.activation(
                    bh_sb[:, :], pb[b][:, 0:1], AF.Tanh, scale=0.5),
                   waits=[("PE", peb), ("VE", S.get("ve_bh", 0))], incs=[("AC", 1)])
                rel_bank(bk, "AC")
                acb = C["AC"]
                op("vector", lambda e, j=j: e.tensor_scalar(
                    beta_sb[:, j:j + 1], bh_sb[:, :], 0.5, 0.5, ALU.mult, ALU.add),
                   waits=[("AC", acb)], incs=[("VE", 1)])
                S["ve_bh"] = C["VE"]
            ve_beta = C["VE"]

            for wi, (wsb, csb, dst) in enumerate([(wq_sb, c_q, qT_sb),
                                                  (wk_sb, c_k, kT_sb),
                                                  (wv_sb, c_v, vT_sb)]):
                for c2 in range(2):
                    op("vector", lambda e, c2=c2, wi=wi: e.tensor_copy(
                        u_sb[:, c2 * (GT + 3): c2 * (GT + 3) + 3],
                        utail[:, (wi * 2 + c2) * 3: (wi * 2 + c2) * 3 + 3]),
                       incs=[("VE", 1)])
                for c2 in range(2):
                    for piece in range(NPIECE):
                        bk, wk_ = get_bank()
                        w = wk_ + [("LDG", ld_xg[-1]), ("LDA", ld_attnw)]
                        for kc in range(NKD):
                            op("tensor", lambda e, kc=kc, c2=c2, piece=piece, b=bk, wsb=wsb: e.matmul(
                                pb[b][:, :PW],
                                wsb[:, kc * 256 + c2 * 128: kc * 256 + c2 * 128 + 128],
                                xg_sb[:, kc * GT + piece * PW: kc * GT + piece * PW + PW],
                                start=(kc == 0), stop=(kc == NKD - 1)),
                               waits=(w if kc == 0 else ()),
                               incs=([("PE", 1)] if kc == NKD - 1 else ()))
                        pep = C["PE"]
                        op("vector", lambda e, c2=c2, piece=piece, b=bk: e.tensor_copy(
                            u_sb[:, c2 * (GT + 3) + 3 + piece * PW: c2 * (GT + 3) + 3 + piece * PW + PW],
                            pb[b][:, :PW]),
                           waits=[("PE", pep), ("AC", S.get("ac_u", 0))], incs=[("VE", 1)])
                        rel_bank(bk, "VE")
                for c2 in range(2):
                    op("vector", lambda e, c2=c2, wi=wi: e.tensor_copy(
                        utail[:, (wi * 2 + c2) * 3: (wi * 2 + c2) * 3 + 3],
                        u_sb[:, c2 * (GT + 3) + GT: c2 * (GT + 3) + GT + 3]),
                       incs=[("VE", 1)])
                for c2 in range(2):
                    ub = c2 * (GT + 3)
                    dcol = c2 * GT
                    w = [("PE", S["pe_scan_q"])] if (c2 == 0) else []
                    op("vector", lambda e, ub=ub, dcol=dcol, csb=csb, dst=dst, c2=c2: e.tensor_scalar_mul(
                        dst[:, dcol:dcol + GT], u_sb[:, ub:ub + GT], csb[:, c2 * 4:c2 * 4 + 1]),
                       waits=w, incs=[("VE", 1)])
                    for tap in range(1, 4):
                        op("vector", lambda e, ub=ub, dcol=dcol, csb=csb, dst=dst, c2=c2, tap=tap: e.scalar_tensor_tensor(
                            dst[:, dcol:dcol + GT], u_sb[:, ub + tap:ub + tap + GT],
                            csb[:, c2 * 4 + tap:c2 * 4 + tap + 1], dst[:, dcol:dcol + GT],
                            ALU.mult, ALU.add),
                           incs=[("VE", 1)])
                ve_conv = C["VE"]
                for c2 in range(2):
                    op("scalar", lambda e, c2=c2, dst=dst: e.activation(
                        dst[:, c2 * GT:(c2 + 1) * GT], dst[:, c2 * GT:(c2 + 1) * GT], AF.Silu),
                       waits=([("VE", ve_conv)] if c2 == 0 else ()), incs=[("AC", 1)])
            ac_silu_all = C["AC"]

            for wi, dst in [(0, qT_sb), (1, kT_sb)]:
                for c2 in range(2):
                    w = [("PE", S["pe_l2"])]
                    if wi == 0 and c2 == 0:
                        w.append(("AC", ac_silu_all))
                    op("scalar", lambda e, c2=c2, dst=dst: e.activation(
                        u_sb[:, c2 * (GT + 3) + 3: c2 * (GT + 3) + 3 + GT],
                        dst[:, c2 * GT:(c2 + 1) * GT], AF.Square),
                       waits=w, incs=[("AC", 1)])
                ac_sq2 = C["AC"]
                S["ac_u"] = ac_sq2
                for piece in range(NPIECE):
                    bk, wk_ = get_bank_slow([("AC", ac_sq2)])
                    for c2 in range(2):
                        op("tensor", lambda e, c2=c2, piece=piece, b=bk: e.matmul(
                            pb[b][0:1, :PW], ones_col[:, :],
                            u_sb[:, c2 * (GT + 3) + 3 + piece * PW: c2 * (GT + 3) + 3 + piece * PW + PW],
                            start=(c2 == 0), stop=(c2 == 1)),
                           waits=(wk_ if c2 == 0 else ()),
                           incs=([("PE", 1)] if c2 == 1 else ()))
                    pel = C["PE"]
                    S["pe_l2"] = pel
                    op("scalar", lambda e, b=bk: e.activation(
                        row_l2[0:1, :PW], pb[b][0:1, :PW], AF.Sqrt, bias=EPS_L2),
                       waits=[("PE", pel), ("VE", S["ve_wo_rd"])], incs=[("AC", 1)])
                    rel_bank(bk, "AC")
                    acl = C["AC"]
                    op("vector", lambda e: e.reciprocal(row_l2b[0:1, :PW], row_l2[0:1, :PW]),
                       waits=[("AC", acl)], incs=[("VE", 1)])
                    vel = C["VE"]
                    S["ve_wo_rd"] = vel
                    b2, w2_ = get_bank_slow([("VE", vel)])
                    op("tensor", lambda e, b=b2: e.matmul(
                        pb[b][:, :PW], ones_row[:, :], row_l2b[0:1, :PW],
                        start=True, stop=True),
                       waits=w2_, incs=[("PE", 1)])
                    pebc = C["PE"]
                    for c2 in range(2):
                        op("vector", lambda e, c2=c2, piece=piece, b=b2, dst=dst: e.tensor_mul(
                            dst[:, c2 * GT + piece * PW: c2 * GT + piece * PW + PW],
                            dst[:, c2 * GT + piece * PW: c2 * GT + piece * PW + PW],
                            pb[b][:, :PW]),
                           waits=([("PE", pebc)] if c2 == 0 else ()), incs=[("VE", 1)])
                    rel_bank(b2, "VE")
            ve_qk = C["VE"]

            SKEW = 2

            def indep_gen(j):
                lane = j % 2
                slot = j % NSLOT
                jb = j * CH
                bcol = beta_sb[:, j:j + 1]
                Ppp = P_pp[lane]
                Tpp = T_pp[lane]
                Bkn_t = Bkn_t2[lane]

                def qs(c2):
                    return qT_sb[:, c2 * GT + jb: c2 * GT + jb + 128]

                def ks(c2):
                    return kT_sb[:, c2 * GT + jb: c2 * GT + jb + 128]

                def vs(c2):
                    return vT_sb[:, c2 * GT + jb: c2 * GT + jb + 128]

                slot_guard = [("PE", S["tail_pe"][slot])]
                for c2 in range(2):
                    bk, wk_ = get_bank([("VE", ve_qk), ("AC", ac_silu_all)])
                    op("tensor", lambda e, b=bk, c2=c2: e.transpose(
                        pb[b][:, :128], vs(c2), ident[:, :]),
                       waits=wk_, incs=[("PE", 1)])
                    pet = C["PE"]
                    op("vector", lambda e, b=bk, c2=c2, slot=slot: e.tensor_scalar_mul(
                        Bv_a[slot][:, c2 * 128:(c2 + 1) * 128], pb[b][:, :128], bcol),
                       waits=[("PE", pet), ("VE", ve_beta)] + (slot_guard if c2 == 0 else []),
                       incs=[("VE", 1)])
                    rel_bank(bk, "VE")
                yield
                for c2 in range(2):
                    bk, wk_ = get_bank()
                    op("tensor", lambda e, b=bk, c2=c2: e.transpose(
                        pb[b][:, :128], ks(c2), ident[:, :]),
                       waits=wk_, incs=[("PE", 1)])
                    pet = C["PE"]
                    op("vector", lambda e, b=bk, c2=c2, slot=slot: e.tensor_copy(
                        Ktm_a[slot][:, c2 * 128:(c2 + 1) * 128], pb[b][:, :128]),
                       waits=[("PE", pet)], incs=[("VE", 1)])
                    rel_bank(bk, "VE")
                yield
                bk, wk_ = get_bank()
                for c2 in range(2):
                    op("tensor", lambda e, b=bk, c2=c2: e.matmul(
                        pb[b][:, :128], ks(c2), ks(c2), start=(c2 == 0), stop=(c2 == 1)),
                       waits=(wk_ if c2 == 0 else ()),
                       incs=([("PE", 1)] if c2 == 1 else ()))
                pea = C["PE"]
                op("vector", lambda e, b=bk: e.scalar_tensor_tensor(
                    Ppp[0][:, :], pb[b][:, :128], bcol, m_trilneg[:, :], ALU.mult, ALU.mult),
                   waits=[("PE", pea), ("VE", ve_beta), ("PE", S["pe_chain_prev"][lane])],
                   incs=[("VE", 1)])
                rel_bank(bk, "VE")
                ve_P = C["VE"]
                yield
                bk, wk_ = get_bank([("VE", ve_P)])
                op("tensor", lambda e, b=bk: e.transpose(pb[b][:, :128], Ppp[0][:, :], ident[:, :]),
                   waits=wk_, incs=[("PE", 1)])
                pet0 = C["PE"]
                op("vector", lambda e, b=bk: e.tensor_copy(Tpp[0][:, :], pb[b][:, :128]),
                   waits=[("PE", pet0)], incs=[("VE", 1)])
                rel_bank(bk, "VE")
                ve_T = C["VE"]
                op("vector", lambda e, slot=slot: e.tensor_add(
                    X_a[slot][:, :], Tpp[0][:, :], ident[:, :]), incs=[("VE", 1)])
                ve_X = C["VE"]
                yield
                for l in range(1, 7):
                    pprev, pcur = Ppp[(l - 1) % 2], Ppp[l % 2]
                    tprev, tcur = Tpp[(l - 1) % 2], Tpp[l % 2]
                    bk, wk_ = get_bank([("VE", ve_T), ("VE", ve_P)])
                    op("tensor", lambda e, b=bk, tp=tprev, pp=pprev: e.matmul(
                        pb[b][:, :128], tp[:, :], pp[:, :], start=True, stop=True),
                       waits=wk_, incs=[("PE", 1)])
                    pe1 = C["PE"]
                    bk2, wk2_ = get_bank()
                    op("tensor", lambda e, b=bk2, tp=tprev, pp=pprev: e.matmul(
                        pb[b][:, :128], pp[:, :], tp[:, :], start=True, stop=True),
                       waits=wk2_, incs=[("PE", 1)])
                    pe2 = C["PE"]
                    op("vector", lambda e, b=bk, pc=pcur: e.tensor_copy(pc[:, :], pb[b][:, :128]),
                       waits=[("PE", pe1)], incs=[("VE", 1)])
                    rel_bank(bk, "VE")
                    ve_P = C["VE"]
                    op("vector", lambda e, b=bk2, tc=tcur: e.tensor_copy(tc[:, :], pb[b][:, :128]),
                       waits=[("PE", pe2)], incs=[("VE", 1)])
                    rel_bank(bk2, "VE")
                    ve_T = C["VE"]
                    bk3, wk3_ = get_bank([("VE", ve_P), ("VE", ve_X)])
                    op("tensor", lambda e, b=bk3, pc=pcur, slot=slot: e.matmul(
                        pb[b][:, :128], pc[:, :], X_a[slot][:, :], start=True, stop=True),
                       waits=wk3_, incs=[("PE", 1)])
                    pex = C["PE"]
                    op("vector", lambda e, b=bk3, slot=slot: e.tensor_add(
                        X_a[slot][:, :], X_a[slot][:, :], pb[b][:, :128]),
                       waits=[("PE", pex)], incs=[("VE", 1)])
                    rel_bank(bk3, "VE")
                    ve_X = C["VE"]
                    yield
                S["pe_chain_prev"][lane] = C["PE"]
                S["ve_mt"][slot] = ve_X
                op("vector", lambda e, slot=slot, Bkn_t=Bkn_t: e.tensor_scalar(
                    Bkn_t[:, :], Ktm_a[slot][:, :], bcol, -1.0, ALU.mult, ALU.mult),
                   waits=[("PE", S["pe_bkn_prev"][lane])], incs=[("VE", 1)])
                ve_bkn = C["VE"]
                for c2 in range(2):
                    bk, wk_ = get_bank([("VE", ve_bkn), ("VE", ve_X)])
                    op("tensor", lambda e, b=bk, c2=c2, slot=slot, Bkn_t=Bkn_t: e.matmul(
                        pb[b][:, :128], Bkn_t[:, c2 * 128:(c2 + 1) * 128], X_a[slot][:, :],
                        start=True, stop=True),
                       waits=wk_, incs=[("PE", 1)])
                    pew = C["PE"]
                    if c2 == 1:
                        S["pe_bkn_prev"][lane] = pew
                    op("vector", lambda e, b=bk, c2=c2, slot=slot: e.tensor_copy(
                        W1Tn_a[slot][:, c2 * 128:(c2 + 1) * 128], pb[b][:, :128]),
                       waits=[("PE", pew)], incs=[("VE", 1)])
                    rel_bank(bk, "VE")
                yield
                bk, wk_ = get_bank()
                for c2 in range(2):
                    op("tensor", lambda e, b=bk, c2=c2: e.matmul(
                        pb[b][:, :128], ks(c2), qs(c2), start=(c2 == 0), stop=(c2 == 1)),
                       waits=(wk_ if c2 == 0 else ()),
                       incs=([("PE", 1)] if c2 == 1 else ()))
                pelt = C["PE"]
                op("vector", lambda e, b=bk, slot=slot: e.tensor_mul(
                    LT_a[slot][:, :], pb[b][:, :128], m_triu[:, :]),
                   waits=[("PE", pelt)], incs=[("VE", 1)])
                rel_bank(bk, "VE")
                S["ve_indep"][slot] = C["VE"]

            def run_indep_pair(ja, jb2):
                gens = [indep_gen(ja)]
                if jb2 is not None:
                    gens.append(indep_gen(jb2))
                done = [False] * len(gens)
                while not all(done):
                    for i, g in enumerate(gens):
                        if not done[i]:
                            try:
                                next(g)
                            except StopIteration:
                                done[i] = True

            def tail(j):
                slot = j % NSLOT
                jb = j * CH
                nonlocal ve_supd

                def qs(c2):
                    return qT_sb[:, c2 * GT + jb: c2 * GT + jb + 128]

                bk, wk_ = get_bank([("VE", S["ve_indep"][slot]), ("VE", ve_supd),
                                    ("PE", S["pe_u_prev"])])
                op("tensor", lambda e, b=bk, slot=slot: e.matmul(
                    pb[b][:, :256], X_a[slot][:, :], Bv_a[slot][:, :],
                    start=True, stop=False),
                   waits=wk_)
                for c2 in range(2):
                    op("tensor", lambda e, b=bk, c2=c2, slot=slot: e.matmul(
                        pb[b][:, :256], W1Tn_a[slot][:, c2 * 128:(c2 + 1) * 128],
                        S_sb[:, c2 * 256:(c2 + 1) * 256],
                        start=False, stop=(c2 == 1)),
                       incs=([("PE", 1)] if c2 == 1 else ()))
                peu = C["PE"]
                op("vector", lambda e, b=bk: e.tensor_copy(U_sb[:, :], pb[b][:, :256]),
                   waits=[("PE", peu)], incs=[("VE", 1)])
                rel_bank(bk, "VE")
                ve_U = C["VE"]
                bo, wo_ = get_bank()
                for c2 in range(2):
                    op("tensor", lambda e, b=bo, c2=c2: e.matmul(
                        pb[b][:, :256], qs(c2), S_sb[:, c2 * 256:(c2 + 1) * 256],
                        start=(c2 == 0), stop=False),
                       waits=(wo_ if c2 == 0 else ()))
                op("tensor", lambda e, b=bo, slot=slot: e.matmul(
                    pb[b][:, :256], LT_a[slot][:, :], U_sb[:, :], start=False, stop=True),
                   waits=[("VE", ve_U)], incs=[("PE", 1)])
                peo = C["PE"]
                for c2 in range(2):
                    bk, wk_ = get_bank()
                    op("tensor", lambda e, b=bk, c2=c2, slot=slot: e.matmul(
                        pb[b][:, :256], Ktm_a[slot][:, c2 * 128:(c2 + 1) * 128], U_sb[:, :],
                        start=True, stop=True),
                       waits=wk_, incs=[("PE", 1)])
                    pes = C["PE"]
                    if c2 == 1:
                        S["pe_u_prev"] = pes
                        S["tail_pe"][slot] = pes
                    op("vector", lambda e, b=bk, c2=c2: e.tensor_add(
                        S_sb[:, c2 * 256:(c2 + 1) * 256], S_sb[:, c2 * 256:(c2 + 1) * 256],
                        pb[b][:, :256]),
                       waits=[("PE", pes)], incs=[("VE", 1)])
                    rel_bank(bk, "VE")
                ve_supd = C["VE"]
                op("vector", lambda e, b=bo: e.tensor_copy(o_sb[:, :], pb[b][:, :256]),
                   waits=[("PE", peo), ("AC", S["ac_osq"])], incs=[("VE", 1)])
                rel_bank(bo, "VE")
                ve_o = C["VE"]
                op("scalar", lambda e: e.activation(on_sb[:, :], o_sb[:, :], AF.Square,
                                                    accum_out=ossq[:, :]),
                   waits=[("VE", ve_o), ("VE", S["ve_orr"])], incs=[("AC", 1)])
                op("scalar", lambda e: e.activation(orr[:, :], ossq[:, :], AF.Sqrt,
                                                    scale=1.0 / DH, bias=EPS_RMS),
                   incs=[("AC", 1)])
                ac_or = C["AC"]
                S["ac_osq"] = ac_or
                op("vector", lambda e: e.reciprocal(orinv[:, :], orr[:, :]),
                   waits=[("AC", ac_or)], incs=[("VE", 1)])
                S["ve_orr"] = C["VE"]
                op("vector", lambda e: e.tensor_scalar_mul(on_sb[:, :], o_sb[:, :], orinv[:, :]),
                   incs=[("VE", 1)])
                ve_on = C["VE"]
                for c2 in range(2):
                    bk, wk_ = get_bank([("VE", ve_on)])
                    op("tensor", lambda e, b=bk, c2=c2: e.transpose(
                        pb[b][:, :128], on_sb[:, c2 * 128:(c2 + 1) * 128], ident[:, :]),
                       waits=wk_, incs=[("PE", 1)])
                    peot = C["PE"]
                    w = [("PE", peot)]
                    if c2 == 0:
                        w.append(S["st_og"])
                    op("vector", lambda e, b=bk, c2=c2: e.tensor_scalar_mul(
                        oT_sb[:, c2 * 128:(c2 + 1) * 128], pb[b][:, :128], g_rms[:, c2:c2 + 1]),
                       waits=w, incs=[("VE", 1)])
                    rel_bank(bk, "VE")
                ve_ot = C["VE"]
                sc = gt0 + jb
                oh = sc // HS
                sco = sc % HS
                sos = "STOA" if oh == 0 else "STOB"
                op("sync", lambda e, oh=oh, sco=sco: e.dma_start(
                    out=og_in[oh][:, sco: sco + 128].rearrange("(c p) t -> p c t", p=128),
                    in_=oT_sb.ap().rearrange("p (c t) -> p c t", c=2)),
                   waits=[("VE", ve_ot)], incs=[(sos, 16)])
                S["st_og"] = (sos, C[sos])

            for jp in range(0, GCH, 2):
                run_indep_pair(jp, jp + 1 if jp + 1 < GCH else None)
                if jp >= SKEW:
                    tail(jp - 2)
                    tail(jp - 1)
            tail(GCH - 2)
            tail(GCH - 1)
            S["pe_scan_q"] = C["PE"]
            S["pe_xg"] = C["PE"]
            if g == max(0, NG // 2 - 1):
                op("gpsimd", lambda e: e.collective_compute(
                    "AllGather", ALU.bypass, replica_groups=REPL_GROUPS,
                    ins=[og_in[0]], outs=[og_out[0]]),
                   waits=[("STOA", C["STOA"])], incs=[("CC", 1)], drain=False)

        op("gpsimd", lambda e: e.collective_compute(
            "AllGather", ALU.bypass, replica_groups=REPL_GROUPS,
            ins=[og_in[1]], outs=[og_out[1]]),
           waits=[("STOB", C["STOB"])], incs=[("CC", 1)], drain=False)
        cc2 = C["CC"]

        f3_pe, f3_ac = C["PE"], C["AC"]
        if me == "sync":
            pid = nc.partition_id(engines=[mybir.EngineType.SP])
            slice_reg = eng_handle.to_reg(pid % 4)

        first3 = [[("PE", f3_pe), ("AC", f3_ac)]]
        pe_y_kc = {}
        st3 = {"ve_ogyp": [0, 0], "ve_r2b": [0, 0], "ve_xln": 0, "ve_x3": 0,
               "pe_og": [0, 0]}
        ve_split3 = {}
        pend3 = {}
        pst3 = {}

        def q_loads_y(t):
            par = t % 2
            cols = slice(t * TT, (t + 1) * TT)
            ogt = og_yp2[par]
            C["LDO"] += 16
            if me == "sync":
                for s_, v_ in (first3[0] + [("PE", st3["pe_og"][par]), ("VE", st3["ve_ogyp"][par])]):
                    if v_ > 0:
                        eng_handle.wait_ge(sems[s_], v_)
                for sl in range(4):
                    with eng_handle.If_cmp(slice_reg, sl, "IS_EQ"):
                        c0 = sl * TCORE + t * TT
                        oh = c0 // HS
                        c0h = c0 % HS
                        eng_handle.wait_ge(sems["CC"], cc2 - 1 if oh == 0 else cc2)
                        eng_handle.dma_start(
                            out=ogt.ap().rearrange("p (k c) -> p k c", k=NKD),
                            in_=og_out[oh][:, c0h:c0h + TT].rearrange("(k p) c -> p k c", p=128),
                        ).then_inc(sems["LDO"], 16)
                eng_handle.end_ifs()
            first3[0] = []
            ld_og = C["LDO"]
            halfL = (t * TT) // HC
            colL = (t * TT) % HC
            op("sync", lambda e, halfL=halfL, colL=colL: e.dma_start(
                out=xlnb_tile.ap().rearrange("p (k c) -> p k c", k=NKD),
                in_=xg_in[halfL][:, colL:colL + TT].rearrange("(k p) c -> p k c", p=128)),
               waits=[("VE", st3["ve_xln"])], incs=[("LDL", 16)])
            ld_xln = C["LDL"]
            op("sync", lambda e, cols=cols: e.dma_start(
                out=x_tile2[0].ap().rearrange("p (k c) -> p k c", k=NKD),
                in_=resid[:, cols].rearrange("(k p) c -> p k c", p=128)),
               waits=[("VE", st3["ve_x3"]), ("STR", S["st_r"])], incs=[("LDR", 16)])
            ld_r1 = C["LDR"]
            ld_wo = {}

            def load_wo(kc):
                sem = "LDW0" if kc % 2 == 0 else "LDW1"
                key = (kc - 2) % NKD if pe_y_kc else None
                w = [("PE", pe_y_kc.get(key, 0))] if key is not None else []
                op("sync", lambda e, kc=kc: e.dma_start(
                    out=wo_buf[kc % 2][:, :], in_=wo_i[kc * 128:(kc + 1) * 128, :]),
                   waits=w, incs=[(sem, 16)])
                ld_wo[kc] = (sem, C[sem])

            load_wo(0)
            load_wo(1)
            ybanks = []
            for oc in range(NKD):
                bk, wk_ = get_bank() if oc < 6 else get_bank_slow()
                ybanks.append((bk, wk_))
            for kc in range(NKD):
                for oc in range(NKD):
                    bk, wk_ = ybanks[oc]
                    w = []
                    if kc == 0:
                        w = list(wk_) + [("LDO", ld_og)]
                    if oc == 0:
                        w = w + [ld_wo[kc]]
                    op("tensor", lambda e, kc=kc, oc=oc, b=bk, ogt=ogt: e.matmul(
                        pb[b][:, :TT], wo_buf[kc % 2][:, oc * 128:(oc + 1) * 128],
                        ogt[:, kc * TT:(kc + 1) * TT],
                        start=(kc == 0), stop=(kc == NKD - 1)),
                       waits=w,
                       incs=([("PE", 1)] if oc == NKD - 1 else ()))
                pe_y_kc[kc] = C["PE"]
                if kc + 2 < NKD:
                    load_wo(kc + 2)
            pe_y = C["PE"]
            st3["pe_og"][par] = pe_y
            for oc in range(NKD):
                bk, _ = ybanks[oc]
                cs = slice(oc * TT, (oc + 1) * TT)
                w = [("PE", pe_y)]
                if oc == 0:
                    w += [("LDL", ld_xln)]
                op("vector", lambda e, b=bk, cs=cs, ogt=ogt: e.tensor_add(
                    ogt[:, cs], pb[b][:, :TT], xlnb_tile[:, cs]),
                   waits=w, incs=[("VE", 1)])
                rel_bank(bk, "VE")
            pst3[t] = {"ve_yp": C["VE"], "ld_r1": ld_r1}
            st3["ve_xln"] = C["VE"]

        def q_stats1(t):
            par = t % 2
            pst3[t]["ve_rb3"] = ln_stats(og_yp2[par], [("VE", pst3[t]["ve_yp"])])

        def q_zchain_kc(t, kc):
            par = t % 2
            ogt = og_yp2[par]
            r2t = r2b2[par]
            cs = slice(kc * TT, (kc + 1) * TT)
            w = []
            if kc == 0:
                w = [("VE", pst3[t]["ve_rb3"])]
            op("vector", lambda e, cs=cs, ogt=ogt: e.tensor_mul(z_tile[:, :], ogt[:, cs], Rb[:, :]),
               waits=w, incs=[("VE", 1)])
            op("vector", lambda e: e.tensor_sub(z_tile[:, :], z_tile[:, :], Ab[:, :]),
               incs=[("VE", 1)])
            ve_z = C["VE"]
            op("scalar", lambda e, kc=kc: e.activation(
                e_tile[:, :], z_tile[:, :], AF.Identity,
                bias=ln_b[:, kc:kc + 1], scale=ln_g[:, kc:kc + 1]),
               waits=[("VE", ve_z)], incs=[("AC", 1)])
            ac_z = C["AC"]
            w = [("AC", ac_z)]
            if kc == 0:
                w += [("LDR", pst3[t]["ld_r1"]), ("VE", st3["ve_r2b"][par])]
            op("vector", lambda e, cs=cs, r2t=r2t: e.tensor_add(
                r2t[:, cs], e_tile[:, :], x_tile2[0][:, cs]),
               waits=w, incs=[("VE", 1)])
            if kc == NKD - 1:
                ve_r2 = C["VE"]
                pst3[t]["ve_r2"] = ve_r2
                st3["ve_ogyp"][par] = ve_r2
                st3["ve_x3"] = ve_r2

        def q_stats2(t):
            par = t % 2
            pst3[t]["ve_rb4"] = ln_stats(r2b2[par], [("VE", pst3[t]["ve_r2"])])

        def q_splits_kc(t, kc):
            par = t % 2
            ve_split3[t] = ln_split_kc(r2b2[par], pst3[t]["ve_rb4"], xsp2[par], pe_xsp_par[par], kc)
            if kc == NKD - 1:
                pst3.pop(t)

        def p3_store(t):
            cols = slice(t * TT, (t + 1) * TT)
            ve_out = pend3.pop(t)
            op("sync", lambda e, cols=cols: e.dma_start(
                out=outT[:, cols].rearrange("(k p) c -> p k c", p=128),
                in_=out3.ap().rearrange("p (k c) -> p k c", k=NKD)),
               waits=[("VE", ve_out)], incs=[("STU", 16)])
            S["st_out"] = C["STU"]

        out3 = x_tile2[1]
        q_loads_y(0)
        q_stats1(0)
        for kc in range(NKD):
            q_zchain_kc(0, kc)
        q_stats2(0)
        for kc in range(NKD):
            q_splits_kc(0, kc)
        for t in range(NT):
            par = t % 2
            inj = {}
            if t + 1 < NT:
                inj["A_end"] = [lambda tt=t + 1: q_loads_y(tt)]
                inj["B_mid"] = [lambda tt=t + 1: q_stats1(tt)]
                for kc in range(NKD):
                    inj.setdefault("B" + str(8 + kc), []).append(
                        lambda tt=t + 1, kc=kc: q_zchain_kc(tt, kc))
                inj["C_mid"] = [lambda tt=t + 1: q_stats2(tt)]
                for kc in range(NKD):
                    inj.setdefault("C" + str(8 + kc), []).append(
                        lambda tt=t + 1, kc=kc: q_splits_kc(tt, kc))
            if t - 1 in pend3:
                inj.setdefault("A_end", []).append(lambda tt=t - 1: p3_store(tt))
            ve_out = ffn(ve_split3.pop(t), r2b2[par], out3,
                         lambda: ("STU", S["st_out"]), xsp2[par], inject=inj)
            pe_xsp_par[par] = ffn.pe_stageA
            st3["ve_r2b"][par] = ve_out
            pend3[t] = ve_out
        p3_store(NT - 1)

        waitonly("sync", [("STU", C["STU"]), ("STR", C["STR"]), ("STXA", C["STXA"]),
                          ("STXB", C["STXB"]), ("STOA", C["STOA"]), ("STOB", C["STOB"])])
        return C

    counts = {}
    with nc.Block() as block:
        def mk(name):
            def f(eng):
                counts[name] = emit(name, eng)
            return f
        block.sync(mk("sync"))
        block.gpsimd(mk("gpsimd"))
        block.tensor(mk("tensor"))
        block.vector(mk("vector"))
        block.scalar(mk("scalar"))

    base = counts["sync"]
    for k, c in counts.items():
        assert c == base, f"counter divergence in {k}"
    return nc


def _prep_shared(inp):
    f = np.float32
    w1T = np.ascontiguousarray(np.asarray(inp["L0_ffn_w1_w"], dtype=f).T)
    w2T = np.ascontiguousarray(np.asarray(inp["L0_ffn_w2_w"], dtype=f).T)
    h1T = np.ascontiguousarray(np.asarray(inp["L0_ffn_h1_w"], dtype=f).T)
    h2T = np.ascontiguousarray(np.asarray(inp["L0_ffn_h2_w"], dtype=f).T)
    w3T = np.ascontiguousarray(np.asarray(inp["L0_ffn_w3_w"], dtype=f).T)

    w12t = np.empty((8, NKD, 128, 1024), f)
    w12t[:, :, :, :512] = w1T.reshape(NKD, 128, 8, 512).transpose(2, 0, 1, 3)
    w12t[:, :, :, 512:] = w2T.reshape(NKD, 128, 8, 512).transpose(2, 0, 1, 3)
    h1t = np.ascontiguousarray(h1T.reshape(NOH, 128, 16, 256).transpose(2, 0, 1, 3))
    h2t = np.ascontiguousarray(h2T.reshape(NOH, 128, 16, 256).transpose(2, 0, 1, 3))
    w3t = np.ascontiguousarray(w3T.reshape(NOH, 128, 4, 256).transpose(2, 0, 1, 3))

    tile_s = lambda s, n: np.ascontiguousarray(np.asarray(s, dtype=f).reshape(n, 128).T)
    return {
        "w12t": np.ascontiguousarray(w12t),
        "h1t": h1t, "h2t": h2t, "w3t": w3t,
        "w1s": tile_s(inp["L0_ffn_w1_s"], NOH),
        "w2s": tile_s(inp["L0_ffn_w2_s"], NOH),
        "h1s": tile_s(inp["L0_ffn_h1_s"], NOH),
        "h2s": tile_s(inp["L0_ffn_h2_s"], NOH),
        "w3s": tile_s(inp["L0_ffn_w3_s"], NKD),
        "wo": np.ascontiguousarray(np.asarray(inp["L0_attn_Wo"], dtype=f)),
        "grms": tile_s(inp["L0_attn_g_rms"], 2),
        "lngt": tile_s(inp["L0_attn_ln_g"], NKD),
        "lnbt": tile_s(inp["L0_attn_ln_b"], NKD),
    }


def _prep_core(inp, c):
    f = np.float32
    b, s = c // 4, c % 4
    h = c % 4
    x = np.asarray(inp["L0_attn_x"], dtype=f)
    x0T = np.ascontiguousarray(x[b, s * TCORE:(s + 1) * TCORE, :].T)
    conv_tile = lambda a: np.ascontiguousarray(
        np.asarray(a, dtype=f)[:, h * DH:(h + 1) * DH].T.reshape(2, 128, 4)
        .transpose(1, 0, 2).reshape(128, 8))
    return {
        "x0T": x0T,
        "wq": np.ascontiguousarray(np.asarray(inp["L0_attn_Wq"], dtype=f)[:, h * DH:(h + 1) * DH]),
        "wk": np.ascontiguousarray(np.asarray(inp["L0_attn_Wk"], dtype=f)[:, h * DH:(h + 1) * DH]),
        "wv": np.ascontiguousarray(np.asarray(inp["L0_attn_Wv"], dtype=f)[:, h * DH:(h + 1) * DH]),
        "wb": np.ascontiguousarray(np.asarray(inp["L0_attn_Wb"], dtype=f)[:, h:h + 1]),
        "convq": conv_tile(inp["L0_attn_conv_q"]),
        "convk": conv_tile(inp["L0_attn_conv_k"]),
        "convv": conv_tile(inp["L0_attn_conv_v"]),
    }


def kernel(**inputs):
    shared = _prep_shared(inputs)
    nc = build_nc()
    in_maps = []
    for c in range(8):
        m = dict(shared)
        m.update(_prep_core(inputs, c))
        in_maps.append(m)
    res = run_bass_kernel_spmd(nc, in_maps, core_ids=list(range(8)))
    out = np.empty((NB, TSEQ, D), dtype=np.float32)
    for c in range(8):
        b, s = c // 4, c % 4
        out[b, s * TCORE:(s + 1) * TCORE, :] = res.results[c]["outT"].T
    return out
